# version 11
# baseline (speedup 1.0000x reference)
import contextlib

import numpy as np

import concourse.bass as bass
import concourse.mybir as mybir

F32 = mybir.dt.float32
BF16 = mybir.dt.bfloat16
AF = mybir.ActivationFunctionType
ALU = mybir.AluOpType
AX = mybir.AxisListType

S = 8192
D = 2048
R = 8
TOK = S // R
NTT = TOK // 128
NDC = D // 128
DH = 128
NCH = S // 128
F_EXP = 1408
F_SH = 2816
N_EXP = 8
EPS_LN = 1e-5
ATT_SCALE = float(DH) ** -0.5

ENGINES = ("sync", "scalar", "vector", "tensor", "gpsimd")


class Sem:
    __slots__ = ("h", "n")

    def __init__(self, h):
        self.h = h
        self.n = 0

    def bump(self, k=1):
        self.n += k
        return self.n


MAX_CONSTS = 32


class Prog:

    def __init__(self, nc, es):
        self.nc = nc
        self.es = es
        self.scopes = [es]
        self.q = {e: [] for e in ENGINES}
        self._sems = {}
        self.const_vals = []
        self.const_tile = None
        self.const_dram = None

    def scope(self):
        import contextlib as _cl
        p = self

        class _Scope:
            def __enter__(self):
                self.es = _cl.ExitStack()
                p.scopes.append(self.es)
                return self.es

            def __exit__(self, *a):
                self.es.close()
                p.scopes.pop()
                return False
        return _Scope()

    def barrier(self, engines=ENGINES):
        s_bar = self.sem("bar")
        targets = tuple((s, s.n) for name, s in self._sems.items()
                        if s.n > 0 and name != "bar")
        for eng in engines:
            def f(e, targets=targets):
                for (s, v) in targets:
                    e.wait_ge(s.h, v)
                e.drain()
                e.sem_inc(s_bar.h, 1)
            self.on(eng, f)
        return s_bar, s_bar.bump(len(engines))

    def init_consts(self, const_dram):
        self.const_dram = const_dram
        self.const_tile = self.sb("const_pool", [128, MAX_CONSTS], F32)

    def constv(self, value):
        value = float(value)
        if value not in self.const_vals:
            assert len(self.const_vals) < MAX_CONSTS
            self.const_vals.append(value)
            j = self.const_vals.index(value)
            self.nc.const_aps.register(F32, value, self.const_tile[:, j:j + 1])
        return value

    SEM_POOL = 96

    def sem(self, name) -> Sem:
        if name not in self._sems:
            if not hasattr(self, "_pool"):
                self._pool = []
            if len(self._pool) < self.SEM_POOL:
                s = Sem(self.es.enter_context(self.nc.semaphore(name)))
                self._pool.append(s)
            else:
                s = self._pool[len(self._sems) % self.SEM_POOL]
            self._sems[name] = s
        return self._sems[name]

    def sb(self, name, shape, dtype):
        return self.scopes[-1].enter_context(self.nc.sbuf_tensor(name, shape, dtype))

    def ps(self, name, shape, dtype=F32):
        return self.scopes[-1].enter_context(self.nc.psum_tensor(name, shape, dtype))

    def on(self, eng, f):
        self.q[eng].append(f)

    def wait(self, eng, sem: Sem, v: int):
        def f(e, sem=sem, v=v):
            e.wait_ge(sem.h, v)
        self.on(eng, f)

    def finalize(self):
        if self.const_dram is not None:
            s_const = self.sem("const_ld")
            ctile = self.const_tile
            cdram = self.const_dram

            def cload(e):
                e.dma_start(out=ctile[:], in_=cdram).then_inc(s_const.h, 16)
            self.q["sync"].insert(0, cload)
            v = s_const.bump(16)
            for eng in ENGINES:
                if eng == "sync":
                    continue

                def cw(e, v=v):
                    e.wait_ge(s_const.h, v)
                self.q[eng].insert(0, cw)

        block = self.es.enter_context(self.nc.Block())

        def runner(eng):
            def body(e):
                for f in self.q[eng]:
                    f(e)
            body.__name__ = f"body_{eng}"
            return body

        block.sync(runner("sync"))
        block.scalar(runner("scalar"))
        block.vector(runner("vector"))
        block.tensor(runner("tensor"))
        block.gpsimd(runner("gpsimd"))



def emit_load_x(p: Prog, x_dram, x_sb):
    ld = p.sem("x_ld")
    src = x_dram.rearrange("(t p) d -> p t d", p=128)
    v = ld.bump(16)

    def f(e, src=src):
        e.dma_start(out=x_sb[:], in_=src).then_inc(ld.h, 16)
    p.on("sync", f)
    return ld, v



def emit_ln(p: Prog, x_sb, t_bf, deps, tag, t_lo=None):
    p.constv(EPS_LN)
    p.constv(-0.5)
    sq = p.sb(f"ln_sq_{tag}", [128, D], F32)
    ssq = p.sb(f"ln_ssq_{tag}", [128, 1], F32)
    sx = p.sb(f"ln_sx_{tag}", [128, 1], F32)
    mean = p.sb(f"ln_mean_{tag}", [128, 1], F32)
    var = p.sb(f"ln_var_{tag}", [128, 1], F32)
    rstd = p.sb(f"ln_rstd_{tag}", [128, 1], F32)
    tf = p.sb(f"ln_tf_{tag}", [128, D], F32) if t_lo is not None else None

    s_sq = p.sem("ln_sq")
    s_st = p.sem("ln_st")
    s_rs = p.sem("ln_rs")
    s_done = p.sem("ln_done")

    for i in range(NTT):
        xt = x_sb[:, i, :]

        st_war = s_st.n

        def act1(e, xt=xt, i=i, st_war=st_war):
            if i == 0:
                for (sm, v) in deps:
                    e.wait_ge(sm.h, v)
            else:
                e.wait_ge(s_st.h, st_war)
            e.activation(sq[:], xt, AF.Square, accum_out=ssq[:]).then_inc(s_sq.h, 1)
        p.on("scalar", act1)
        sq_v = s_sq.bump()

        def dve1(e, xt=xt, sq_v=sq_v, i=i):
            if i == 0:
                for (sm, v) in deps:
                    e.wait_ge(sm.h, v)
            e.reduce_sum(sx[:], xt, axis=AX.X)
            e.drain()
            e.wait_ge(s_sq.h, sq_v)
            e.tensor_scalar_mul(mean[:], sx[:], 1.0 / D)
            e.drain()
            e.tensor_tensor(sx[:], mean[:], mean[:], ALU.mult)
            e.tensor_scalar_mul(var[:], ssq[:], 1.0 / D)
            e.drain()
            e.tensor_tensor(var[:], var[:], sx[:], ALU.subtract).then_inc(s_st.h, 1)
        p.on("vector", dve1)
        st_v = s_st.bump()

        def act2(e, st_v=st_v):
            e.wait_ge(s_st.h, st_v)
            e.activation(var[:], var[:], AF.Ln, bias=EPS_LN)
            e.drain()
            e.activation(rstd[:], var[:], AF.Exp, scale=-0.5).then_inc(s_rs.h, 1)
        p.on("scalar", act2)
        rs_v = s_rs.bump()

        def dve2(e, xt=xt, i=i, rs_v=rs_v):
            e.wait_ge(s_rs.h, rs_v)
            if tf is not None:
                e.tensor_scalar(tf[:], xt, mean[:], rstd[:], ALU.subtract, ALU.mult)
                e.tensor_copy(t_bf[:, i, :], tf[:])
                e.tensor_tensor(t_lo[:, i, :], tf[:], t_bf[:, i, :],
                                ALU.subtract).then_inc(s_done.h, 1)
            else:
                e.tensor_scalar(t_bf[:, i, :], xt, mean[:], rstd[:],
                                ALU.subtract, ALU.mult).then_inc(s_done.h, 1)
        p.on("vector", dve2)
        s_done.bump()

    return s_done, s_done.n



def emit_allgather_tm(p: Prog, sb_tile, cc_in, cc_out, deps, tag):
    s_cc = p.sem("ag")

    def f(e):
        for (sm, v) in deps:
            e.wait_ge(sm.h, v)
        e.dma_start(out=cc_in.rearrange("(t p) d -> p t d", p=128), in_=sb_tile[:]).then_inc(s_cc.h, 16)
    p.on("scalar", f)
    v0 = s_cc.bump(16)

    def g(e, v0=v0):
        e.wait_ge(s_cc.h, v0)
        e.collective_compute(
            "AllGather", ALU.bypass,
            replica_groups=[list(range(R))],
            ins=[cc_in],
            outs=[cc_out],
        ).then_inc(s_cc.h, 1)
    p.on("gpsimd", g)
    v1 = s_cc.bump(1)
    return s_cc, v1



def emit_attn_proj(p: Prog, ag_t, specs, deps, tag):
    nm_tot = sum(n_m for (_, _, n_m, _) in specs)
    wsl = []
    s_w = p.sem("prj_w")
    for si, (w_dram, col0, n_m, outT) in enumerate(specs):
        for m in range(n_m):
            wt = p.sb(f"prj_w_{tag}_{si}_{m}", [128, NDC, 128], BF16)
            src = w_dram[:, col0 + m * 128: col0 + (m + 1) * 128].rearrange(
                "(a p) c -> p a c", p=128)

            def f(e, wt=wt, src=src):
                e.dma_start(out=wt[:], in_=src).then_inc(s_w.h, 16)
            p.on("gpsimd", f)
            s_w.bump(16)
            wsl.append((wt, outT, m))
    w_all = s_w.n

    NBUF_T = 1
    tT = [p.sb(f"prj_tT_{tag}_{b}", [128, NDC, TOK], BF16) for b in range(NBUF_T)]
    tT = [tT[b % NBUF_T] for b in range(2)]
    ps = [p.ps(f"prj_ps_{tag}_{b}", [128, 512]) for b in range(2)]
    s_tl = p.sem("prj_tl")
    s_mm = p.sem("prj_mm")
    s_ev = p.sem("prj_ev")
    s_done = p.sem("prj_done")

    n_psum_per_rank = nm_tot * (TOK // 512)
    base_ev = s_ev.n

    for r in range(R):
        buf = tT[r % 2]
        def loads(e, r=r, buf=buf, ev_war=base_ev + r * n_psum_per_rank):
            if r == 0:
                for (sm, v) in deps:
                    e.wait_ge(sm.h, v)
            else:
                e.wait_ge(s_ev.h, ev_war)
            for j in range(NDC):
                e.dma_start_transpose(
                    buf[:, j, :],
                    ag_t[r * TOK:(r + 1) * TOK, j * 128:(j + 1) * 128],
                ).then_inc(s_tl.h, 16)
        p.on("sync", loads)
        tl_v = s_tl.bump(16 * NDC)

        kk = 0
        for (wt, outT, m) in wsl:
            for nck in range(TOK // 512):
                gidx = r * n_psum_per_rank + kk
                bank = ps[gidx % 2]

                def mm(e, wt=wt, nck=nck, bank=bank, buf=buf, tl_v=tl_v,
                       ev_need=base_ev + max(0, gidx - 1), w_all=w_all, first=(kk == 0)):
                    if first:
                        e.wait_ge(s_w.h, w_all)
                        e.wait_ge(s_tl.h, tl_v)
                    e.wait_ge(s_ev.h, ev_need)
                    ins = None
                    for j in range(NDC):
                        ins = e.matmul(bank[:], wt[:, j, :],
                                       buf[:, j, nck * 512:(nck + 1) * 512],
                                       start=(j == 0), stop=(j == NDC - 1))
                    ins.then_inc(s_mm.h, 1)
                p.on("tensor", mm)
                mm_v = s_mm.bump()

                def ev(e, bank=bank, outT=outT, m=m, r=r, nck=nck, mm_v=mm_v):
                    e.wait_ge(s_mm.h, mm_v)
                    e.tensor_copy(outT[:, m, r * TOK + nck * 512: r * TOK + (nck + 1) * 512],
                                  bank[:]).then_inc(s_ev.h, 1)
                p.on("vector", ev)
                s_ev.bump()
                kk += 1

    return s_ev, s_ev.n



def emit_flash_head(p: Prog, qT, kT, v_tm, outT, ones_bf, ones1_f, mask4,
                    deps, tag, olinT=None, wbase=1.0, wlin=0.5, ones_f=None):
    p.constv(ATT_SCALE)
    NQB = S // 512
    BQ = 3
    ps_s = [p.ps(f"fl_s_{tag}_{b}", [128, BQ * 512]) for b in range(2)]
    ps_c = p.ps(f"fl_c_{tag}", [128, 512])
    ps_b = p.ps(f"fl_b_{tag}", [128, 512])
    ps_d = ps_b[0:1, :]
    pT = [p.sb(f"fl_p_{tag}_{b}", [128, BQ * 512], BF16) for b in range(2)]
    sacc = [p.sb(f"fl_sacc_{tag}_{b}", [128, 512], F32) for b in range(2)]
    stmp = [p.sb(f"fl_stmp_{tag}_{b}", [128, 512], BF16) for b in range(2)]
    den = p.sb(f"fl_den_{tag}", [1, 512], F32)
    bc = p.sb(f"fl_bc_{tag}", [128, 512], F32)
    tmp = p.sb(f"fl_tmp_{tag}", [128, 512], F32)

    s_sc = p.sem("fl_sc")
    s_exp = p.sem("fl_exp")
    s_pm = p.sem("fl_pm")
    s_cd = p.sem("fl_cd")
    s_no = p.sem("fl_no")

    base_pm = s_pm.n

    batches = []
    for qb in range(NQB):
        nkt = 4 * (qb + 1)
        kts = [list(range(b0, min(b0 + BQ, nkt))) for b0 in range(0, nkt, BQ)]
        for bi, batch in enumerate(kts):
            batches.append((qb, bi, len(kts), batch))
    NB = len(batches)

    pe_s = [None] * NB
    pe_cd = [None] * NB
    pe_nm = {}
    cd_v_of = {}

    for gbt, (qb, bi, nbt, batch) in enumerate(batches):
        b = gbt % 2
        nb = len(batch)

        def mm_s(e, b=b, batch=batch, qb=qb, pm_need=base_pm + max(0, gbt - 1),
                 first=(gbt == 0)):
            if first:
                for (sm, v) in deps:
                    e.wait_ge(sm.h, v)
            e.wait_ge(s_pm.h, pm_need)
            ins = None
            for i, kt in enumerate(batch):
                ins = e.matmul(ps_s[b][:, i * 512:(i + 1) * 512],
                               kT[:, kt * 128:(kt + 1) * 128],
                               qT[:, qb * 512:(qb + 1) * 512],
                               start=True, stop=True)
            ins.then_inc(s_sc.h, 1)
        pe_s[gbt] = mm_s
        sc_v = s_sc.bump()

        def act_e(e, b=b, nb=nb, sc_v=sc_v):
            e.wait_ge(s_sc.h, sc_v)
            e.activation(pT[b][:, 0:nb * 512], ps_s[b][:, 0:nb * 512], AF.Exp,
                         scale=ATT_SCALE).then_inc(s_exp.h, 1)
        p.on("scalar", act_e)
        exp_v = s_exp.bump()

        ab = bi % 2

        def dve_a(e, b=b, ab=ab, batch=batch, qb=qb, exp_v=exp_v,
                  first=(bi == 0), no_need=s_no.n):
            e.wait_ge(s_exp.h, exp_v)
            for i, kt in enumerate(batch):
                diag = kt - 4 * qb
                if diag >= 0:
                    e.tensor_tensor(pT[b][:, i * 512:(i + 1) * 512],
                                    pT[b][:, i * 512:(i + 1) * 512],
                                    mask4[:, diag, :], ALU.mult)
                    e.drain()
            nbq = len(batch)
            if nbq == 3:
                e.tensor_tensor(stmp[0][:], pT[b][:, 0:512], pT[b][:, 512:1024], ALU.add)
                e.drain()
                e.tensor_tensor(stmp[1][:], stmp[0][:], pT[b][:, 1024:1536], ALU.add)
                e.drain()
                part = stmp[1]
            elif nbq == 2:
                e.tensor_tensor(stmp[1][:], pT[b][:, 0:512], pT[b][:, 512:1024], ALU.add)
                e.drain()
                part = stmp[1]
            else:
                part = None
            if first:
                e.wait_ge(s_no.h, no_need)
                if part is None:
                    e.tensor_copy(sacc[ab][:], pT[b][:, 0:512])
                    e.drain()
                else:
                    e.tensor_copy(sacc[ab][:], part[:])
                    e.drain()
            else:
                e.tensor_tensor(sacc[ab][:], sacc[1 - ab][:],
                                part[:] if part is not None else pT[b][:, 0:512],
                                ALU.add)
                e.drain()
            e.engine_nop().then_inc(s_pm.h, 1)
        p.on("vector", dve_a)
        pm_v = s_pm.bump()

        def mm_cd(e, b=b, batch=batch, pm_v=pm_v, firstb=(bi == 0),
                  lastb=(bi == nbt - 1), no_need=s_no.n, ab=ab):
            e.wait_ge(s_pm.h, pm_v)
            if firstb:
                e.wait_ge(s_no.h, no_need)
            ins = None
            for i, kt in enumerate(batch):
                ins = e.matmul(ps_c[:], v_tm[:, kt, :],
                               pT[b][:, i * 512:(i + 1) * 512],
                               start=(firstb and i == 0),
                               stop=(lastb and i == len(batch) - 1))
            if lastb:
                ins2 = e.matmul(ps_d, ones_f[:], sacc[ab][:], start=True, stop=True)
                ins2.then_inc(s_cd.h, 1)
        pe_cd[gbt] = mm_cd
        if bi == nbt - 1:
            cd_v_of[qb] = s_cd.bump()
            cd_v = cd_v_of[qb]

            def dve_n1(e, cd_v=cd_v, wb=float(wbase)):
                e.wait_ge(s_cd.h, cd_v)
                e.tensor_copy(den[:], ps_d)
                e.drain()
                e.reciprocal(den[:], den[:])
                e.drain()
                e.tensor_scalar_mul(den[:], den[:], wb)
                e.drain()
                e.engine_nop().then_inc(s_no.h, 1)
            p.on("vector", dve_n1)
            no_v1 = s_no.bump()

            def mm_b(e, no_v1=no_v1):
                e.wait_ge(s_no.h, no_v1)
                e.matmul(ps_b[:], ones1_f[:], den[:], start=True,
                         stop=True).then_inc(s_cd.h, 1)
            pe_nm[qb] = mm_b
            cd_v2 = s_cd.bump()

            def dve_n2(e, qb=qb, cd_v2=cd_v2):
                e.wait_ge(s_cd.h, cd_v2)
                e.tensor_copy(bc[:], ps_b[:])
                e.drain()
                sl = slice(qb * 512, (qb + 1) * 512)
                if olinT is None:
                    e.tensor_tensor(outT[:, sl], ps_c[:], bc[:], ALU.mult).then_inc(s_no.h, 1)
                else:
                    e.tensor_tensor(tmp[:], ps_c[:], bc[:], ALU.mult)
                    e.drain()
                    e.scalar_tensor_tensor(outT[:, sl], olinT[:, sl], float(wlin), tmp[:],
                                           ALU.mult, ALU.add).then_inc(s_no.h, 1)
            p.on("vector", dve_n2)
            s_no.bump()

    if NB > 0:
        p.on("tensor", pe_s[0])
    if NB > 1:
        p.on("tensor", pe_s[1])
    for j, (qb, bi, nbt, batch) in enumerate(batches):
        p.on("tensor", pe_cd[j])
        if bi == nbt - 1:
            p.on("tensor", pe_nm[qb])
        if j + 2 < NB:
            p.on("tensor", pe_s[j + 2])

    return s_no, s_no.n



def emit_transpose_fm_tm(p: Prog, srcT, dst_tm, n, ident, deps, tag):
    ps = [p.ps(f"tp_{tag}_{b}", [128, 128], BF16) for b in range(2)]
    s_t = p.sem("tp_t")
    s_e = p.sem("tp_e")
    base_e = s_e.n
    for c in range(n):
        b = c % 2

        def mm(e, c=c, b=b, ev_need=base_e + max(0, c - 1), first=(c == 0)):
            if first:
                for (sm, v) in deps:
                    e.wait_ge(sm.h, v)
            e.wait_ge(s_e.h, ev_need)
            e.transpose(ps[b][:], srcT[:, c * 128:(c + 1) * 128], ident[:]).then_inc(s_t.h, 1)
        p.on("tensor", mm)
        t_v = s_t.bump()

        def ev(e, c=c, b=b, t_v=t_v):
            e.wait_ge(s_t.h, t_v)
            e.tensor_copy(dst_tm[:, c, :], ps[b][:]).then_inc(s_e.h, 1)
        p.on("vector", ev)
        s_e.bump()
    return s_e, s_e.n



def emit_out_proj(p: Prog, mixT_loc, w_dram, x_src, x_sb, deps, tag):
    ctxT = p.sb(f"op_ctx_{tag}", [128, NDC, TOK], BF16)
    w_sb = p.sb(f"op_w_{tag}", [128, NDC, D], BF16)
    ps_o = [p.ps(f"op_ps_{tag}_{b}", [128, D]) for b in range(2)]

    s_ld = p.sem("op_ld")
    s_w = p.sem("op_w")
    s_mm = p.sem("op_mm")
    s_res = p.sem("op_res")

    def loads(e):
        for (sm, v) in deps:
            e.wait_ge(sm.h, v)
        if x_src is not None:
            e.dma_start(out=x_sb[:], in_=x_src.rearrange("(t p) d -> p t d", p=128)).then_inc(s_ld.h, 16)
        for j in range(NDC):
            e.dma_start(out=ctxT[:, j, :],
                        in_=mixT_loc[j * 128:(j + 1) * 128, :]).then_inc(s_ld.h, 16)
    p.on("sync", loads)
    ld_v = s_ld.bump(16 * (NDC + (1 if x_src is not None else 0)))

    for grp in range(4):
        def wload(e, grp=grp):
            if grp == 0:
                for (sm, v) in deps:
                    e.wait_ge(sm.h, v)
            src = w_dram[grp * 512:(grp + 1) * 512, :].rearrange("(a p) d -> p a d", p=128)
            e.dma_start(out=w_sb[:, grp * 4:(grp + 1) * 4, :], in_=src).then_inc(s_w.h, 16)
        p.on("gpsimd", wload)
        s_w.bump(16)
    w_all = s_w.n

    base_res = s_res.n
    for tt in range(NTT):
        bank = ps_o[tt % 2]

        def mm(e, tt=tt, bank=bank, res_need=base_res + max(0, tt - 1), first=(tt == 0)):
            if first:
                e.wait_ge(s_ld.h, ld_v)
                e.wait_ge(s_w.h, w_all)
            e.wait_ge(s_res.h, res_need)
            ins = None
            for j in range(NDC):
                for nch in range(4):
                    ins = e.matmul(bank[:, nch * 512:(nch + 1) * 512],
                                   ctxT[:, j, tt * 128:(tt + 1) * 128],
                                   w_sb[:, j, nch * 512:(nch + 1) * 512],
                                   start=(j == 0), stop=(j == NDC - 1))
            ins.then_inc(s_mm.h, 1)
        p.on("tensor", mm)
        mm_v = s_mm.bump()

        def res(e, tt=tt, bank=bank, mm_v=mm_v):
            e.wait_ge(s_mm.h, mm_v)
            e.tensor_tensor(x_sb[:, tt, :], x_sb[:, tt, :], bank[:], ALU.add).then_inc(s_res.h, 1)
        p.on("vector", res)
        s_res.bump()

    return s_res, s_res.n



def emit_aux_load(p: Prog, dram_ap, shape, dtype, tag):
    sb = p.sb(f"auxl_{tag}", shape, dtype)
    s = p.sem("auxl")

    def lf(e):
        e.dma_start(out=sb[:], in_=dram_ap).then_inc(s.h, 16)
    p.on("sync", lf)
    return sb, (s, s.bump(16))


def emit_attn_a(p: Prog, d, aux, x_sb, xdep, tag="a"):
    with p.scope():
        qT = p.sb(f"qT_{tag}", [128, 2, S], BF16)
        kT = p.sb(f"kT_{tag}", [128, 1, S], BF16)
        vT = p.sb(f"vT_{tag}", [128, 1, S], BF16)

        with p.scope():
            t_bf = p.sb(f"t_{tag}", [128, NTT, D], BF16)
            s_ln, lnv = emit_ln(p, x_sb, t_bf, [xdep], f"{tag}")
            emit_allgather_tm(p, t_bf, d[f"cc_t_{tag}_in"], d[f"cc_t_{tag}"],
                              [(s_ln, lnv)], f"{tag}t")
            bar0 = p.barrier()

        with p.scope():
            specs = [
                (d[f"wq_sl_{tag}"], 0, 2, qT),
                (d[f"wk_sl_{tag}"], 0, 1, kT),
                (d[f"wv_sl_{tag}"], 0, 1, vT),
            ]
            emit_attn_proj(p, d[f"cc_t_{tag}"], specs, [bar0], f"{tag}")
            bar1 = p.barrier()

        with p.scope():
            v_tm = p.sb(f"vtm_{tag}", [128, NCH, 128], BF16)
            mask4, m4dep = emit_aux_load(p, d["mask4"], [128, 4, 512], BF16, f"m4{tag}",
                                         deps=[bar1])
            with p.scope():
                emit_transpose_fm_tm(p, vT[:, 0, :], v_tm, NCH, aux["ident"],
                                     [bar1], f"{tag}v")
                bar_tp = p.barrier(engines=("tensor", "vector"))
            s_mx = p.sem("mx")
            hdep = [bar1, bar_tp, m4dep]
            for h in range(2):
                with p.scope():
                    outT = p.sb(f"mixT_{tag}_{h}", [128, S], BF16)
                    s_fl, flv = emit_flash_head(p, qT[:, h, :], kT[:, 0, :], v_tm, outT,
                                                aux["ones_bf"], aux["ones1_f"], mask4,
                                                hdep, f"{tag}h{h}", ones_f=aux["ones_f"])

                    def mxo(e, h=h, outT=outT, flv=flv, s_fl=s_fl):
                        e.wait_ge(s_fl.h, flv)
                        for j in range(R):
                            e.dma_start(out=d[f"cc_mix_{tag}_in"][256 * j + 128 * h:256 * j + 128 * (h + 1), :],
                                        in_=outT[:, j * TOK:(j + 1) * TOK]).then_inc(s_mx.h, 16)
                    p.on("scalar", mxo)
                    s_mx.bump(16 * R)
                    bar_h = p.barrier(engines=("tensor", "scalar", "vector"))
                    hdep = [bar_h]

            def agm(e, mxv=s_mx.n):
                from concourse.aot_env import allow_ici_all_to_all
                e.wait_ge(s_mx.h, mxv)
                with allow_ici_all_to_all():
                    e.collective_compute(
                        "AllToAll", ALU.bypass,
                        replica_groups=[list(range(R))],
                        ins=[d[f"cc_mix_{tag}_in"]],
                        outs=[d[f"cc_mix_{tag}"]],
                    ).then_inc(s_mx.h, 1)
            p.on("gpsimd", agm)
            s_mx.bump(1)

        bar2 = p.barrier()

    with p.scope():
        s_res, resv = emit_out_proj(p, d[f"cc_mix_{tag}"], d[f"w_o_{tag}"], None,
                                    x_sb, [bar2], f"{tag}")
    return s_res, resv






def emit_ln_dram(p: Prog, x_sb, trb, trb_lo, deps, tag):
    p.constv(EPS_LN)
    p.constv(-0.5)
    sq = p.sb(f"lnd_sq_{tag}", [128, D], F32)
    ssq = p.sb(f"lnd_ssq_{tag}", [128, 1], F32)
    sx = p.sb(f"lnd_sx_{tag}", [128, 1], F32)
    mean = p.sb(f"lnd_mean_{tag}", [128, 1], F32)
    var = p.sb(f"lnd_var_{tag}", [128, 1], F32)
    rstd = p.sb(f"lnd_rstd_{tag}", [128, 1], F32)
    tf = p.sb(f"lnd_tf_{tag}", [128, D], F32)
    stage = [p.sb(f"lnd_st_{tag}_{b}", [128, D], BF16) for b in range(2)]
    stage_lo = [p.sb(f"lnd_sl_{tag}_{b}", [128, D], BF16) for b in range(2)] if trb_lo is not None else None
    ndma = 2 if trb_lo is not None else 1

    s_sq = p.sem("lnd_sq")
    s_st = p.sem("lnd_st")
    s_rs = p.sem("lnd_rs")
    s_done = p.sem("lnd_done")
    s_tb = p.sem("lnd_tb")
    base_tb = s_tb.n

    trb_r = trb.rearrange("(t p) x -> t p x", p=128)
    trb_lo_r = trb_lo.rearrange("(t p) x -> t p x", p=128) if trb_lo is not None else None

    for i in range(NTT):
        xt = x_sb[:, i, :]
        st_war = s_st.n

        def act1(e, xt=xt, i=i, st_war=st_war):
            if i == 0:
                for (sm, v) in deps:
                    e.wait_ge(sm.h, v)
            else:
                e.wait_ge(s_st.h, st_war)
            e.activation(sq[:], xt, AF.Square, accum_out=ssq[:]).then_inc(s_sq.h, 1)
        p.on("scalar", act1)
        sq_v = s_sq.bump()

        def dve1(e, xt=xt, sq_v=sq_v, i=i):
            if i == 0:
                for (sm, v) in deps:
                    e.wait_ge(sm.h, v)
            e.reduce_sum(sx[:], xt, axis=AX.X)
            e.drain()
            e.wait_ge(s_sq.h, sq_v)
            e.tensor_scalar_mul(mean[:], sx[:], 1.0 / D)
            e.drain()
            e.tensor_tensor(sx[:], mean[:], mean[:], ALU.mult)
            e.tensor_scalar_mul(var[:], ssq[:], 1.0 / D)
            e.drain()
            e.tensor_tensor(var[:], var[:], sx[:], ALU.subtract).then_inc(s_st.h, 1)
        p.on("vector", dve1)
        st_v = s_st.bump()

        def act2(e, st_v=st_v):
            e.wait_ge(s_st.h, st_v)
            e.activation(var[:], var[:], AF.Ln, bias=EPS_LN)
            e.drain()
            e.activation(rstd[:], var[:], AF.Exp, scale=-0.5).then_inc(s_rs.h, 1)
        p.on("scalar", act2)
        rs_v = s_rs.bump()

        def dve2(e, xt=xt, i=i, rs_v=rs_v,
                 tb_war=base_tb + i * 16 * ndma):
            e.wait_ge(s_rs.h, rs_v)
            e.wait_ge(s_tb.h, tb_war)
            b = i % 2
            if trb_lo is not None:
                e.tensor_scalar(tf[:], xt, mean[:], rstd[:], ALU.subtract, ALU.mult)
                e.drain()
                e.tensor_copy(stage[b][:], tf[:])
                e.drain()
                e.tensor_tensor(stage_lo[b][:], tf[:], stage[b][:],
                                ALU.subtract).then_inc(s_done.h, 1)
            else:
                e.tensor_scalar(stage[b][:], xt, mean[:], rstd[:],
                                ALU.subtract, ALU.mult).then_inc(s_done.h, 1)
        p.on("vector", dve2)
        dn_v = s_done.bump()

        def tdma(e, i=i, dn_v=dn_v):
            e.wait_ge(s_done.h, dn_v)
            b = i % 2
            e.dma_start(out=trb_r[i], in_=stage[b][:]).then_inc(s_tb.h, 16)
            if trb_lo is not None:
                e.dma_start(out=trb_lo_r[i], in_=stage_lo[b][:]).then_inc(s_tb.h, 16)
        p.on("scalar", tdma)
        s_tb.bump(16 * ndma)

    return s_tb, s_tb.n


def emit_ffn(p: Prog, d, aux, x_sb, deps, tag, upto="full", dbg=None):
    BIG = 1.0e9
    NM_E = F_EXP // 128

    tT = p.sb(f"ffn_tT_{tag}", [128, NDC, TOK], BF16)
    wrows = p.sb(f"ffn_wr_{tag}", [1, N_EXP, TOK], BF16)
    s_tb = p.sem("ffn_tb")
    s_tt = p.sem("ffn_tt")
    s_rt = p.sem("ffn_rt")

    with p.scope():
        s_tbl, tbv = emit_ln_dram(p, x_sb, d[f"trb_{tag}"], d[f"trb_lo_{tag}"],
                                  deps, f"{tag}")

        def tbfwd(e, tbv=tbv):
            e.wait_ge(s_tbl.h, tbv)
            e.sem_inc(s_tb.h, 1)
        p.on("scalar", tbfwd)
        tbv = s_tb.bump(1)
        p.barrier(engines=("vector", "scalar"))

    def treads(e, tbv=tbv):
        e.wait_ge(s_tb.h, tbv)
        for j in range(NDC):
            e.dma_start_transpose(tT[:, j, :],
                                  d[f"trb_{tag}"][:, j * 128:(j + 1) * 128]).then_inc(s_tt.h, 16)
    p.on("sync", treads)
    ttv = s_tt.bump(16 * NDC)

    with p.scope():
        tT_lo = p.sb(f"ffn_tTlo_{tag}", [128, NDC, TOK], BF16)

        def treads_lo(e, tbv=tbv):
            e.wait_ge(s_tb.h, tbv)
            for j in range(NDC):
                e.dma_start_transpose(tT_lo[:, j, :],
                                      d[f"trb_lo_{tag}"][:, j * 128:(j + 1) * 128]).then_inc(s_tt.h, 16)
        p.on("sync", treads_lo)
        ttv_lo = s_tt.bump(16 * NDC)

        gw_f = p.sb(f"ffn_gwf_{tag}", [128, NDC, 8], F32)
        gw_hi = p.sb(f"ffn_gwh_{tag}", [128, NDC, 8], BF16)
        gw_lo = p.sb(f"ffn_gwl_{tag}", [128, NDC, 8], BF16)

        def gwload(e):
            e.dma_start(out=gw_f[:], in_=d["gwT"].rearrange("(a p) n -> p a n", p=128)).then_inc(s_rt.h, 16)
        p.on("sync", gwload)
        rt_gw = s_rt.bump(16)

        def gwsplit(e, rt_gw=rt_gw):
            e.wait_ge(s_rt.h, rt_gw)
            e.tensor_copy(gw_hi[:], gw_f[:])
            e.drain()
            e.tensor_tensor(gw_lo[:], gw_f[:], gw_hi[:], ALU.subtract).then_inc(s_rt.h, 1)
        p.on("vector", gwsplit)
        rt_sp = s_rt.bump()

        ps_lg = p.ps(f"ffn_pslg_{tag}", [128, 8])
        ps_wr = p.ps(f"ffn_pswr_{tag}", [16, 128], BF16)
        lg = p.sb(f"ffn_lg_{tag}", [128, 8], F32)
        elg = p.sb(f"ffn_elg_{tag}", [128, 8], F32)
        ssum = p.sb(f"ffn_ssum_{tag}", [128, 1], F32)
        sc = p.sb(f"ffn_sc_{tag}", [128, 8], F32)
        gs = p.sb(f"ffn_gs_{tag}", [128, 4], F32)
        gx = p.sb(f"ffn_gx_{tag}", [128, 4], F32)
        mcol = p.sb(f"ffn_mcol_{tag}", [128, 1], F32)
        sm8 = p.sb(f"ffn_sm8_{tag}", [128, 8], F32)
        msk = p.sb(f"ffn_msk_{tag}", [128, 8], F32)
        sel = p.sb(f"ffn_sel_{tag}", [128, 8], F32)
        wm = p.sb(f"ffn_wm_{tag}", [128, 8], F32)
        wm_pad = p.sb(f"ffn_wmpad_{tag}", [128, 128], BF16)
        s_lg = p.sem("ffn_lg")
        s_wr = p.sem("ffn_wrow")

        for tt in range(NTT):
            def mml(e, tt=tt, first=(tt == 0), rt_sp=rt_sp, ttv=ttv, ttv_lo=ttv_lo,
                    lg_done=s_lg.n):
                if first:
                    e.wait_ge(s_rt.h, rt_sp)
                    e.wait_ge(s_tt.h, ttv_lo)
                else:
                    e.wait_ge(s_lg.h, lg_done)
                ins = None
                for j in range(NDC):
                    sl = tT[:, j, tt * 128:(tt + 1) * 128]
                    sl_lo = tT_lo[:, j, tt * 128:(tt + 1) * 128]
                    ins = e.matmul(ps_lg[:], sl, gw_hi[:, j, :], start=(j == 0), stop=False)
                    ins = e.matmul(ps_lg[:], sl, gw_lo[:, j, :], start=False, stop=False)
                    ins = e.matmul(ps_lg[:], sl_lo, gw_hi[:, j, :], start=False,
                                   stop=(j == NDC - 1))
                ins.then_inc(s_lg.h, 1)
            p.on("tensor", mml)
            lg_v = s_lg.bump()

            def route(e, lg_v=lg_v):
                e.wait_ge(s_lg.h, lg_v)
                e.tensor_copy(lg[:], ps_lg[:])
                e.drain()
                e.engine_nop().then_inc(s_lg.h, 1)
            p.on("vector", route)
            lg_cp = s_lg.bump()

            def act_exp(e, lg_cp=lg_cp):
                e.wait_ge(s_lg.h, lg_cp)
                e.activation(elg[:], lg[:], AF.Exp, accum_out=ssum[:]).then_inc(s_rt.h, 1)
            p.on("scalar", act_exp)
            rt_e = s_rt.bump()

            def route2(e, rt_e=rt_e):
                e.wait_ge(s_rt.h, rt_e)
                e.reciprocal(ssum[:], ssum[:])
                e.drain()
                e.tensor_scalar_mul(sc[:], elg[:], ssum[:])
                e.drain()
                e.tensor_tensor(gs[:], sc[:, 0:8:2], sc[:, 1:8:2], ALU.max)
                e.drain()
                e.reduce_max(mcol[:], gs[:], axis=AX.X)
                e.drain()
                e.tensor_scalar(gx[:], gs[:], mcol[:], None, ALU.is_ge)
                e.drain()
                e.scalar_tensor_tensor(gx[:], gx[:], -BIG, gs[:], ALU.mult, ALU.add)
                e.drain()
                e.reduce_max(mcol[:], gx[:], axis=AX.X)
                e.drain()
                e.tensor_scalar(gx[:], gs[:], mcol[:], None, ALU.is_ge)
                e.drain()
                e.tensor_copy(sm8[:, 0:8:2], gx[:])
                e.drain()
                e.tensor_copy(sm8[:, 1:8:2], gx[:])
                e.drain()
                e.tensor_tensor(msk[:], sc[:], sm8[:], ALU.mult)
                e.drain()
                e.reduce_max(mcol[:], msk[:], axis=AX.X)
                e.drain()
                e.tensor_scalar(sel[:], msk[:], mcol[:], None, ALU.is_ge)
                e.drain()
                e.scalar_tensor_tensor(sel[:], sel[:], -BIG, msk[:], ALU.mult, ALU.add)
                e.drain()
                e.reduce_max(mcol[:], sel[:], axis=AX.X)
                e.drain()
                e.tensor_scalar(sel[:], msk[:], mcol[:], None, ALU.is_ge)
                e.drain()
                e.tensor_tensor(wm[:], sel[:], msk[:], ALU.mult)
                e.drain()
                e.memset(wm_pad[:], 0.0)
                e.drain()
                e.tensor_copy(wm_pad[:, 0:128:16], wm[:]).then_inc(s_rt.h, 1)
            p.on("vector", route2)
            rt_w = s_rt.bump()

            if upto == "logits":
                continue
            for n in range(N_EXP):
                def wrt(e, n=n, rt_w=rt_w, wr_prev=s_wr.n):
                    e.wait_ge(s_rt.h, rt_w)
                    if wr_prev > 0:
                        e.wait_ge(s_wr.h, wr_prev)
                    e.transpose(ps_wr[:], wm_pad[:, 16 * n:16 * (n + 1)],
                                aux["ident"][:]).then_inc(s_rt.h, 1)
                p.on("tensor", wrt)
                rt_t = s_rt.bump()

                def wre(e, n=n, tt=tt, rt_t=rt_t):
                    e.wait_ge(s_rt.h, rt_t)
                    e.tensor_copy(wrows[0:1, n, tt * 128:(tt + 1) * 128],
                                  ps_wr[0:1, :]).then_inc(s_wr.h, 1)
                p.on("vector", wre)
                s_wr.bump()
            if dbg is not None and "wm_to_x" in dbg:
                def wmdump(e, tt=tt, rt_w=rt_w):
                    e.wait_ge(s_rt.h, rt_w)
                    e.tensor_copy(x_sb[:, tt, 0:8], wm[:])
                    e.drain()
                    e.tensor_copy(x_sb[:, tt, 8:16], sc[:])
                    e.drain()
                    e.tensor_copy(x_sb[:, tt, 16:20], gs[:])
                    e.drain()
                    e.tensor_copy(x_sb[:, tt, 20:24], gx[:])
                    e.drain()
                    e.tensor_copy(x_sb[:, tt, 24:32], sm8[:]).then_inc(s_rt.h, 1)
                p.on("vector", wmdump)
                s_rt.bump(1)
        wr_all = s_wr.n
        if upto in ("router", "logits"):
            return s_rt, s_rt.n
        bar_b = p.barrier(engines=("tensor", "vector", "scalar"))

    with p.scope():
        NRING = 2
        wg_ring = [p.sb(f"ffn_wg_{tag}_{i}", [128, NDC, 128], BF16) for i in range(NRING)]
        wu_ring = [p.sb(f"ffn_wu_{tag}_{i}", [128, NDC, 128], BF16) for i in range(NRING)]
        wd_sb = p.sb(f"ffn_wd_{tag}", [128, NM_E, D], BF16)
        act_sb = p.sb(f"ffn_act_{tag}", [128, NM_E, TOK], BF16)
        bc_bf = p.sb(f"ffn_bc_{tag}", [128, TOK], BF16)
        sg_bf = [p.sb(f"ffn_sg_{tag}_{i}", [128, TOK], BF16) for i in range(2)]
        a1 = p.sb(f"ffn_a1_{tag}", [128, TOK], BF16)
        ps_g = [p.ps(f"ffn_psg_{tag}_{i}", [128, TOK]) for i in range(2)]
        ps_u = [p.ps(f"ffn_psu_{tag}_{i}", [128, TOK]) for i in range(2)]

        s_wl = p.sem("ffn_wl")
        s_mf = p.sem("ffn_mf")
        s_ae = p.sem("ffn_ae")
        s_av = p.sem("ffn_av")
        s_bc = p.sem("ffn_bc")
        s_dn = p.sem("ffn_dn")
        s_xa = p.sem("ffn_xa")
        s_dl = p.sem("ffn_dl")
        base_wl, base_mf, base_dn, base_dl = s_wl.n, s_mf.n, s_dn.n, s_dl.n

        blocks = [("e", n) for n in range(N_EXP)] + [("s", h) for h in range(2)]
        SKIP_DOWN = False

        mseq = 0
        for bi, (kind, n) in enumerate(blocks):
            for m in range(NM_E):
                for (ring, wsel) in ((wg_ring, "g"), (wu_ring, "u")):
                    if kind == "e":
                        wtns = d["gate_wT"] if wsel == "g" else d["up_wT"]
                        src = wtns[n, :, m * 128:(m + 1) * 128]
                    else:
                        wtns = d["sh_gate_wT"] if wsel == "g" else d["sh_up_wT"]
                        c0 = n * F_EXP + m * 128
                        src = wtns[:, c0:c0 + 128]
                    src = src.rearrange("(a p) c -> p a c", p=128)

                    def wl(e, src=src, ring=ring, mseq=mseq, bar_b=bar_b,
                           need=base_mf + max(0, mseq - 1)):
                        e.wait_ge(bar_b[0].h, bar_b[1])
                        if need > base_mf:
                            e.wait_ge(s_mf.h, need)
                        e.dma_start(out=ring[mseq % NRING][:], in_=src).then_inc(s_wl.h, 16)
                    p.on("gpsimd", wl)
                    s_wl.bump(16)
                mseq += 1
            for k in range(NM_E):
                if kind == "e":
                    dsrc = d["down_wT"][n, k * 128:(k + 1) * 128, :]
                else:
                    r0 = n * F_EXP + k * 128
                    dsrc = d["sh_down_wT"][r0:r0 + 128, :]

                def dl(e, dsrc=dsrc, k=k, bi=bi, bar_b=bar_b,
                       dn_need=base_dn + bi * NTT):
                    e.wait_ge(bar_b[0].h, bar_b[1])
                    if bi > 0:
                        e.wait_ge(s_dn.h, dn_need)
                    e.dma_start(out=wd_sb[:, k, :], in_=dsrc).then_inc(s_dl.h, 16)
                p.on("gpsimd", dl)
                s_dl.bump(16)

        mseq = 0
        for bi, (kind, n) in enumerate(blocks):
            routed = (kind == "e")
            if routed:
                def bcast(e, n=n, wr_all=wr_all, bar_b=bar_b, bi=bi,
                          av_need=s_av.n, xa_need=s_xa.n):
                    e.wait_ge(s_wr.h, wr_all)
                    e.wait_ge(bar_b[0].h, bar_b[1])
                    if bi > 0:
                        e.wait_ge(s_av.h, av_need)
                        e.wait_ge(s_xa.h, xa_need)
                    for nck in range(TOK // 512):
                        e.matmul(ps_g[0][:, nck * 512:(nck + 1) * 512], aux["ones1_bf"][:],
                                 wrows[0:1, n, nck * 512:(nck + 1) * 512],
                                 start=True, stop=True).then_inc(s_bc.h, 1)
                p.on("tensor", bcast)
                bc_f = s_bc.bump(2)

                def bccp(e, bc_f=bc_f):
                    e.wait_ge(s_bc.h, bc_f)
                    e.activation(bc_bf[:], ps_g[0][:], AF.Copy).then_inc(s_bc.h, 1)
                p.on("scalar", bccp)
                bc_c = s_bc.bump()
            else:
                bc_c = s_bc.n

            for m in range(NM_E):
                mb = mseq % 2
                wg = wg_ring[mseq % NRING]
                wu = wu_ring[mseq % NRING]

                def mmg(e, mb=mb, wg=wg, wu=wu, wl_need=base_wl + 32 * (mseq + 1),
                        bc_c=bc_c, m=m, routed=routed, bi=bi,
                        av_need=max(0, s_av.n - 1), xa_need=s_xa.n if m <= 1 and bi > 0 else 0):
                    e.wait_ge(s_wl.h, wl_need)
                    if m == 0 and routed:
                        e.wait_ge(s_bc.h, bc_c)
                    if xa_need > 0:
                        e.wait_ge(s_xa.h, xa_need)
                    e.wait_ge(s_av.h, av_need)
                    ins = None
                    for j in range(NDC):
                        for nck in range(TOK // 512):
                            ins = e.matmul(ps_g[mb][:, nck * 512:(nck + 1) * 512],
                                           wg[:, j, :], tT[:, j, nck * 512:(nck + 1) * 512],
                                           start=(j == 0), stop=(j == NDC - 1))
                    for j in range(NDC):
                        for nck in range(TOK // 512):
                            ins = e.matmul(ps_u[mb][:, nck * 512:(nck + 1) * 512],
                                           wu[:, j, :], tT[:, j, nck * 512:(nck + 1) * 512],
                                           start=(j == 0), stop=(j == NDC - 1))
                    ins.then_inc(s_mf.h, 1)
                p.on("tensor", mmg)
                mf_v = s_mf.bump()

                def silu(e, mb=mb, mf_v=mf_v):
                    e.wait_ge(s_mf.h, mf_v)
                    e.activation(sg_bf[mb][:], ps_g[mb][:], AF.Silu).then_inc(s_ae.h, 1)
                p.on("scalar", silu)
                ae_v = s_ae.bump()

                def acte(e, m=m, mb=mb, ae_v=ae_v, routed=routed,
                         dn_need=base_dn + bi * NTT):
                    e.wait_ge(s_ae.h, ae_v)
                    if dn_need > base_dn:
                        e.wait_ge(s_dn.h, dn_need)
                    if routed:
                        e.tensor_tensor(a1[:], sg_bf[mb][:], ps_u[mb][:], ALU.mult)
                        e.drain()
                        e.tensor_tensor(act_sb[:, m, :], a1[:], bc_bf[:],
                                        ALU.mult).then_inc(s_av.h, 1)
                    else:
                        e.tensor_tensor(act_sb[:, m, :], sg_bf[mb][:], ps_u[mb][:],
                                        ALU.mult).then_inc(s_av.h, 1)
                p.on("vector", acte)
                s_av.bump()
                mseq += 1

            for tt in range(NTT if not SKIP_DOWN else 0):
                def mmd(e, tt=tt, av_all=s_av.n, dl_need=base_dl + 16 * NM_E * (bi + 1),
                        xa_need=s_xa.n):
                    e.wait_ge(s_av.h, av_all)
                    e.wait_ge(s_dl.h, dl_need)
                    if xa_need > 0:
                        e.wait_ge(s_xa.h, xa_need)
                    ins = None
                    if True:
                        for k in range(NM_E):
                            for nch in range(4):
                                bank = ps_g[nch // 2]
                                tgt = bank[:, (nch % 2) * 512:(nch % 2 + 1) * 512]
                                ins = e.matmul(tgt, act_sb[:, k, tt * 128:(tt + 1) * 128],
                                               wd_sb[:, k, nch * 512:(nch + 1) * 512],
                                               start=(k == 0), stop=(k == NM_E - 1))
                    ins.then_inc(s_dn.h, 1)
                p.on("tensor", mmd)
                dn_v = s_dn.bump()

                def xadd(e, tt=tt, dn_v=dn_v):
                    e.wait_ge(s_dn.h, dn_v)
                    e.tensor_tensor(x_sb[:, tt, 0:TOK], x_sb[:, tt, 0:TOK],
                                    ps_g[0][:], ALU.add)
                    e.drain()
                    e.tensor_tensor(x_sb[:, tt, TOK:D], x_sb[:, tt, TOK:D],
                                    ps_g[1][:], ALU.add).then_inc(s_xa.h, 1)
                p.on("vector", xadd)
                s_xa.bump()

    return s_xa, s_xa.n



def emit_delta_pair(p: Prog, heads, aux, deps, tag):
    p.constv(ATT_SCALE)
    GRP = 8
    NG = NCH // GRP
    GW = GRP * 128
    NH = len(heads)

    s_tf = p.sem("dl_tf")
    s_af = p.sem("dl_af")
    s_ae = p.sem("dl_ae")
    s_aa = p.sem("dl_aa")
    s_bf = p.sem("dl_bf")
    s_be = p.sem("dl_be")
    base_be0 = s_be.n

    ctxs = []
    for h in range(NH):
        c = dict(heads[h])
        c["qh_g"] = [p.sb(f"dl_qh_{tag}{h}_{b}", [128, GW], BF16) for b in range(2)]
        c["kt_g"] = [p.sb(f"dl_kt_{tag}{h}_{b}", [128, GW], BF16) for b in range(2)]
        c["kb_g"] = [p.sb(f"dl_kb_{tag}{h}_{b}", [128, GW], BF16) for b in range(2)]
        c["vb_g"] = [p.sb(f"dl_vb_{tag}{h}_{b}", [128, GW], BF16) for b in range(2)]
        c["vb_tm"] = p.sb(f"dl_vbtm_{tag}{h}", [128, 128], BF16)
        c["kb_tm"] = p.sb(f"dl_kbtm_{tag}{h}", [128, 128], BF16)
        c["Lm"] = p.sb(f"dl_L_{tag}{h}", [128, 128], BF16)
        c["LTm"] = p.sb(f"dl_LT_{tag}{h}", [128, 128], BF16)
        c["Pw"] = [p.sb(f"dl_pw_{tag}{h}_{i}", [128, 128], BF16) for i in range(5)]
        c["PwT"] = [p.sb(f"dl_pwt_{tag}{h}_{i}", [128, 128], BF16) for i in range(2)]
        c["Y"] = p.sb(f"dl_Y_{tag}{h}", [128, 128], BF16)
        c["TT"] = p.sb(f"dl_TT_{tag}{h}", [128, 128], BF16)
        c["u_sb"] = [p.sb(f"dl_u_{tag}{h}_{b}", [128, 128], BF16) for b in range(2)]
        c["wT_sb"] = [p.sb(f"dl_w_{tag}{h}_{b}", [128, 128], BF16) for b in range(2)]
        c["at_sb"] = [p.sb(f"dl_at_{tag}{h}_{b}", [128, 128], BF16) for b in range(2)]
        c["ktm_sb"] = [p.sb(f"dl_ktm_{tag}{h}_{b}", [128, 128], BF16) for b in range(2)]
        c["u2"] = p.sb(f"dl_u2_{tag}{h}", [128, 128], BF16)
        c["oc"] = p.sb(f"dl_oc_{tag}{h}", [128, 128], BF16)
        c["S_f"] = p.sb(f"dl_Sf_{tag}{h}", [128, 128], F32)
        c["S_b"] = p.sb(f"dl_Sb_{tag}{h}", [128, 128], BF16)
        c["ps_a"] = p.ps(f"dl_psa_{tag}{h}", [128, 256])
        c["ps_t"] = p.ps(f"dl_pst_{tag}{h}", [128, 128], BF16)
        c["ps_b"] = p.ps(f"dl_psb_{tag}{h}", [128, 256])
        c["grp_done"] = {}
        c["a_done"] = {}
        c["ae_own"] = 0
        c["aa_own"] = 0
        c["sb_ready"] = 0
        ctxs.append(c)
    eq = p.sb(f"dl_eq_{tag}", [128, GW], BF16)
    tmpg = p.sb(f"dl_tmpg_{tag}", [128, GW], F32)
    den = p.sb(f"dl_den_{tag}", [1, GW], F32)
    akg = p.sb(f"dl_ak_{tag}", [128, GW], BF16)
    e2g = p.sb(f"dl_e2_{tag}", [128, GW], F32)
    mng = p.sb(f"dl_mn_{tag}", [128, GW], F32)
    gcg = p.sb(f"dl_gc_{tag}", [128, GW], BF16)
    ps_r = p.ps(f"dl_psr_{tag}", [128, 512])

    def sinit(e, cs=tuple(c["S_f"] for c in ctxs) + tuple(c["S_b"] for c in ctxs)):
        for (sm, v) in deps:
            e.wait_ge(sm.h, v)
        for t in cs:
            e.memset(t[:], 0.0)
            e.drain()
        e.engine_nop().then_inc(s_be.h, 1)
    p.on("vector", sinit)
    s_be.bump()
    for c_ in ctxs:
        c_["sb_ready"] = s_be.n

    def emit_group(hc, g):
        b = g % 2
        qT, kT, vT = hc["qT"], hc["kT"], hc["vT"]
        gsl = slice(g * GW, (g + 1) * GW)
        war = base_be0 + 1 + max(0, (NH * 3 * GRP) * (g - 1))

        def t_act1(e, g=g, gsl=gsl, war=war, prev_grp=s_tf.n, qT=qT):
            if g == 0:
                for (sm, v) in deps:
                    e.wait_ge(sm.h, v)
            e.wait_ge(s_tf.h, prev_grp)
            e.wait_ge(s_be.h, war)
            e.activation(eq[:], qT[:, gsl], AF.Exp).then_inc(s_tf.h, 1)
        p.on("scalar", t_act1)
        v1 = s_tf.bump()

        def softmax_pipe(dst, scale_after, v_ready):
            vr = v_ready
            for ncl in range(2):
                def pe_row(e, ncl=ncl, vr=vr):
                    e.wait_ge(s_tf.h, vr)
                    e.matmul(ps_r[0:1, 0:512], aux["ones_bf"][:],
                             eq[:, ncl * 512:(ncl + 1) * 512],
                             start=True, stop=True).then_inc(s_tf.h, 1)
                p.on("tensor", pe_row)
                v_r = s_tf.bump()

                def dv_rec(e, ncl=ncl, v_r=v_r):
                    e.wait_ge(s_tf.h, v_r)
                    e.tensor_copy(den[0:1, ncl * 512:(ncl + 1) * 512], ps_r[0:1, 0:512])
                    e.drain()
                    e.reciprocal(den[0:1, ncl * 512:(ncl + 1) * 512],
                                 den[0:1, ncl * 512:(ncl + 1) * 512])
                    e.drain()
                    e.engine_nop().then_inc(s_tf.h, 1)
                p.on("vector", dv_rec)
                v_c = s_tf.bump()

                def pe_bc(e, ncl=ncl, v_c=v_c):
                    e.wait_ge(s_tf.h, v_c)
                    e.matmul(ps_r[:, 0:512], aux["ones1_f"][:],
                             den[0:1, ncl * 512:(ncl + 1) * 512],
                             start=True, stop=True).then_inc(s_tf.h, 1)
                p.on("tensor", pe_bc)
                v_b = s_tf.bump()

                def dv_fin(e, dst=dst, ncl=ncl, v_b=v_b, scale_after=scale_after):
                    e.wait_ge(s_tf.h, v_b)
                    csl = slice(ncl * 512, (ncl + 1) * 512)
                    e.tensor_tensor(tmpg[:, csl], eq[:, csl], ps_r[:, 0:512], ALU.mult)
                    e.drain()
                    if scale_after is None:
                        e.tensor_scalar(dst[:, csl], tmpg[:, csl], 1e-6, 1.0 - 1e-6,
                                        ALU.max, ALU.min).then_inc(s_tf.h, 1)
                    else:
                        e.tensor_scalar(tmpg[:, csl], tmpg[:, csl], 1e-6, 1.0 - 1e-6,
                                        ALU.max, ALU.min)
                        e.drain()
                        e.tensor_scalar_mul(dst[:, csl], tmpg[:, csl],
                                            float(scale_after)).then_inc(s_tf.h, 1)
                p.on("vector", dv_fin)
                vr = s_tf.bump()
            return vr

        vq = softmax_pipe(hc["qh_g"][b], ATT_SCALE, v1)

        def t_act2(e, gsl=gsl, vq=vq, kT=kT):
            e.wait_ge(s_tf.h, vq)
            e.activation(eq[:], kT[:, gsl], AF.Exp).then_inc(s_tf.h, 1)
        p.on("scalar", t_act2)
        v2 = s_tf.bump()
        vk = softmax_pipe(hc["kt_g"][b], None, v2)

        def t_gate_a(e, gsl=gsl, vk=vk, kT=kT):
            e.wait_ge(s_tf.h, vk)
            e.activation(akg[:], kT[:, gsl], AF.Abs)
            e.drain()
            e.activation(e2g[:], akg[:], AF.Exp, scale=-1.0)
            e.drain()
            e.activation(e2g[:], e2g[:], AF.Ln, bias=1.0).then_inc(s_tf.h, 1)
        p.on("scalar", t_gate_a)
        v4 = s_tf.bump()

        def t_gate_d2(e, gsl=gsl, b=b, v4=v4, hc=hc, vT=vT, kT=kT):
            e.wait_ge(s_tf.h, v4)
            e.tensor_scalar(mng[:], kT[:, gsl], 0.0, None, ALU.min)
            e.drain()
            e.tensor_tensor(mng[:], mng[:], e2g[:], ALU.subtract)
            e.drain()
            e.tensor_scalar(mng[:], mng[:], 1.0 / 16.0, -16.0, ALU.mult, ALU.max)
            e.drain()
            e.tensor_scalar(gcg[:], mng[:], 16.0, None, ALU.min)
            e.drain()
            e.tensor_tensor(hc["kb_g"][b][:], hc["kt_g"][b][:], gcg[:], ALU.mult)
            e.drain()
            e.tensor_tensor(hc["vb_g"][b][:], vT[:, gsl], gcg[:], ALU.mult).then_inc(s_tf.h, 1)
        p.on("vector", t_gate_d2)
        return s_tf.bump()

    def wait_evacs(e, ae_n, aa_n):
        e.wait_ge(s_ae.h, ae_n)
        e.wait_ge(s_aa.h, aa_n)

    def gen_A(hc, c):
        g = c // GRP
        b2 = c % 2
        bg = g % 2
        lsl = slice((c % GRP) * 128, (c % GRP + 1) * 128)
        qh, kt_, kb, vb = (hc["qh_g"][bg], hc["kt_g"][bg], hc["kb_g"][bg],
                          hc["vb_g"][bg])
        ps_a, ps_t = hc["ps_a"], hc["ps_t"]
        gd = hc["grp_done"][g]
        be_war = s_be.n

        for (srcbuf, dst) in ((vb, hc["vb_tm"]), (kb, hc["kb_tm"]),
                              (kt_, hc["ktm_sb"][b2])):
            def tp_pe(e, srcbuf=srcbuf, lsl=lsl, gd=gd, be_war=be_war,
                      ae_n=hc["ae_own"], aa_n=hc["aa_own"], first=(srcbuf is vb)):
                if first:
                    e.wait_ge(s_tf.h, gd)
                    if be_war > base_be0 + 1:
                        e.wait_ge(s_be.h, be_war)
                wait_evacs(e, ae_n, aa_n)
                e.transpose(ps_t[:], srcbuf[:, lsl], aux["ident"][:]).then_inc(s_af.h, 1)
            p.on("tensor", tp_pe)
            af_v = s_af.bump()

            def tp_cp(e, ps_t=ps_t, dst=dst, af_v=af_v):
                e.wait_ge(s_af.h, af_v)
                e.copy(dst[:], ps_t[:]).then_inc(s_aa.h, 1)
            p.on("scalar", tp_cp)
            hc["aa_own"] = s_aa.bump()
            yield

        def llt_pe(e, kb=kb, kt_=kt_, lsl=lsl, ae_n=hc["ae_own"], aa_n=hc["aa_own"]):
            wait_evacs(e, ae_n, aa_n)
            e.matmul(ps_a[:, 0:128], kb[:, lsl], kt_[:, lsl], start=True, stop=True)
            e.matmul(ps_a[:, 128:256], kt_[:, lsl], kb[:, lsl], start=True,
                     stop=True).then_inc(s_af.h, 1)
        p.on("tensor", llt_pe)
        af_v = s_af.bump()

        def llt_dv(e, hc=hc, af_v=af_v):
            e.wait_ge(s_af.h, af_v)
            e.tensor_tensor(hc["Lm"][:], ps_a[:, 0:128], aux["m_sl"][:], ALU.mult)
            e.tensor_tensor(hc["LTm"][:], ps_a[:, 128:256], aux["m_su"][:],
                            ALU.mult).then_inc(s_ae.h, 1)
        p.on("vector", llt_dv)
        hc["ae_own"] = s_ae.bump()
        yield

        def at_pe(e, kt_=kt_, qh=qh, lsl=lsl, ae_n=hc["ae_own"], aa_n=hc["aa_own"]):
            wait_evacs(e, ae_n, aa_n)
            e.matmul(ps_a[:, 0:128], kt_[:, lsl], qh[:, lsl], start=True,
                     stop=True).then_inc(s_af.h, 1)
        p.on("tensor", at_pe)
        af_v = s_af.bump()

        def at_dv(e, hc=hc, b2=b2, af_v=af_v):
            e.wait_ge(s_af.h, af_v)
            e.tensor_tensor(hc["at_sb"][b2][:], ps_a[:, 0:128], aux["m_iu"][:],
                            ALU.mult).then_inc(s_ae.h, 1)
        p.on("vector", at_dv)
        hc["ae_own"] = s_ae.bump()
        yield

        for lev in range(6):
            curL = hc["Lm"] if lev == 0 else hc["Pw"][lev - 1]
            curLT = hc["LTm"] if lev == 0 else hc["PwT"][(lev - 1) % 2]
            last = (lev == 5)

            def p_pe(e, curL=curL, curLT=curLT, last=last, ae_n=hc["ae_own"],
                     aa_n=hc["aa_own"]):
                wait_evacs(e, ae_n, aa_n)
                if not last:
                    e.matmul(ps_a[:, 0:128], curLT[:], curL[:], start=True, stop=True)
                e.matmul(ps_a[:, 128:256], curL[:], curLT[:], start=True,
                         stop=True).then_inc(s_af.h, 1)
            p.on("tensor", p_pe)
            af_v = s_af.bump()

            def p_cp(e, hc=hc, lev=lev, last=last, af_v=af_v):
                e.wait_ge(s_af.h, af_v)
                if not last:
                    e.copy(hc["Pw"][lev][:], ps_a[:, 0:128])
                e.copy(hc["PwT"][lev % 2][:], ps_a[:, 128:256]).then_inc(s_aa.h, 1)
            p.on("scalar", p_cp)
            hc["aa_own"] = s_aa.bump()
            yield

        def y_init(e, hc=hc, ae_n=hc["ae_own"], aa_n=hc["aa_own"]):
            wait_evacs(e, ae_n, aa_n)
            e.tensor_tensor(hc["Y"][:], hc["PwT"][1][:], aux["eye_bf"][:],
                            ALU.add).then_inc(s_ae.h, 1)
        p.on("vector", y_init)
        hc["ae_own"] = s_ae.bump()
        yield

        prod_lhs = [hc["Pw"][4], hc["Pw"][3], hc["Pw"][2], hc["Pw"][1],
                    hc["Pw"][0], hc["Lm"]]
        for pi, lhs in enumerate(prod_lhs):
            lastp = (pi == 5)

            def pr_pe(e, lhs=lhs, hc=hc, ae_n=hc["ae_own"], aa_n=hc["aa_own"]):
                wait_evacs(e, ae_n, aa_n)
                e.matmul(ps_a[:, 0:128], lhs[:], hc["Y"][:], start=True,
                         stop=True).then_inc(s_af.h, 1)
            p.on("tensor", pr_pe)
            af_v = s_af.bump()

            def pr_dv(e, hc=hc, lastp=lastp, af_v=af_v):
                e.wait_ge(s_af.h, af_v)
                if lastp:
                    e.tensor_tensor(hc["TT"][:], hc["Y"][:], ps_a[:, 0:128],
                                    ALU.subtract).then_inc(s_ae.h, 1)
                else:
                    e.tensor_tensor(hc["Y"][:], hc["Y"][:], ps_a[:, 0:128],
                                    ALU.add).then_inc(s_ae.h, 1)
            p.on("vector", pr_dv)
            hc["ae_own"] = s_ae.bump()
            yield

        def uw_pe(e, hc=hc, ae_n=hc["ae_own"], aa_n=hc["aa_own"]):
            wait_evacs(e, ae_n, aa_n)
            e.matmul(ps_a[:, 0:128], hc["TT"][:], hc["vb_tm"][:], start=True, stop=True)
            e.matmul(ps_a[:, 128:256], hc["kb_tm"][:], hc["TT"][:], start=True,
                     stop=True).then_inc(s_af.h, 1)
        p.on("tensor", uw_pe)
        af_v = s_af.bump()

        def uw_cp(e, hc=hc, b2=b2, af_v=af_v):
            e.wait_ge(s_af.h, af_v)
            e.copy(hc["u_sb"][b2][:], ps_a[:, 0:128])
            e.copy(hc["wT_sb"][b2][:], ps_a[:, 128:256]).then_inc(s_aa.h, 1)
        p.on("scalar", uw_cp)
        hc["aa_own"] = s_aa.bump()
        hc["a_done"][c] = (s_ae.n, s_aa.n)
        yield

    def gen_B(hc, c):
        b2 = c % 2
        csl = slice(c * 128, (c + 1) * 128)
        lsl = slice((c % GRP) * 128, (c % GRP + 1) * 128)
        g = c // GRP
        qh = hc["qh_g"][g % 2]
        ps_b2, ps_t = hc["ps_b"], hc["ps_t"]
        a_done = hc["a_done"][c]

        def b_pe1(e, hc=hc, b2=b2, a_done=a_done, be_n=hc["sb_ready"]):
            wait_evacs(e, a_done[0], a_done[1])
            e.wait_ge(s_be.h, be_n)
            e.matmul(ps_b2[:, 0:128], hc["wT_sb"][b2][:], hc["S_b"][:], start=True,
                     stop=True).then_inc(s_bf.h, 1)
        p.on("tensor", b_pe1)
        bf1 = s_bf.bump()

        def b_dv1(e, hc=hc, b2=b2, bf1=bf1):
            e.wait_ge(s_bf.h, bf1)
            e.tensor_tensor(hc["u2"][:], hc["u_sb"][b2][:], ps_b2[:, 0:128],
                            ALU.subtract).then_inc(s_be.h, 1)
        p.on("vector", b_dv1)
        be1 = s_be.bump()
        yield

        def b_pe2(e, hc=hc, b2=b2, lsl=lsl, qh=qh, be1=be1):
            e.wait_ge(s_be.h, be1)
            e.matmul(ps_b2[:, 128:256], qh[:, lsl], hc["S_b"][:], start=True, stop=False)
            e.matmul(ps_b2[:, 128:256], hc["at_sb"][b2][:], hc["u2"][:], start=False,
                     stop=True)
            e.matmul(ps_b2[:, 0:128], hc["ktm_sb"][b2][:], hc["u2"][:], start=True,
                     stop=True).then_inc(s_bf.h, 1)
        p.on("tensor", b_pe2)
        bf2 = s_bf.bump()

        def b_dv2(e, hc=hc, bf2=bf2):
            e.wait_ge(s_bf.h, bf2)
            e.tensor_copy(hc["oc"][:], ps_b2[:, 128:256])
            e.tensor_tensor(hc["S_f"][:], hc["S_f"][:], ps_b2[:, 0:128], ALU.add)
            e.drain()
            e.tensor_copy(hc["S_b"][:], hc["S_f"][:]).then_inc(s_be.h, 1)
        p.on("vector", b_dv2)
        be2 = s_be.bump()
        hc["sb_ready"] = be2
        yield

        def b_pe3(e, hc=hc, be2=be2, ae_n=hc["ae_own"], aa_n=hc["aa_own"]):
            e.wait_ge(s_be.h, be2)
            wait_evacs(e, ae_n, aa_n)
            e.transpose(ps_t[:], hc["oc"][:], aux["ident"][:]).then_inc(s_bf.h, 1)
        p.on("tensor", b_pe3)
        bf3 = s_bf.bump()

        def b_dv3(e, hc=hc, csl=csl, bf3=bf3):
            e.wait_ge(s_bf.h, bf3)
            e.tensor_copy(hc["olinT"][:, csl], ps_t[:]).then_inc(s_be.h, 1)
        p.on("vector", b_dv3)
        s_be.bump()
        yield

    def drive(gens):
        gens = [g for g in gens if g is not None]
        while gens:
            nxt = []
            for g in gens:
                try:
                    next(g)
                    nxt.append(g)
                except StopIteration:
                    pass
            gens = nxt

    for hc in ctxs:
        hc["grp_done"][0] = emit_group(hc, 0)
    drive([gen_A(hc, 0) for hc in ctxs])
    for c in range(NCH):
        if c + 1 < NCH:
            g_next = (c + 1) // GRP
            for hc in ctxs:
                if g_next not in hc["grp_done"]:
                    hc["grp_done"][g_next] = emit_group(hc, g_next)
            drive([gen_A(hc, c + 1) for hc in ctxs] + [gen_B(hc, c) for hc in ctxs])
        else:
            drive([gen_B(hc, c) for hc in ctxs])

    return s_be, s_be.n



def emit_aux_load(p: Prog, dram_ap, shape, dtype, tag, deps=()):
    sb = p.sb(f"auxl_{tag}", shape, dtype)
    s = p.sem("auxl")

    def lf(e, deps=tuple(deps)):
        for (sm, v) in deps:
            e.wait_ge(sm.h, v)
        e.dma_start(out=sb[:], in_=dram_ap).then_inc(s.h, 16)
    p.on("sync", lf)
    return sb, (s, s.bump(16))


def emit_attn_a(p: Prog, d, aux, x_sb, xdep, tag="a"):
    with p.scope():
        qT = p.sb(f"qT_{tag}", [128, 2, S], BF16)
        kT = p.sb(f"kT_{tag}", [128, 1, S], BF16)
        vT = p.sb(f"vT_{tag}", [128, 1, S], BF16)

        with p.scope():
            t_bf = p.sb(f"t_{tag}", [128, NTT, D], BF16)
            s_ln, lnv = emit_ln(p, x_sb, t_bf, [xdep], f"{tag}")
            emit_allgather_tm(p, t_bf, d[f"cc_t_{tag}_in"], d[f"cc_t_{tag}"],
                              [(s_ln, lnv)], f"{tag}t")
            bar0 = p.barrier()

        with p.scope():
            specs = [
                (d[f"wq_sl_{tag}"], 0, 2, qT),
                (d[f"wk_sl_{tag}"], 0, 1, kT),
                (d[f"wv_sl_{tag}"], 0, 1, vT),
            ]
            emit_attn_proj(p, d[f"cc_t_{tag}"], specs, [bar0], f"{tag}")
            bar1 = p.barrier()

        with p.scope():
            v_tm = p.sb(f"vtm_{tag}", [128, NCH, 128], BF16)
            mask4, m4dep = emit_aux_load(p, d["mask4"], [128, 4, 512], BF16, f"m4{tag}",
                                         deps=[bar1])
            with p.scope():
                emit_transpose_fm_tm(p, vT[:, 0, :], v_tm, NCH, aux["ident"],
                                     [bar1], f"{tag}v")
                bar_tp = p.barrier(engines=("tensor", "vector"))
            s_mx = p.sem("mx")
            hdep = [bar1, bar_tp, m4dep]
            for h in range(2):
                with p.scope():
                    outT = p.sb(f"mixT_{tag}_{h}", [128, S], BF16)
                    s_fl, flv = emit_flash_head(p, qT[:, h, :], kT[:, 0, :], v_tm, outT,
                                                aux["ones_bf"], aux["ones1_f"], mask4,
                                                hdep, f"{tag}h{h}", ones_f=aux["ones_f"])

                    def mxo(e, h=h, outT=outT, flv=flv, s_fl=s_fl):
                        e.wait_ge(s_fl.h, flv)
                        for j in range(R):
                            e.dma_start(out=d[f"cc_mix_{tag}_in"][256 * j + 128 * h:256 * j + 128 * (h + 1), :],
                                        in_=outT[:, j * TOK:(j + 1) * TOK]).then_inc(s_mx.h, 16)
                    p.on("scalar", mxo)
                    s_mx.bump(16 * R)
                    bar_h = p.barrier(engines=("tensor", "scalar", "vector"))
                    hdep = [bar_h]

            def agm(e, mxv=s_mx.n):
                from concourse.aot_env import allow_ici_all_to_all
                e.wait_ge(s_mx.h, mxv)
                with allow_ici_all_to_all():
                    e.collective_compute(
                        "AllToAll", ALU.bypass,
                        replica_groups=[list(range(R))],
                        ins=[d[f"cc_mix_{tag}_in"]],
                        outs=[d[f"cc_mix_{tag}"]],
                    ).then_inc(s_mx.h, 1)
            p.on("gpsimd", agm)
            s_mx.bump(1)

        bar2 = p.barrier()

    with p.scope():
        s_res, resv = emit_out_proj(p, d[f"cc_mix_{tag}"], d[f"w_o_{tag}"], None,
                                    x_sb, [bar2], f"{tag}")
    return s_res, resv






def emit_ln_dram(p: Prog, x_sb, trb, trb_lo, deps, tag):
    p.constv(EPS_LN)
    p.constv(-0.5)
    sq = p.sb(f"lnd_sq_{tag}", [128, D], F32)
    ssq = p.sb(f"lnd_ssq_{tag}", [128, 1], F32)
    sx = p.sb(f"lnd_sx_{tag}", [128, 1], F32)
    mean = p.sb(f"lnd_mean_{tag}", [128, 1], F32)
    var = p.sb(f"lnd_var_{tag}", [128, 1], F32)
    rstd = p.sb(f"lnd_rstd_{tag}", [128, 1], F32)
    tf = p.sb(f"lnd_tf_{tag}", [128, D], F32)
    stage = [p.sb(f"lnd_st_{tag}_{b}", [128, D], BF16) for b in range(2)]
    stage_lo = [p.sb(f"lnd_sl_{tag}_{b}", [128, D], BF16) for b in range(2)] if trb_lo is not None else None
    ndma = 2 if trb_lo is not None else 1

    s_sq = p.sem("lnd_sq")
    s_st = p.sem("lnd_st")
    s_rs = p.sem("lnd_rs")
    s_done = p.sem("lnd_done")
    s_tb = p.sem("lnd_tb")
    base_tb = s_tb.n

    trb_r = trb.rearrange("(t p) x -> t p x", p=128)
    trb_lo_r = trb_lo.rearrange("(t p) x -> t p x", p=128) if trb_lo is not None else None

    for i in range(NTT):
        xt = x_sb[:, i, :]
        st_war = s_st.n

        def act1(e, xt=xt, i=i, st_war=st_war):
            if i == 0:
                for (sm, v) in deps:
                    e.wait_ge(sm.h, v)
            else:
                e.wait_ge(s_st.h, st_war)
            e.activation(sq[:], xt, AF.Square, accum_out=ssq[:]).then_inc(s_sq.h, 1)
        p.on("scalar", act1)
        sq_v = s_sq.bump()

        def dve1(e, xt=xt, sq_v=sq_v, i=i):
            if i == 0:
                for (sm, v) in deps:
                    e.wait_ge(sm.h, v)
            e.reduce_sum(sx[:], xt, axis=AX.X)
            e.drain()
            e.wait_ge(s_sq.h, sq_v)
            e.tensor_scalar_mul(mean[:], sx[:], 1.0 / D)
            e.drain()
            e.tensor_tensor(sx[:], mean[:], mean[:], ALU.mult)
            e.tensor_scalar_mul(var[:], ssq[:], 1.0 / D)
            e.drain()
            e.tensor_tensor(var[:], var[:], sx[:], ALU.subtract).then_inc(s_st.h, 1)
        p.on("vector", dve1)
        st_v = s_st.bump()

        def act2(e, st_v=st_v):
            e.wait_ge(s_st.h, st_v)
            e.activation(var[:], var[:], AF.Ln, bias=EPS_LN)
            e.drain()
            e.activation(rstd[:], var[:], AF.Exp, scale=-0.5).then_inc(s_rs.h, 1)
        p.on("scalar", act2)
        rs_v = s_rs.bump()

        def dve2(e, xt=xt, i=i, rs_v=rs_v,
                 tb_war=base_tb + i * 16 * ndma):
            e.wait_ge(s_rs.h, rs_v)
            e.wait_ge(s_tb.h, tb_war)
            b = i % 2
            if trb_lo is not None:
                e.tensor_scalar(tf[:], xt, mean[:], rstd[:], ALU.subtract, ALU.mult)
                e.drain()
                e.tensor_copy(stage[b][:], tf[:])
                e.drain()
                e.tensor_tensor(stage_lo[b][:], tf[:], stage[b][:],
                                ALU.subtract).then_inc(s_done.h, 1)
            else:
                e.tensor_scalar(stage[b][:], xt, mean[:], rstd[:],
                                ALU.subtract, ALU.mult).then_inc(s_done.h, 1)
        p.on("vector", dve2)
        dn_v = s_done.bump()

        def tdma(e, i=i, dn_v=dn_v):
            e.wait_ge(s_done.h, dn_v)
            b = i % 2
            e.dma_start(out=trb_r[i], in_=stage[b][:]).then_inc(s_tb.h, 16)
            if trb_lo is not None:
                e.dma_start(out=trb_lo_r[i], in_=stage_lo[b][:]).then_inc(s_tb.h, 16)
        p.on("scalar", tdma)
        s_tb.bump(16 * ndma)

    return s_tb, s_tb.n


def emit_ffn(p: Prog, d, aux, x_sb, deps, tag, upto="full", dbg=None):
    BIG = 1.0e9
    NM_E = F_EXP // 128

    tT = p.sb(f"ffn_tT_{tag}", [128, NDC, TOK], BF16)
    wrows = p.sb(f"ffn_wr_{tag}", [1, N_EXP, TOK], BF16)
    s_tb = p.sem("ffn_tb")
    s_tt = p.sem("ffn_tt")
    s_rt = p.sem("ffn_rt")

    with p.scope():
        s_tbl, tbv = emit_ln_dram(p, x_sb, d[f"trb_{tag}"], d[f"trb_lo_{tag}"],
                                  deps, f"{tag}")

        def tbfwd(e, tbv=tbv):
            e.wait_ge(s_tbl.h, tbv)
            e.sem_inc(s_tb.h, 1)
        p.on("scalar", tbfwd)
        tbv = s_tb.bump(1)
        p.barrier(engines=("vector", "scalar"))

    def treads(e, tbv=tbv):
        e.wait_ge(s_tb.h, tbv)
        for j in range(NDC):
            e.dma_start_transpose(tT[:, j, :],
                                  d[f"trb_{tag}"][:, j * 128:(j + 1) * 128]).then_inc(s_tt.h, 16)
    p.on("sync", treads)
    ttv = s_tt.bump(16 * NDC)

    with p.scope():
        tT_lo = p.sb(f"ffn_tTlo_{tag}", [128, NDC, TOK], BF16)

        def treads_lo(e, tbv=tbv):
            e.wait_ge(s_tb.h, tbv)
            for j in range(NDC):
                e.dma_start_transpose(tT_lo[:, j, :],
                                      d[f"trb_lo_{tag}"][:, j * 128:(j + 1) * 128]).then_inc(s_tt.h, 16)
        p.on("sync", treads_lo)
        ttv_lo = s_tt.bump(16 * NDC)

        gw_f = p.sb(f"ffn_gwf_{tag}", [128, NDC, 8], F32)
        gw_hi = p.sb(f"ffn_gwh_{tag}", [128, NDC, 8], BF16)
        gw_lo = p.sb(f"ffn_gwl_{tag}", [128, NDC, 8], BF16)

        def gwload(e):
            e.dma_start(out=gw_f[:], in_=d["gwT"].rearrange("(a p) n -> p a n", p=128)).then_inc(s_rt.h, 16)
        p.on("sync", gwload)
        rt_gw = s_rt.bump(16)

        def gwsplit(e, rt_gw=rt_gw):
            e.wait_ge(s_rt.h, rt_gw)
            e.tensor_copy(gw_hi[:], gw_f[:])
            e.drain()
            e.tensor_tensor(gw_lo[:], gw_f[:], gw_hi[:], ALU.subtract).then_inc(s_rt.h, 1)
        p.on("vector", gwsplit)
        rt_sp = s_rt.bump()

        ps_lg = p.ps(f"ffn_pslg_{tag}", [128, 8])
        ps_wr = p.ps(f"ffn_pswr_{tag}", [16, 128], BF16)
        lg = p.sb(f"ffn_lg_{tag}", [128, 8], F32)
        elg = p.sb(f"ffn_elg_{tag}", [128, 8], F32)
        ssum = p.sb(f"ffn_ssum_{tag}", [128, 1], F32)
        sc = p.sb(f"ffn_sc_{tag}", [128, 8], F32)
        gs = p.sb(f"ffn_gs_{tag}", [128, 4], F32)
        gx = p.sb(f"ffn_gx_{tag}", [128, 4], F32)
        mcol = p.sb(f"ffn_mcol_{tag}", [128, 1], F32)
        sm8 = p.sb(f"ffn_sm8_{tag}", [128, 8], F32)
        msk = p.sb(f"ffn_msk_{tag}", [128, 8], F32)
        sel = p.sb(f"ffn_sel_{tag}", [128, 8], F32)
        wm = p.sb(f"ffn_wm_{tag}", [128, 8], F32)
        wm_pad = p.sb(f"ffn_wmpad_{tag}", [128, 128], BF16)
        s_lg = p.sem("ffn_lg")
        s_wr = p.sem("ffn_wrow")

        for tt in range(NTT):
            def mml(e, tt=tt, first=(tt == 0), rt_sp=rt_sp, ttv=ttv, ttv_lo=ttv_lo,
                    lg_done=s_lg.n):
                if first:
                    e.wait_ge(s_rt.h, rt_sp)
                    e.wait_ge(s_tt.h, ttv_lo)
                else:
                    e.wait_ge(s_lg.h, lg_done)
                ins = None
                for j in range(NDC):
                    sl = tT[:, j, tt * 128:(tt + 1) * 128]
                    sl_lo = tT_lo[:, j, tt * 128:(tt + 1) * 128]
                    ins = e.matmul(ps_lg[:], sl, gw_hi[:, j, :], start=(j == 0), stop=False)
                    ins = e.matmul(ps_lg[:], sl, gw_lo[:, j, :], start=False, stop=False)
                    ins = e.matmul(ps_lg[:], sl_lo, gw_hi[:, j, :], start=False,
                                   stop=(j == NDC - 1))
                ins.then_inc(s_lg.h, 1)
            p.on("tensor", mml)
            lg_v = s_lg.bump()

            def route(e, lg_v=lg_v):
                e.wait_ge(s_lg.h, lg_v)
                e.tensor_copy(lg[:], ps_lg[:])
                e.drain()
                e.engine_nop().then_inc(s_lg.h, 1)
            p.on("vector", route)
            lg_cp = s_lg.bump()

            def act_exp(e, lg_cp=lg_cp):
                e.wait_ge(s_lg.h, lg_cp)
                e.activation(elg[:], lg[:], AF.Exp, accum_out=ssum[:]).then_inc(s_rt.h, 1)
            p.on("scalar", act_exp)
            rt_e = s_rt.bump()

            def route2(e, rt_e=rt_e):
                e.wait_ge(s_rt.h, rt_e)
                e.reciprocal(ssum[:], ssum[:])
                e.drain()
                e.tensor_scalar_mul(sc[:], elg[:], ssum[:])
                e.drain()
                e.tensor_tensor(gs[:], sc[:, 0:8:2], sc[:, 1:8:2], ALU.max)
                e.drain()
                e.reduce_max(mcol[:], gs[:], axis=AX.X)
                e.drain()
                e.tensor_scalar(gx[:], gs[:], mcol[:], None, ALU.is_ge)
                e.drain()
                e.scalar_tensor_tensor(gx[:], gx[:], -BIG, gs[:], ALU.mult, ALU.add)
                e.drain()
                e.reduce_max(mcol[:], gx[:], axis=AX.X)
                e.drain()
                e.tensor_scalar(gx[:], gs[:], mcol[:], None, ALU.is_ge)
                e.drain()
                e.tensor_copy(sm8[:, 0:8:2], gx[:])
                e.drain()
                e.tensor_copy(sm8[:, 1:8:2], gx[:])
                e.drain()
                e.tensor_tensor(msk[:], sc[:], sm8[:], ALU.mult)
                e.drain()
                e.reduce_max(mcol[:], msk[:], axis=AX.X)
                e.drain()
                e.tensor_scalar(sel[:], msk[:], mcol[:], None, ALU.is_ge)
                e.drain()
                e.scalar_tensor_tensor(sel[:], sel[:], -BIG, msk[:], ALU.mult, ALU.add)
                e.drain()
                e.reduce_max(mcol[:], sel[:], axis=AX.X)
                e.drain()
                e.tensor_scalar(sel[:], msk[:], mcol[:], None, ALU.is_ge)
                e.drain()
                e.tensor_tensor(wm[:], sel[:], msk[:], ALU.mult)
                e.drain()
                e.memset(wm_pad[:], 0.0)
                e.drain()
                e.tensor_copy(wm_pad[:, 0:128:16], wm[:]).then_inc(s_rt.h, 1)
            p.on("vector", route2)
            rt_w = s_rt.bump()

            if upto == "logits":
                continue
            for n in range(N_EXP):
                def wrt(e, n=n, rt_w=rt_w, wr_prev=s_wr.n):
                    e.wait_ge(s_rt.h, rt_w)
                    if wr_prev > 0:
                        e.wait_ge(s_wr.h, wr_prev)
                    e.transpose(ps_wr[:], wm_pad[:, 16 * n:16 * (n + 1)],
                                aux["ident"][:]).then_inc(s_rt.h, 1)
                p.on("tensor", wrt)
                rt_t = s_rt.bump()

                def wre(e, n=n, tt=tt, rt_t=rt_t):
                    e.wait_ge(s_rt.h, rt_t)
                    e.tensor_copy(wrows[0:1, n, tt * 128:(tt + 1) * 128],
                                  ps_wr[0:1, :]).then_inc(s_wr.h, 1)
                p.on("vector", wre)
                s_wr.bump()
            if dbg is not None and "wm_to_x" in dbg:
                def wmdump(e, tt=tt, rt_w=rt_w):
                    e.wait_ge(s_rt.h, rt_w)
                    e.tensor_copy(x_sb[:, tt, 0:8], wm[:])
                    e.drain()
                    e.tensor_copy(x_sb[:, tt, 8:16], sc[:])
                    e.drain()
                    e.tensor_copy(x_sb[:, tt, 16:20], gs[:])
                    e.drain()
                    e.tensor_copy(x_sb[:, tt, 20:24], gx[:])
                    e.drain()
                    e.tensor_copy(x_sb[:, tt, 24:32], sm8[:]).then_inc(s_rt.h, 1)
                p.on("vector", wmdump)
                s_rt.bump(1)
        wr_all = s_wr.n
        if upto in ("router", "logits"):
            return s_rt, s_rt.n
        bar_b = p.barrier(engines=("tensor", "vector", "scalar"))

    with p.scope():
        NRING = 2
        wg_ring = [p.sb(f"ffn_wg_{tag}_{i}", [128, NDC, 128], BF16) for i in range(NRING)]
        wu_ring = [p.sb(f"ffn_wu_{tag}_{i}", [128, NDC, 128], BF16) for i in range(NRING)]
        wd_sb = p.sb(f"ffn_wd_{tag}", [128, NM_E, D], BF16)
        act_sb = p.sb(f"ffn_act_{tag}", [128, NM_E, TOK], BF16)
        bc_bf = p.sb(f"ffn_bc_{tag}", [128, TOK], BF16)
        sg_bf = [p.sb(f"ffn_sg_{tag}_{i}", [128, TOK], BF16) for i in range(2)]
        a1 = p.sb(f"ffn_a1_{tag}", [128, TOK], BF16)
        ps_g = [p.ps(f"ffn_psg_{tag}_{i}", [128, TOK]) for i in range(2)]
        ps_u = [p.ps(f"ffn_psu_{tag}_{i}", [128, TOK]) for i in range(2)]

        s_wl = p.sem("ffn_wl")
        s_mf = p.sem("ffn_mf")
        s_ae = p.sem("ffn_ae")
        s_av = p.sem("ffn_av")
        s_bc = p.sem("ffn_bc")
        s_dn = p.sem("ffn_dn")
        s_xa = p.sem("ffn_xa")
        s_dl = p.sem("ffn_dl")
        base_wl, base_mf, base_dn, base_dl = s_wl.n, s_mf.n, s_dn.n, s_dl.n

        blocks = [("e", n) for n in range(N_EXP)] + [("s", h) for h in range(2)]
        SKIP_DOWN = False

        mseq = 0
        for bi, (kind, n) in enumerate(blocks):
            for m in range(NM_E):
                for (ring, wsel) in ((wg_ring, "g"), (wu_ring, "u")):
                    if kind == "e":
                        wtns = d["gate_wT"] if wsel == "g" else d["up_wT"]
                        src = wtns[n, :, m * 128:(m + 1) * 128]
                    else:
                        wtns = d["sh_gate_wT"] if wsel == "g" else d["sh_up_wT"]
                        c0 = n * F_EXP + m * 128
                        src = wtns[:, c0:c0 + 128]
                    src = src.rearrange("(a p) c -> p a c", p=128)

                    def wl(e, src=src, ring=ring, mseq=mseq, bar_b=bar_b,
                           need=base_mf + max(0, mseq - 1)):
                        e.wait_ge(bar_b[0].h, bar_b[1])
                        if need > base_mf:
                            e.wait_ge(s_mf.h, need)
                        e.dma_start(out=ring[mseq % NRING][:], in_=src).then_inc(s_wl.h, 16)
                    p.on("gpsimd", wl)
                    s_wl.bump(16)
                mseq += 1
            for k in range(NM_E):
                if kind == "e":
                    dsrc = d["down_wT"][n, k * 128:(k + 1) * 128, :]
                else:
                    r0 = n * F_EXP + k * 128
                    dsrc = d["sh_down_wT"][r0:r0 + 128, :]

                def dl(e, dsrc=dsrc, k=k, bi=bi, bar_b=bar_b,
                       dn_need=base_dn + bi * NTT):
                    e.wait_ge(bar_b[0].h, bar_b[1])
                    if bi > 0:
                        e.wait_ge(s_dn.h, dn_need)
                    e.dma_start(out=wd_sb[:, k, :], in_=dsrc).then_inc(s_dl.h, 16)
                p.on("gpsimd", dl)
                s_dl.bump(16)

        mseq = 0
        for bi, (kind, n) in enumerate(blocks):
            routed = (kind == "e")
            if routed:
                def bcast(e, n=n, wr_all=wr_all, bar_b=bar_b, bi=bi,
                          av_need=s_av.n, xa_need=s_xa.n):
                    e.wait_ge(s_wr.h, wr_all)
                    e.wait_ge(bar_b[0].h, bar_b[1])
                    if bi > 0:
                        e.wait_ge(s_av.h, av_need)
                        e.wait_ge(s_xa.h, xa_need)
                    for nck in range(TOK // 512):
                        e.matmul(ps_g[0][:, nck * 512:(nck + 1) * 512], aux["ones1_bf"][:],
                                 wrows[0:1, n, nck * 512:(nck + 1) * 512],
                                 start=True, stop=True).then_inc(s_bc.h, 1)
                p.on("tensor", bcast)
                bc_f = s_bc.bump(2)

                def bccp(e, bc_f=bc_f):
                    e.wait_ge(s_bc.h, bc_f)
                    e.activation(bc_bf[:], ps_g[0][:], AF.Copy).then_inc(s_bc.h, 1)
                p.on("scalar", bccp)
                bc_c = s_bc.bump()
            else:
                bc_c = s_bc.n

            for m in range(NM_E):
                mb = mseq % 2
                wg = wg_ring[mseq % NRING]
                wu = wu_ring[mseq % NRING]

                def mmg(e, mb=mb, wg=wg, wu=wu, wl_need=base_wl + 32 * (mseq + 1),
                        bc_c=bc_c, m=m, routed=routed, bi=bi,
                        av_need=max(0, s_av.n - 1), xa_need=s_xa.n if m <= 1 and bi > 0 else 0):
                    e.wait_ge(s_wl.h, wl_need)
                    if m == 0 and routed:
                        e.wait_ge(s_bc.h, bc_c)
                    if xa_need > 0:
                        e.wait_ge(s_xa.h, xa_need)
                    e.wait_ge(s_av.h, av_need)
                    ins = None
                    for j in range(NDC):
                        for nck in range(TOK // 512):
                            ins = e.matmul(ps_g[mb][:, nck * 512:(nck + 1) * 512],
                                           wg[:, j, :], tT[:, j, nck * 512:(nck + 1) * 512],
                                           start=(j == 0), stop=(j == NDC - 1))
                    for j in range(NDC):
                        for nck in range(TOK // 512):
                            ins = e.matmul(ps_u[mb][:, nck * 512:(nck + 1) * 512],
                                           wu[:, j, :], tT[:, j, nck * 512:(nck + 1) * 512],
                                           start=(j == 0), stop=(j == NDC - 1))
                    ins.then_inc(s_mf.h, 1)
                p.on("tensor", mmg)
                mf_v = s_mf.bump()

                def silu(e, mb=mb, mf_v=mf_v):
                    e.wait_ge(s_mf.h, mf_v)
                    e.activation(sg_bf[mb][:], ps_g[mb][:], AF.Silu).then_inc(s_ae.h, 1)
                p.on("scalar", silu)
                ae_v = s_ae.bump()

                def acte(e, m=m, mb=mb, ae_v=ae_v, routed=routed,
                         dn_need=base_dn + bi * NTT):
                    e.wait_ge(s_ae.h, ae_v)
                    if dn_need > base_dn:
                        e.wait_ge(s_dn.h, dn_need)
                    if routed:
                        e.tensor_tensor(a1[:], sg_bf[mb][:], ps_u[mb][:], ALU.mult)
                        e.drain()
                        e.tensor_tensor(act_sb[:, m, :], a1[:], bc_bf[:],
                                        ALU.mult).then_inc(s_av.h, 1)
                    else:
                        e.tensor_tensor(act_sb[:, m, :], sg_bf[mb][:], ps_u[mb][:],
                                        ALU.mult).then_inc(s_av.h, 1)
                p.on("vector", acte)
                s_av.bump()
                mseq += 1

            for tt in range(NTT if not SKIP_DOWN else 0):
                def mmd(e, tt=tt, av_all=s_av.n, dl_need=base_dl + 16 * NM_E * (bi + 1),
                        xa_need=s_xa.n):
                    e.wait_ge(s_av.h, av_all)
                    e.wait_ge(s_dl.h, dl_need)
                    if xa_need > 0:
                        e.wait_ge(s_xa.h, xa_need)
                    ins = None
                    if True:
                        for k in range(NM_E):
                            for nch in range(4):
                                bank = ps_g[nch // 2]
                                tgt = bank[:, (nch % 2) * 512:(nch % 2 + 1) * 512]
                                ins = e.matmul(tgt, act_sb[:, k, tt * 128:(tt + 1) * 128],
                                               wd_sb[:, k, nch * 512:(nch + 1) * 512],
                                               start=(k == 0), stop=(k == NM_E - 1))
                    ins.then_inc(s_dn.h, 1)
                p.on("tensor", mmd)
                dn_v = s_dn.bump()

                def xadd(e, tt=tt, dn_v=dn_v):
                    e.wait_ge(s_dn.h, dn_v)
                    e.tensor_tensor(x_sb[:, tt, 0:TOK], x_sb[:, tt, 0:TOK],
                                    ps_g[0][:], ALU.add)
                    e.drain()
                    e.tensor_tensor(x_sb[:, tt, TOK:D], x_sb[:, tt, TOK:D],
                                    ps_g[1][:], ALU.add).then_inc(s_xa.h, 1)
                p.on("vector", xadd)
                s_xa.bump()

    return s_xa, s_xa.n



def emit_delta_head(p: Prog, qT, kT, vT, olinT, aux, deps, tag):
    p.constv(ATT_SCALE)
    base_be0 = p.sem("dl_be").n
    GRP = 8
    NG = NCH // GRP
    GW = GRP * 128

    qh_g = [p.sb(f"dl_qh_{tag}_{b}", [128, GW], BF16) for b in range(2)]
    kt_g = [p.sb(f"dl_kt_{tag}_{b}", [128, GW], BF16) for b in range(2)]
    kb_g = [p.sb(f"dl_kb_{tag}_{b}", [128, GW], BF16) for b in range(2)]
    vb_g = [p.sb(f"dl_vb_{tag}_{b}", [128, GW], BF16) for b in range(2)]
    eq = p.sb(f"dl_eq_{tag}", [128, GW], BF16)
    tmpg = p.sb(f"dl_tmpg_{tag}", [128, GW], F32)
    den = p.sb(f"dl_den_{tag}", [1, GW], F32)
    akg = p.sb(f"dl_ak_{tag}", [128, GW], BF16)
    e2g = p.sb(f"dl_e2_{tag}", [128, GW], F32)
    mng = p.sb(f"dl_mn_{tag}", [128, GW], F32)
    gcg = p.sb(f"dl_gc_{tag}", [128, GW], BF16)

    vb_tm = p.sb(f"dl_vbtm_{tag}", [128, 128], BF16)
    kb_tm = p.sb(f"dl_kbtm_{tag}", [128, 128], BF16)
    Lm = p.sb(f"dl_L_{tag}", [128, 128], BF16)
    LTm = p.sb(f"dl_LT_{tag}", [128, 128], BF16)
    Pw = [p.sb(f"dl_pw_{tag}_{i}", [128, 128], BF16) for i in range(5)]
    PwT = [p.sb(f"dl_pwt_{tag}_{i}", [128, 128], BF16) for i in range(2)]
    Y = p.sb(f"dl_Y_{tag}", [128, 128], BF16)
    TT = p.sb(f"dl_TT_{tag}", [128, 128], BF16)
    u_sb = [p.sb(f"dl_u_{tag}_{b}", [128, 128], BF16) for b in range(2)]
    wT_sb = [p.sb(f"dl_w_{tag}_{b}", [128, 128], BF16) for b in range(2)]
    at_sb = [p.sb(f"dl_at_{tag}_{b}", [128, 128], BF16) for b in range(2)]
    ktm_sb = [p.sb(f"dl_ktm_{tag}_{b}", [128, 128], BF16) for b in range(2)]
    u2 = p.sb(f"dl_u2_{tag}", [128, 128], BF16)
    oc = p.sb(f"dl_oc_{tag}", [128, 128], BF16)
    S_f = p.sb(f"dl_Sf_{tag}", [128, 128], F32)
    S_b = p.sb(f"dl_Sb_{tag}", [128, 128], BF16)

    ps_a = [p.ps(f"dl_psa_{tag}_{i}", [128, 128]) for i in range(2)]
    ps_r = p.ps(f"dl_psr_{tag}", [128, 512])
    ps_t = p.ps(f"dl_pst_{tag}", [128, 128], BF16)
    ps_b = [p.ps(f"dl_psb_{tag}_{i}", [128, 128]) for i in range(2)]

    s_tf = p.sem("dl_tf")
    s_af = p.sem("dl_af")
    s_ae = p.sem("dl_ae")
    s_bf = p.sem("dl_bf")
    s_be = p.sem("dl_be")

    def sinit(e):
        for (sm, v) in deps:
            e.wait_ge(sm.h, v)
        e.memset(S_f[:], 0.0)
        e.drain()
        e.memset(S_b[:], 0.0)
        e.drain()
        e.engine_nop().then_inc(s_be.h, 1)
    p.on("vector", sinit)
    s_be.bump()

    def emit_group(g):
        b = g % 2
        gsl = slice(g * GW, (g + 1) * GW)
        war = base_be0 + 1 + max(0, 24 * (g - 1))

        def t_act1(e, g=g, gsl=gsl, war=war, prev_grp=s_tf.n):
            if g == 0:
                for (sm, v) in deps:
                    e.wait_ge(sm.h, v)
            e.wait_ge(s_tf.h, prev_grp)
            e.wait_ge(s_be.h, war)
            e.activation(eq[:], qT[:, gsl], AF.Exp).then_inc(s_tf.h, 1)
        p.on("scalar", t_act1)
        v1 = s_tf.bump()

        def softmax_pipe(src_fm, dst, scale_after, v_ready):
            vr = v_ready
            for ncl in range(2):
                def pe_row(e, src_fm=src_fm, ncl=ncl, vr=vr, ae_w=s_tf.n):
                    e.wait_ge(s_tf.h, vr)
                    e.matmul(ps_r[0:1, 0:512], aux["ones_bf"][:],
                             eq[:, ncl * 512:(ncl + 1) * 512],
                             start=True, stop=True).then_inc(s_tf.h, 1)
                p.on("tensor", pe_row)
                v_r = s_tf.bump()

                def dv_rec(e, ncl=ncl, v_r=v_r):
                    e.wait_ge(s_tf.h, v_r)
                    e.tensor_copy(den[0:1, ncl * 512:(ncl + 1) * 512], ps_r[0:1, 0:512])
                    e.drain()
                    e.reciprocal(den[0:1, ncl * 512:(ncl + 1) * 512],
                                 den[0:1, ncl * 512:(ncl + 1) * 512])
                    e.drain()
                    e.engine_nop().then_inc(s_tf.h, 1)
                p.on("vector", dv_rec)
                v_c = s_tf.bump()

                def pe_bc(e, ncl=ncl, v_c=v_c):
                    e.wait_ge(s_tf.h, v_c)
                    e.matmul(ps_r[:, 0:512], aux["ones1_f"][:],
                             den[0:1, ncl * 512:(ncl + 1) * 512],
                             start=True, stop=True).then_inc(s_tf.h, 1)
                p.on("tensor", pe_bc)
                v_b = s_tf.bump()

                def dv_fin(e, dst=dst, ncl=ncl, v_b=v_b, scale_after=scale_after):
                    e.wait_ge(s_tf.h, v_b)
                    csl = slice(ncl * 512, (ncl + 1) * 512)
                    e.tensor_tensor(tmpg[:, csl], eq[:, csl], ps_r[:, 0:512], ALU.mult)
                    e.drain()
                    if scale_after is None:
                        e.tensor_scalar(dst[:, csl], tmpg[:, csl], 1e-6, 1.0 - 1e-6,
                                        ALU.max, ALU.min).then_inc(s_tf.h, 1)
                    else:
                        e.tensor_scalar(tmpg[:, csl], tmpg[:, csl], 1e-6, 1.0 - 1e-6,
                                        ALU.max, ALU.min)
                        e.drain()
                        e.tensor_scalar_mul(dst[:, csl], tmpg[:, csl],
                                            float(scale_after)).then_inc(s_tf.h, 1)
                p.on("vector", dv_fin)
                vr = s_tf.bump()
            return vr

        vq = softmax_pipe(None, qh_g[b], ATT_SCALE, v1)

        def t_act2(e, gsl=gsl, vq=vq):
            e.wait_ge(s_tf.h, vq)
            e.activation(eq[:], kT[:, gsl], AF.Exp).then_inc(s_tf.h, 1)
        p.on("scalar", t_act2)
        v2 = s_tf.bump()
        vk = softmax_pipe(None, kt_g[b], None, v2)

        def t_gate_a(e, gsl=gsl, vk=vk):
            e.wait_ge(s_tf.h, vk)
            e.activation(akg[:], kT[:, gsl], AF.Abs)
            e.drain()
            e.activation(e2g[:], akg[:], AF.Exp, scale=-1.0)
            e.drain()
            e.activation(e2g[:], e2g[:], AF.Ln, bias=1.0).then_inc(s_tf.h, 1)
        p.on("scalar", t_gate_a)
        v4 = s_tf.bump()

        def t_gate_d2(e, gsl=gsl, b=b, v4=v4):
            e.wait_ge(s_tf.h, v4)
            e.tensor_scalar(mng[:], kT[:, gsl], 0.0, None, ALU.min)
            e.drain()
            e.tensor_tensor(mng[:], mng[:], e2g[:], ALU.subtract)
            e.drain()
            e.tensor_scalar(mng[:], mng[:], 1.0 / 16.0, -16.0, ALU.mult, ALU.max)
            e.drain()
            e.tensor_scalar(gcg[:], mng[:], 16.0, None, ALU.min)
            e.drain()
            e.tensor_tensor(kb_g[b][:], kt_g[b][:], gcg[:], ALU.mult)
            e.drain()
            e.tensor_tensor(vb_g[b][:], vT[:, gsl], gcg[:], ALU.mult).then_inc(s_tf.h, 1)
        p.on("vector", t_gate_d2)
        return s_tf.bump()

    grp_done = {}
    def emit_A(c):
        g = c // GRP
        b2 = c % 2
        bg = g % 2
        csl = slice((c % GRP) * 128, (c % GRP + 1) * 128)
        qh, kt_, kb, vb = qh_g[bg], kt_g[bg], kb_g[bg], vb_g[bg]

        be_war = s_be.n

        def a_pe(e, c=c, csl=csl, qh=qh, kt_=kt_, kb=kb, vb=vb, b2=b2,
                 gd=grp_done[g], be_war=s_be.n, ae_prev=s_ae.n):
            e.wait_ge(s_tf.h, gd)
            if c >= 2:
                e.wait_ge(s_be.h, be_war)
            if c >= 1:
                e.wait_ge(s_ae.h, ae_prev)
            e.transpose(ps_t[:], vb[:, csl], aux["ident"][:]).then_inc(s_af.h, 1)
            e.wait_ge(s_ae.h, ae_prev + 1)
            e.transpose(ps_t[:], kb[:, csl], aux["ident"][:]).then_inc(s_af.h, 1)
            e.wait_ge(s_ae.h, ae_prev + 2)
            e.transpose(ps_t[:], kt_[:, csl], aux["ident"][:]).then_inc(s_af.h, 1)
            e.matmul(ps_a[0][:], kb[:, csl], kt_[:, csl], start=True, stop=True).then_inc(s_af.h, 1)
            e.matmul(ps_a[1][:], kt_[:, csl], kb[:, csl], start=True, stop=True).then_inc(s_af.h, 1)
            e.wait_ge(s_ae.h, ae_prev + 5)
            e.matmul(ps_a[0][:], kt_[:, csl], qh[:, csl], start=True, stop=True).then_inc(s_af.h, 1)
        p.on("tensor", a_pe)
        ae_prev = s_ae.n
        s_af.bump(6)

        def a_dv(e, c=c, b2=b2, af0=ae_prev * 0 + s_af.n - 6):
            e.wait_ge(s_af.h, af0 + 1)
            e.tensor_copy(vb_tm[:], ps_t[:]).then_inc(s_ae.h, 1)
            e.wait_ge(s_af.h, af0 + 2)
            e.tensor_copy(kb_tm[:], ps_t[:]).then_inc(s_ae.h, 1)
            e.wait_ge(s_af.h, af0 + 3)
            e.tensor_copy(ktm_sb[b2][:], ps_t[:]).then_inc(s_ae.h, 1)
            e.wait_ge(s_af.h, af0 + 4)
            e.tensor_tensor(Lm[:], ps_a[0][:], aux["m_sl"][:], ALU.mult).then_inc(s_ae.h, 1)
            e.wait_ge(s_af.h, af0 + 5)
            e.tensor_tensor(LTm[:], ps_a[1][:], aux["m_su"][:], ALU.mult).then_inc(s_ae.h, 1)
            e.wait_ge(s_af.h, af0 + 6)
            e.tensor_tensor(at_sb[b2][:], ps_a[0][:], aux["m_iu"][:], ALU.mult).then_inc(s_ae.h, 1)
        p.on("vector", a_dv)
        s_ae.bump(6)

        for lev in range(6):
            curL = Lm if lev == 0 else Pw[lev - 1]
            curLT = LTm if lev == 0 else PwT[(lev - 1) % 2]
            last = (lev == 5)

            def p_pe(e, curL=curL, curLT=curLT, last=last, ae_n=hc["ae_own"]):
                e.wait_ge(s_ae.h, ae_n)
                if not last:
                    e.matmul(ps_a[0][:], curLT[:], curL[:], start=True, stop=True)
                e.matmul(ps_a[1][:], curL[:], curLT[:], start=True,
                         stop=True).then_inc(s_af.h, 1)
            p.on("tensor", p_pe)
            af_v = s_af.bump()

            def p_dv(e, lev=lev, last=last, af_v=af_v):
                e.wait_ge(s_af.h, af_v)
                if not last:
                    e.tensor_copy(Pw[lev][:], ps_a[0][:])
                    e.drain()
                e.tensor_copy(PwT[lev % 2][:], ps_a[1][:]).then_inc(s_ae.h, 1)
            p.on("vector", p_dv)
            s_ae.bump()

        def y_init(e, ae_n=s_ae.n):
            e.wait_ge(s_ae.h, ae_n)
            e.tensor_tensor(Y[:], PwT[5 % 2][:], aux["eye_bf"][:], ALU.add).then_inc(s_ae.h, 1)
        p.on("vector", y_init)
        s_ae.bump()

        prod_lhs = [Pw[4], Pw[3], Pw[2], Pw[1], Pw[0], Lm]
        for pi, lhs in enumerate(prod_lhs):
            lastp = (pi == 5)

            def pr_pe(e, lhs=lhs, ae_n=s_ae.n):
                e.wait_ge(s_ae.h, ae_n)
                e.matmul(ps_a[0][:], lhs[:], Y[:], start=True, stop=True).then_inc(s_af.h, 1)
            p.on("tensor", pr_pe)
            af_v = s_af.bump()

            def pr_dv(e, lastp=lastp, af_v=af_v):
                e.wait_ge(s_af.h, af_v)
                if lastp:
                    e.tensor_tensor(TT[:], Y[:], ps_a[0][:], ALU.subtract).then_inc(s_ae.h, 1)
                else:
                    e.tensor_tensor(Y[:], Y[:], ps_a[0][:], ALU.add).then_inc(s_ae.h, 1)
            p.on("vector", pr_dv)
            s_ae.bump()

        def uw_pe(e, ae_n=s_ae.n):
            e.wait_ge(s_ae.h, ae_n)
            e.matmul(ps_a[0][:], TT[:], vb_tm[:], start=True, stop=True)
            e.matmul(ps_a[1][:], kb_tm[:], TT[:], start=True, stop=True).then_inc(s_af.h, 1)
        p.on("tensor", uw_pe)
        af_v = s_af.bump()

        def uw_dv(e, b2=b2, af_v=af_v):
            e.wait_ge(s_af.h, af_v)
            e.tensor_copy(u_sb[b2][:], ps_a[0][:])
            e.drain()
            e.tensor_copy(wT_sb[b2][:], ps_a[1][:]).then_inc(s_ae.h, 1)
        p.on("vector", uw_dv)
        s_ae.bump()
        return s_ae.n

    def emit_B(c, a_done):
        b2 = c % 2
        csl = slice(c * 128, (c + 1) * 128)
        lsl = slice((c % GRP) * 128, (c % GRP + 1) * 128)
        g = c // GRP
        bg = g % 2
        qh = qh_g[bg]

        def b_pe1(e, b2=b2, a_done=a_done, be_n=s_be.n):
            e.wait_ge(s_ae.h, a_done)
            e.wait_ge(s_be.h, be_n)
            e.matmul(ps_b[0][:], wT_sb[b2][:], S_b[:], start=True, stop=True).then_inc(s_bf.h, 1)
        p.on("tensor", b_pe1)
        bf1 = s_bf.bump()

        def b_dv1(e, b2=b2, bf1=bf1):
            e.wait_ge(s_bf.h, bf1)
            e.tensor_tensor(u2[:], u_sb[b2][:], ps_b[0][:], ALU.subtract).then_inc(s_be.h, 1)
        p.on("vector", b_dv1)
        be1 = s_be.bump()

        def b_pe2(e, b2=b2, lsl=lsl, qh=qh, be1=be1):
            e.wait_ge(s_be.h, be1)
            e.matmul(ps_b[1][:], qh[:, lsl], S_b[:], start=True, stop=False)
            e.matmul(ps_b[1][:], at_sb[b2][:], u2[:], start=False, stop=True)
            e.matmul(ps_b[0][:], ktm_sb[b2][:], u2[:], start=True,
                     stop=True).then_inc(s_bf.h, 1)
        p.on("tensor", b_pe2)
        bf2 = s_bf.bump()

        def b_dv2(e, bf2=bf2):
            e.wait_ge(s_bf.h, bf2)
            e.tensor_copy(oc[:], ps_b[1][:])
            e.drain()
            e.tensor_tensor(S_f[:], S_f[:], ps_b[0][:], ALU.add)
            e.drain()
            e.tensor_copy(S_b[:], S_f[:]).then_inc(s_be.h, 1)
        p.on("vector", b_dv2)
        be2 = s_be.bump()

        def b_pe3(e, be2=be2, ae_n=s_ae.n):
            e.wait_ge(s_be.h, be2)
            e.transpose(ps_t[:], oc[:], aux["ident"][:]).then_inc(s_bf.h, 1)
        p.on("tensor", b_pe3)
        bf3 = s_bf.bump()

        def b_dv3(e, csl=csl, bf3=bf3):
            e.wait_ge(s_bf.h, bf3)
            e.tensor_copy(olinT[:, csl], ps_t[:]).then_inc(s_be.h, 1)
        p.on("vector", b_dv3)
        s_be.bump()

    a_done_of = {}
    grp_done[0] = emit_group(0)
    a_done_of[0] = emit_A(0)
    for c in range(NCH):
        g_next = (c + 1) // GRP
        if c + 1 < NCH:
            if g_next not in grp_done:
                grp_done[g_next] = emit_group(g_next)
            a_done_of[c + 1] = emit_A(c + 1)
        emit_B(c, a_done_of[c])

    return s_be, s_be.n



def emit_attn_b_core(p: Prog, d, aux, deps, tag="b"):
    with p.scope():
        qT = p.sb(f"qT_{tag}", [128, 2, S], BF16)
        kT = p.sb(f"kT_{tag}", [128, 2, S], BF16)
        vT = p.sb(f"vT_{tag}", [128, 2, S], BF16)
        with p.scope():
            specs = [
                (d[f"wq_sl_{tag}"], 0, 2, qT),
                (d[f"wk_sl_{tag}"], 0, 2, kT),
                (d[f"wv_sl_{tag}"], 0, 2, vT),
            ]
            emit_attn_proj(p, d[f"cc_t_{tag}"], specs, deps, f"{tag}")
            bar1 = p.barrier()

        s_mx = p.sem("mx")
        mask4, m4dep = emit_aux_load(p, d["mask4"], [128, 4, 512], BF16, f"m4{tag}",
                                     deps=deps)
        olins = [p.sb(f"olinT_{tag}_{h}", [128, S], BF16) for h in range(2)]
        with p.scope():
            heads = [dict(qT=qT[:, h, :], kT=kT[:, h, :], vT=vT[:, h, :],
                          olinT=olins[h]) for h in range(2)]
            emit_delta_pair(p, heads, aux, [bar1], f"{tag}")
            bar_d = p.barrier(engines=("tensor", "vector", "scalar"))

        hdep = [bar_d, m4dep]
        for h in range(2):
            with p.scope():
                v_tm = p.sb(f"vtm_{tag}_{h}", [128, NCH, 128], BF16)
                with p.scope():
                    emit_transpose_fm_tm(p, vT[:, h, :], v_tm, NCH,
                                         aux["ident"], hdep, f"{tag}v{h}")
                    bar_tp = p.barrier(engines=("tensor", "vector"))
                outT = p.sb(f"mixT_{tag}_{h}", [128, S], BF16)
                s_fl, flv = emit_flash_head(p, qT[:, h, :], kT[:, h, :], v_tm, outT,
                                            aux["ones_bf"], aux["ones1_f"], mask4,
                                            hdep + [bar_tp], f"{tag}h{h}",
                                            olinT=olins[h], wbase=0.5, wlin=0.5,
                                            ones_f=aux["ones_f"])

                def mxo(e, h=h, outT=outT, flv=flv, s_fl=s_fl):
                    e.wait_ge(s_fl.h, flv)
                    for j in range(R):
                        e.dma_start(out=d[f"cc_mix_{tag}_in"][256 * j + 128 * h:256 * j + 128 * (h + 1), :],
                                    in_=outT[:, j * TOK:(j + 1) * TOK]).then_inc(s_mx.h, 16)
                p.on("scalar", mxo)
                s_mx.bump(16 * R)
                bar_h = p.barrier(engines=("tensor", "scalar", "vector"))
                hdep = [bar_h, m4dep]

        def agm(e, mxv=s_mx.n):
            from concourse.aot_env import allow_ici_all_to_all
            e.wait_ge(s_mx.h, mxv)
            with allow_ici_all_to_all():
                e.collective_compute(
                    "AllToAll", ALU.bypass,
                    replica_groups=[list(range(R))],
                    ins=[d[f"cc_mix_{tag}_in"]],
                    outs=[d[f"cc_mix_{tag}"]],
                ).then_inc(s_mx.h, 1)
        p.on("gpsimd", agm)
        mxv2 = s_mx.bump(1)
    return s_mx, mxv2



AUX_SPECS = [("ident", [128, 128], "bf16"), ("ones_bf", [128, 1], "bf16"),
             ("ones1_f", [1, 128], "f32"), ("ones1_bf", [1, 128], "bf16"),
             ("ones_f", [128, 1], "f32"), ("m_sl", [128, 128], "bf16"),
             ("m_su", [128, 128], "bf16"), ("m_iu", [128, 128], "bf16")]


def build_full(nc):
    import contextlib
    d = {}

    def din(name, shape, dt):
        d[name] = nc.dram_tensor(name, shape, dt, kind="ExternalInput").ap()

    din("x", [TOK, D], F32)
    din("consts", [128, MAX_CONSTS], F32)
    for nm, shape, dt in AUX_SPECS:
        din(nm, shape, BF16 if dt == "bf16" else F32)
    din("mask4", [128, 4, 512], BF16)
    din("wq_sl_a", [D, 256], F32)
    din("wk_sl_a", [D, 128], F32)
    din("wv_sl_a", [D, 128], F32)
    din("w_o_a", [D, D], F32)
    din("wq_sl_b", [D, 256], F32)
    din("wk_sl_b", [D, 256], F32)
    din("wv_sl_b", [D, 256], F32)
    din("w_o_b", [D, D], F32)
    din("gwT", [D, 8], F32)
    din("gate_wT", [N_EXP, D, F_EXP], F32)
    din("up_wT", [N_EXP, D, F_EXP], F32)
    din("down_wT", [N_EXP, F_EXP, D], F32)
    din("sh_gate_wT", [D, F_SH], F32)
    din("sh_up_wT", [D, F_SH], F32)
    din("sh_down_wT", [F_SH, D], F32)
    d["out"] = nc.dram_tensor("out", [TOK, D], F32, kind="ExternalOutput").ap()
    d["cc_t_a_in"] = nc.dram_tensor("cc_t_a_in", [TOK, D], BF16).ap()
    d["cc_t_a"] = nc.dram_tensor("cc_t_a", [S, D], BF16).ap()
    d["cc_mix_a_in"] = nc.dram_tensor("cc_mix_a_in", [D, TOK], BF16).ap()
    d["cc_mix_a"] = nc.dram_tensor("cc_mix_a", [D, TOK], BF16).ap()
    d["cc_t_b_in"] = nc.dram_tensor("cc_t_b_in", [TOK, D], BF16).ap()
    d["cc_t_b"] = nc.dram_tensor("cc_t_b", [S, D], BF16).ap()
    d["cc_mix_b_in"] = nc.dram_tensor("cc_mix_b_in", [D, TOK], BF16).ap()
    d["cc_mix_b"] = nc.dram_tensor("cc_mix_b", [D, TOK], BF16).ap()
    for t in ("f1", "f2"):
        d[f"trb_{t}"] = nc.dram_tensor(f"trb_{t}", [TOK, D], BF16).ap()
        d[f"trb_lo_{t}"] = nc.dram_tensor(f"trb_lo_{t}", [TOK, D], BF16).ap()
    d["xs_dram"] = nc.dram_tensor("xs_dram", [TOK, D], F32).ap()

    es = contextlib.ExitStack()
    with es:
        p = Prog(nc, es)
        p.init_consts(d["consts"])
        aux = {}
        s_aux = p.sem("aux_ld")
        for nm, shape, dt in AUX_SPECS:
            aux[nm] = p.sb(nm + "_s", shape, BF16 if dt == "bf16" else F32)

            def lf(e, nm=nm):
                e.dma_start(out=aux[nm][:], in_=d[nm]).then_inc(s_aux.h, 16)
            p.on("sync", lf)
            s_aux.bump(16)
        aux["eye_bf"] = aux["ident"]
        for eng in ("tensor", "vector", "scalar", "gpsimd"):
            p.wait(eng, s_aux, s_aux.n)

        st = p.sem("store")
        UPTO = "full"

        def early_store(x_buf, dep):
            def stf(e, dep=dep, stv=st.n + 16):
                e.wait_ge(dep[0].h, dep[1])
                e.dma_start(out=d["out"].rearrange("(t p) x -> p t x", p=128),
                            in_=x_buf[:]).then_inc(st.h, 16)
                e.wait_ge(st.h, stv)
            p.on("sync", stf)
            st.bump(16)

        with p.scope():
            x_sb = p.sb("x_sb", [128, NTT, D], F32)
            ld, ldv = emit_load_x(p, d["x"], x_sb)
            sra, vra = emit_attn_a(p, d, aux, x_sb, (ld, ldv), tag="a")
            bar_a = p.barrier()
            if UPTO == "attn_a":
                early_store(x_sb, bar_a)
                p.finalize()
                return d, p
            emit_ffn(p, d, aux, x_sb, [bar_a], "f1")
            bar_f1 = p.barrier()
            if UPTO == "ffn1":
                early_store(x_sb, bar_f1)
                p.finalize()
                return d, p
            with p.scope():
                t_bf3 = p.sb("t_b3", [128, NTT, D], BF16)
                s_ln3, ln3v = emit_ln(p, x_sb, t_bf3, [bar_f1], "t3")
                emit_allgather_tm(p, t_bf3, d["cc_t_b_in"], d["cc_t_b"],
                                  [(s_ln3, ln3v)], "bt")

                def spill(e, ln3v=ln3v):
                    e.wait_ge(s_ln3.h, ln3v)
                    e.dma_start(out=d["xs_dram"].rearrange("(t p) x -> p t x", p=128),
                                in_=x_sb[:]).then_inc(st.h, 16)
                p.on("sync", spill)
                st.bump(16)
                bar_x = p.barrier()

        if UPTO == "ln3":
            with p.scope():
                x_sb3 = p.sb("x_sb3", [128, NTT, D], F32)
                sld = p.sem("xreload")

                def rl(e, bx=bar_x):
                    e.wait_ge(bx[0].h, bx[1])
                    e.dma_start(out=x_sb3[:], in_=d["xs_dram"].rearrange("(t p) x -> p t x", p=128)).then_inc(sld.h, 16)
                p.on("sync", rl)
                early_store(x_sb3, (sld, sld.bump(16)))
                p.finalize()
                return d, p

        with p.scope():
            emit_attn_b_core(p, d, aux, [bar_x], "b")
            bar_b = p.barrier()

        if UPTO == "attn_b":
            with p.scope():
                x_sb3 = p.sb("x_sb3", [128, NTT, D], F32)
                sld = p.sem("xreload")

                def rl(e, bx=bar_b):
                    e.wait_ge(bx[0].h, bx[1])
                    e.dma_start(out=x_sb3[:], in_=d["xs_dram"].rearrange("(t p) x -> p t x", p=128)).then_inc(sld.h, 16)
                p.on("sync", rl)
                early_store(x_sb3, (sld, sld.bump(16)))
                p.finalize()
                return d, p

        with p.scope():
            x_sb2 = p.sb("x_sb2", [128, NTT, D], F32)
            with p.scope():
                s_rb, v_rb = emit_out_proj(p, d["cc_mix_b"], d["w_o_b"], d["xs_dram"], x_sb2,
                                           [bar_b], "bo")
                bar_bo = p.barrier()
            if UPTO == "outproj":
                early_store(x_sb2, bar_bo)
                p.finalize()
                return d, p
            s_xa2, xa2v = emit_ffn(p, d, aux, x_sb2, [bar_bo], "f2")

            def store(e, xa2v=xa2v, stv=st.n + 16):
                e.wait_ge(s_xa2.h, xa2v)
                e.dma_start(out=d["out"].rearrange("(t p) x -> p t x", p=128),
                            in_=x_sb2[:]).then_inc(st.h, 16)
                e.wait_ge(st.h, stv)
            p.on("sync", store)
            st.bump(16)
        p.finalize()
    return d, p


def make_in_maps(inputs, const_vals):
    import ml_dtypes
    x_full = np.asarray(inputs["L0_attn_a_x"][0], np.float32)
    w_q = np.asarray(inputs["L0_attn_a_w_q"], np.float32)
    w_kv = np.asarray(inputs["L0_attn_a_w_kv"], np.float32)
    w_dense = np.ascontiguousarray(np.asarray(inputs["L0_attn_a_w_dense"], np.float32))
    Wq = np.asarray(inputs["L0_attn_b_Wq"], np.float32)
    Wk = np.asarray(inputs["L0_attn_b_Wk"], np.float32)
    Wv = np.asarray(inputs["L0_attn_b_Wv"], np.float32)
    Wo = np.ascontiguousarray(np.asarray(inputs["L0_attn_b_Wo"], np.float32))
    gw = np.asarray(inputs["L0_ffn_gate_weight"], np.float32)
    gate_w = np.asarray(inputs["L0_ffn_gate_w"], np.float32)
    up_w = np.asarray(inputs["L0_ffn_up_w"], np.float32)
    down_w = np.asarray(inputs["L0_ffn_down_w"], np.float32)
    sh_gate = np.asarray(inputs["L0_ffn_sh_gate_w"], np.float32)
    sh_up = np.asarray(inputs["L0_ffn_sh_up_w"], np.float32)
    sh_down = np.asarray(inputs["L0_ffn_sh_down_w"], np.float32)

    cv = np.zeros((128, MAX_CONSTS), np.float32)
    cv[:, :len(const_vals)] = np.array(const_vals, np.float32)[None, :]
    ident = np.eye(128, dtype=ml_dtypes.bfloat16)
    ones_bf = np.ones((128, 1), ml_dtypes.bfloat16)
    ones1_f = np.ones((1, 128), np.float32)
    ones1_bf = np.ones((1, 128), ml_dtypes.bfloat16)
    ones_f = np.ones((128, 1), np.float32)
    mask4 = np.zeros((128, 4, 512), np.float32)
    for o in range(4):
        for kk in range(128):
            mask4[kk, o, :] = (np.arange(512) >= 128 * o + kk)
    mask4 = mask4.astype(ml_dtypes.bfloat16)
    m_sl = np.tril(np.ones((128, 128), np.float32), -1).astype(ml_dtypes.bfloat16)
    m_su = np.triu(np.ones((128, 128), np.float32), 1).astype(ml_dtypes.bfloat16)
    m_iu = np.triu(np.ones((128, 128), np.float32)).astype(ml_dtypes.bfloat16)

    shared = {
        "consts": cv, "ident": ident, "ones_bf": ones_bf, "ones1_f": ones1_f,
        "ones1_bf": ones1_bf, "ones_f": ones_f, "mask4": mask4, "m_sl": m_sl,
        "m_su": m_su, "m_iu": m_iu,
        "w_o_a": w_dense, "w_o_b": Wo,
        "gwT": np.ascontiguousarray(gw.T),
        "gate_wT": np.ascontiguousarray(gate_w.transpose(0, 2, 1)),
        "up_wT": np.ascontiguousarray(up_w.transpose(0, 2, 1)),
        "down_wT": np.ascontiguousarray(down_w.transpose(0, 2, 1)),
        "sh_gate_wT": np.ascontiguousarray(sh_gate.T),
        "sh_up_wT": np.ascontiguousarray(sh_up.T),
        "sh_down_wT": np.ascontiguousarray(sh_down.T),
    }
    in_maps = []
    for c in range(R):
        g = c // 2
        m = dict(shared)
        m["x"] = np.ascontiguousarray(x_full[c * TOK:(c + 1) * TOK])
        m["wq_sl_a"] = np.ascontiguousarray(w_q[:, 256 * c:256 * (c + 1)])
        m["wk_sl_a"] = np.ascontiguousarray(w_kv[:, 128 * g:128 * (g + 1)])
        m["wv_sl_a"] = np.ascontiguousarray(w_kv[:, 512 + 128 * g:512 + 128 * (g + 1)])
        m["wq_sl_b"] = np.ascontiguousarray(Wq[:, 256 * c:256 * (c + 1)])
        m["wk_sl_b"] = np.ascontiguousarray(Wk[:, 256 * c:256 * (c + 1)])
        m["wv_sl_b"] = np.ascontiguousarray(Wv[:, 256 * c:256 * (c + 1)])
        in_maps.append(m)
    return in_maps



_CACHE = {}


def kernel(**inputs):
    import concourse.bass as bass_mod
    from concourse.bass_utils import run_bass_kernel_spmd as _run
    if "nc" not in _CACHE:
        nc = bass_mod.Bass("TRN2", target_bir_lowering=False, debug=False)
        d, p = build_full(nc)
        _CACHE["nc"] = nc
        _CACHE["const_vals"] = list(p.const_vals)
    nc = _CACHE["nc"]
    in_maps = make_in_maps(inputs, _CACHE["const_vals"])
    res = _run(nc, in_maps, core_ids=list(range(R)))
    out = np.concatenate([np.asarray(res.results[c]["out"]) for c in range(R)], axis=0)
    return out[None].astype(np.float32)


# version 12
# speedup vs baseline: 1.0985x; 1.0985x over previous
import contextlib

import numpy as np

import concourse.bass as bass
import concourse.mybir as mybir

F32 = mybir.dt.float32
BF16 = mybir.dt.bfloat16
AF = mybir.ActivationFunctionType
ALU = mybir.AluOpType
AX = mybir.AxisListType

S = 8192
D = 2048
R = 8
TOK = S // R
NTT = TOK // 128
NDC = D // 128
DH = 128
NCH = S // 128
F_EXP = 1408
F_SH = 2816
N_EXP = 8
EPS_LN = 1e-5
ATT_SCALE = float(DH) ** -0.5

ENGINES = ("sync", "scalar", "vector", "tensor", "gpsimd")


class Sem:
    __slots__ = ("h", "n")

    def __init__(self, h):
        self.h = h
        self.n = 0

    def bump(self, k=1):
        self.n += k
        return self.n


MAX_CONSTS = 32


class Prog:

    def __init__(self, nc, es):
        self.nc = nc
        self.es = es
        self.scopes = [es]
        self.q = {e: [] for e in ENGINES}
        self._sems = {}
        self.const_vals = []
        self.const_tile = None
        self.const_dram = None

    def scope(self):
        import contextlib as _cl
        p = self

        class _Scope:
            def __enter__(self):
                self.es = _cl.ExitStack()
                p.scopes.append(self.es)
                return self.es

            def __exit__(self, *a):
                self.es.close()
                p.scopes.pop()
                return False
        return _Scope()

    def barrier(self, engines=ENGINES):
        s_bar = self.sem("bar")
        targets = tuple((s, s.n) for name, s in self._sems.items()
                        if s.n > 0 and name != "bar")
        for eng in engines:
            def f(e, targets=targets):
                for (s, v) in targets:
                    e.wait_ge(s.h, v)
                e.drain()
                e.sem_inc(s_bar.h, 1)
            self.on(eng, f)
        return s_bar, s_bar.bump(len(engines))

    def init_consts(self, const_dram):
        self.const_dram = const_dram
        self.const_tile = self.sb("const_pool", [128, MAX_CONSTS], F32)

    def constv(self, value):
        value = float(value)
        if value not in self.const_vals:
            assert len(self.const_vals) < MAX_CONSTS
            self.const_vals.append(value)
            j = self.const_vals.index(value)
            self.nc.const_aps.register(F32, value, self.const_tile[:, j:j + 1])
        return value

    SEM_POOL = 96

    def sem(self, name) -> Sem:
        if name not in self._sems:
            if not hasattr(self, "_pool"):
                self._pool = []
            if len(self._pool) < self.SEM_POOL:
                s = Sem(self.es.enter_context(self.nc.semaphore(name)))
                self._pool.append(s)
            else:
                s = self._pool[len(self._sems) % self.SEM_POOL]
            self._sems[name] = s
        return self._sems[name]

    def sb(self, name, shape, dtype):
        return self.scopes[-1].enter_context(self.nc.sbuf_tensor(name, shape, dtype))

    def ps(self, name, shape, dtype=F32):
        return self.scopes[-1].enter_context(self.nc.psum_tensor(name, shape, dtype))

    def on(self, eng, f):
        self.q[eng].append(f)

    def wait(self, eng, sem: Sem, v: int):
        def f(e, sem=sem, v=v):
            e.wait_ge(sem.h, v)
        self.on(eng, f)

    def finalize(self):
        if self.const_dram is not None:
            s_const = self.sem("const_ld")
            ctile = self.const_tile
            cdram = self.const_dram

            def cload(e):
                e.dma_start(out=ctile[:], in_=cdram).then_inc(s_const.h, 16)
            self.q["sync"].insert(0, cload)
            v = s_const.bump(16)
            for eng in ENGINES:
                if eng == "sync":
                    continue

                def cw(e, v=v):
                    e.wait_ge(s_const.h, v)
                self.q[eng].insert(0, cw)

        block = self.es.enter_context(self.nc.Block())

        def runner(eng):
            def body(e):
                for f in self.q[eng]:
                    f(e)
            body.__name__ = f"body_{eng}"
            return body

        block.sync(runner("sync"))
        block.scalar(runner("scalar"))
        block.vector(runner("vector"))
        block.tensor(runner("tensor"))
        block.gpsimd(runner("gpsimd"))



def emit_load_x(p: Prog, x_dram, x_sb):
    ld = p.sem("x_ld")
    src = x_dram.rearrange("(t p) d -> p t d", p=128)
    v = ld.bump(16)

    def f(e, src=src):
        e.dma_start(out=x_sb[:], in_=src).then_inc(ld.h, 16)
    p.on("sync", f)
    return ld, v



def emit_ln(p: Prog, x_sb, t_bf, deps, tag, t_lo=None):
    p.constv(EPS_LN)
    p.constv(-0.5)
    sq = p.sb(f"ln_sq_{tag}", [128, D], F32)
    ssq = p.sb(f"ln_ssq_{tag}", [128, 1], F32)
    sx = p.sb(f"ln_sx_{tag}", [128, 1], F32)
    mean = p.sb(f"ln_mean_{tag}", [128, 1], F32)
    var = p.sb(f"ln_var_{tag}", [128, 1], F32)
    rstd = p.sb(f"ln_rstd_{tag}", [128, 1], F32)
    tf = p.sb(f"ln_tf_{tag}", [128, D], F32) if t_lo is not None else None

    s_sq = p.sem("ln_sq")
    s_st = p.sem("ln_st")
    s_rs = p.sem("ln_rs")
    s_done = p.sem("ln_done")

    for i in range(NTT):
        xt = x_sb[:, i, :]

        st_war = s_st.n

        def act1(e, xt=xt, i=i, st_war=st_war):
            if i == 0:
                for (sm, v) in deps:
                    e.wait_ge(sm.h, v)
            else:
                e.wait_ge(s_st.h, st_war)
            e.activation(sq[:], xt, AF.Square, accum_out=ssq[:]).then_inc(s_sq.h, 1)
        p.on("scalar", act1)
        sq_v = s_sq.bump()

        def dve1(e, xt=xt, sq_v=sq_v, i=i):
            if i == 0:
                for (sm, v) in deps:
                    e.wait_ge(sm.h, v)
            e.reduce_sum(sx[:], xt, axis=AX.X)
            e.drain()
            e.wait_ge(s_sq.h, sq_v)
            e.tensor_scalar_mul(mean[:], sx[:], 1.0 / D)
            e.drain()
            e.tensor_tensor(sx[:], mean[:], mean[:], ALU.mult)
            e.tensor_scalar_mul(var[:], ssq[:], 1.0 / D)
            e.drain()
            e.tensor_tensor(var[:], var[:], sx[:], ALU.subtract).then_inc(s_st.h, 1)
        p.on("vector", dve1)
        st_v = s_st.bump()

        def act2(e, st_v=st_v):
            e.wait_ge(s_st.h, st_v)
            e.activation(var[:], var[:], AF.Ln, bias=EPS_LN)
            e.drain()
            e.activation(rstd[:], var[:], AF.Exp, scale=-0.5).then_inc(s_rs.h, 1)
        p.on("scalar", act2)
        rs_v = s_rs.bump()

        def dve2(e, xt=xt, i=i, rs_v=rs_v):
            e.wait_ge(s_rs.h, rs_v)
            if tf is not None:
                e.tensor_scalar(tf[:], xt, mean[:], rstd[:], ALU.subtract, ALU.mult)
                e.tensor_copy(t_bf[:, i, :], tf[:])
                e.tensor_tensor(t_lo[:, i, :], tf[:], t_bf[:, i, :],
                                ALU.subtract).then_inc(s_done.h, 1)
            else:
                e.tensor_scalar(t_bf[:, i, :], xt, mean[:], rstd[:],
                                ALU.subtract, ALU.mult).then_inc(s_done.h, 1)
        p.on("vector", dve2)
        s_done.bump()

    return s_done, s_done.n



def emit_allgather_tm(p: Prog, sb_tile, cc_in, cc_out, deps, tag):
    s_cc = p.sem("ag")

    def f(e):
        for (sm, v) in deps:
            e.wait_ge(sm.h, v)
        e.dma_start(out=cc_in.rearrange("(t p) d -> p t d", p=128), in_=sb_tile[:]).then_inc(s_cc.h, 16)
    p.on("scalar", f)
    v0 = s_cc.bump(16)

    def g(e, v0=v0):
        e.wait_ge(s_cc.h, v0)
        e.collective_compute(
            "AllGather", ALU.bypass,
            replica_groups=[list(range(R))],
            ins=[cc_in],
            outs=[cc_out],
        ).then_inc(s_cc.h, 1)
    p.on("gpsimd", g)
    v1 = s_cc.bump(1)
    return s_cc, v1



def emit_attn_proj(p: Prog, ag_t, specs, deps, tag):
    nm_tot = sum(n_m for (_, _, n_m, _) in specs)
    wsl = []
    s_w = p.sem("prj_w")
    for si, (w_dram, col0, n_m, outT) in enumerate(specs):
        for m in range(n_m):
            wt = p.sb(f"prj_w_{tag}_{si}_{m}", [128, NDC, 128], BF16)
            src = w_dram[:, col0 + m * 128: col0 + (m + 1) * 128].rearrange(
                "(a p) c -> p a c", p=128)

            def f(e, wt=wt, src=src):
                e.dma_start(out=wt[:], in_=src).then_inc(s_w.h, 16)
            p.on("gpsimd", f)
            s_w.bump(16)
            wsl.append((wt, outT, m))
    w_all = s_w.n

    NBUF_T = 1
    tT = [p.sb(f"prj_tT_{tag}_{b}", [128, NDC, TOK], BF16) for b in range(NBUF_T)]
    tT = [tT[b % NBUF_T] for b in range(2)]
    ps = [p.ps(f"prj_ps_{tag}_{b}", [128, 512]) for b in range(2)]
    s_tl = p.sem("prj_tl")
    s_mm = p.sem("prj_mm")
    s_ev = p.sem("prj_ev")
    s_done = p.sem("prj_done")

    n_psum_per_rank = nm_tot * (TOK // 512)
    base_ev = s_ev.n

    for r in range(R):
        buf = tT[r % 2]
        def loads(e, r=r, buf=buf, ev_war=base_ev + r * n_psum_per_rank):
            if r == 0:
                for (sm, v) in deps:
                    e.wait_ge(sm.h, v)
            else:
                e.wait_ge(s_ev.h, ev_war)
            for j in range(NDC):
                e.dma_start_transpose(
                    buf[:, j, :],
                    ag_t[r * TOK:(r + 1) * TOK, j * 128:(j + 1) * 128],
                ).then_inc(s_tl.h, 16)
        p.on("sync", loads)
        tl_v = s_tl.bump(16 * NDC)

        kk = 0
        for (wt, outT, m) in wsl:
            for nck in range(TOK // 512):
                gidx = r * n_psum_per_rank + kk
                bank = ps[gidx % 2]

                def mm(e, wt=wt, nck=nck, bank=bank, buf=buf, tl_v=tl_v,
                       ev_need=base_ev + max(0, gidx - 1), w_all=w_all, first=(kk == 0)):
                    if first:
                        e.wait_ge(s_w.h, w_all)
                        e.wait_ge(s_tl.h, tl_v)
                    e.wait_ge(s_ev.h, ev_need)
                    ins = None
                    for j in range(NDC):
                        ins = e.matmul(bank[:], wt[:, j, :],
                                       buf[:, j, nck * 512:(nck + 1) * 512],
                                       start=(j == 0), stop=(j == NDC - 1))
                    ins.then_inc(s_mm.h, 1)
                p.on("tensor", mm)
                mm_v = s_mm.bump()

                def ev(e, bank=bank, outT=outT, m=m, r=r, nck=nck, mm_v=mm_v):
                    e.wait_ge(s_mm.h, mm_v)
                    e.tensor_copy(outT[:, m, r * TOK + nck * 512: r * TOK + (nck + 1) * 512],
                                  bank[:]).then_inc(s_ev.h, 1)
                p.on("vector", ev)
                s_ev.bump()
                kk += 1

    return s_ev, s_ev.n



def emit_flash_head(p: Prog, qT, kT, v_tm, outT, ones_bf, ones1_f, mask4,
                    deps, tag, olinT=None, wbase=1.0, wlin=0.5, ones_f=None):
    p.constv(ATT_SCALE)
    NQB = S // 512
    BQ = 3
    ps_s = [p.ps(f"fl_s_{tag}_{b}", [128, BQ * 512]) for b in range(2)]
    ps_c = p.ps(f"fl_c_{tag}", [128, 512])
    ps_b = p.ps(f"fl_b_{tag}", [128, 512])
    ps_d = ps_b[0:1, :]
    pT = [p.sb(f"fl_p_{tag}_{b}", [128, BQ * 512], BF16) for b in range(2)]
    sacc = [p.sb(f"fl_sacc_{tag}_{b}", [128, 512], F32) for b in range(2)]
    stmp = [p.sb(f"fl_stmp_{tag}_{b}", [128, 512], BF16) for b in range(2)]
    den = p.sb(f"fl_den_{tag}", [1, 512], F32)
    bc = p.sb(f"fl_bc_{tag}", [128, 512], F32)
    tmp = p.sb(f"fl_tmp_{tag}", [128, 512], F32)

    s_sc = p.sem("fl_sc")
    s_exp = p.sem("fl_exp")
    s_pm = p.sem("fl_pm")
    s_cd = p.sem("fl_cd")
    s_no = p.sem("fl_no")

    base_pm = s_pm.n

    batches = []
    for qb in range(NQB):
        nkt = 4 * (qb + 1)
        kts = [list(range(b0, min(b0 + BQ, nkt))) for b0 in range(0, nkt, BQ)]
        for bi, batch in enumerate(kts):
            batches.append((qb, bi, len(kts), batch))
    NB = len(batches)

    pe_s = [None] * NB
    pe_cd = [None] * NB
    pe_nm = {}
    cd_v_of = {}

    for gbt, (qb, bi, nbt, batch) in enumerate(batches):
        b = gbt % 2
        nb = len(batch)

        def mm_s(e, b=b, batch=batch, qb=qb, pm_need=base_pm + max(0, gbt - 1),
                 first=(gbt == 0)):
            if first:
                for (sm, v) in deps:
                    e.wait_ge(sm.h, v)
            e.wait_ge(s_pm.h, pm_need)
            ins = None
            for i, kt in enumerate(batch):
                ins = e.matmul(ps_s[b][:, i * 512:(i + 1) * 512],
                               kT[:, kt * 128:(kt + 1) * 128],
                               qT[:, qb * 512:(qb + 1) * 512],
                               start=True, stop=True)
            ins.then_inc(s_sc.h, 1)
        pe_s[gbt] = mm_s
        sc_v = s_sc.bump()

        def act_e(e, b=b, nb=nb, sc_v=sc_v):
            e.wait_ge(s_sc.h, sc_v)
            e.activation(pT[b][:, 0:nb * 512], ps_s[b][:, 0:nb * 512], AF.Exp,
                         scale=ATT_SCALE).then_inc(s_exp.h, 1)
        p.on("scalar", act_e)
        exp_v = s_exp.bump()

        ab = bi % 2

        def dve_a(e, b=b, ab=ab, batch=batch, qb=qb, exp_v=exp_v,
                  first=(bi == 0), no_need=s_no.n):
            e.wait_ge(s_exp.h, exp_v)
            for i, kt in enumerate(batch):
                diag = kt - 4 * qb
                if diag >= 0:
                    e.tensor_tensor(pT[b][:, i * 512:(i + 1) * 512],
                                    pT[b][:, i * 512:(i + 1) * 512],
                                    mask4[:, diag, :], ALU.mult)
                    e.drain()
            nbq = len(batch)
            if nbq == 3:
                e.tensor_tensor(stmp[0][:], pT[b][:, 0:512], pT[b][:, 512:1024], ALU.add)
                e.drain()
                e.tensor_tensor(stmp[1][:], stmp[0][:], pT[b][:, 1024:1536], ALU.add)
                e.drain()
                part = stmp[1]
            elif nbq == 2:
                e.tensor_tensor(stmp[1][:], pT[b][:, 0:512], pT[b][:, 512:1024], ALU.add)
                e.drain()
                part = stmp[1]
            else:
                part = None
            if first:
                e.wait_ge(s_no.h, no_need)
                if part is None:
                    e.tensor_copy(sacc[ab][:], pT[b][:, 0:512])
                    e.drain()
                else:
                    e.tensor_copy(sacc[ab][:], part[:])
                    e.drain()
            else:
                e.tensor_tensor(sacc[ab][:], sacc[1 - ab][:],
                                part[:] if part is not None else pT[b][:, 0:512],
                                ALU.add)
                e.drain()
            e.engine_nop().then_inc(s_pm.h, 1)
        p.on("vector", dve_a)
        pm_v = s_pm.bump()

        def mm_cd(e, b=b, batch=batch, pm_v=pm_v, firstb=(bi == 0),
                  lastb=(bi == nbt - 1), no_need=s_no.n, ab=ab):
            e.wait_ge(s_pm.h, pm_v)
            if firstb:
                e.wait_ge(s_no.h, no_need)
            ins = None
            for i, kt in enumerate(batch):
                ins = e.matmul(ps_c[:], v_tm[:, kt, :],
                               pT[b][:, i * 512:(i + 1) * 512],
                               start=(firstb and i == 0),
                               stop=(lastb and i == len(batch) - 1))
            if lastb:
                ins2 = e.matmul(ps_d, ones_f[:], sacc[ab][:], start=True, stop=True)
                ins2.then_inc(s_cd.h, 1)
        pe_cd[gbt] = mm_cd
        if bi == nbt - 1:
            cd_v_of[qb] = s_cd.bump()
            cd_v = cd_v_of[qb]

            def dve_n1(e, cd_v=cd_v, wb=float(wbase)):
                e.wait_ge(s_cd.h, cd_v)
                e.tensor_copy(den[:], ps_d)
                e.drain()
                e.reciprocal(den[:], den[:])
                e.drain()
                e.tensor_scalar_mul(den[:], den[:], wb)
                e.drain()
                e.engine_nop().then_inc(s_no.h, 1)
            p.on("vector", dve_n1)
            no_v1 = s_no.bump()

            def mm_b(e, no_v1=no_v1):
                e.wait_ge(s_no.h, no_v1)
                e.matmul(ps_b[:], ones1_f[:], den[:], start=True,
                         stop=True).then_inc(s_cd.h, 1)
            pe_nm[qb] = mm_b
            cd_v2 = s_cd.bump()

            def dve_n2(e, qb=qb, cd_v2=cd_v2):
                e.wait_ge(s_cd.h, cd_v2)
                e.tensor_copy(bc[:], ps_b[:])
                e.drain()
                sl = slice(qb * 512, (qb + 1) * 512)
                if olinT is None:
                    e.tensor_tensor(outT[:, sl], ps_c[:], bc[:], ALU.mult).then_inc(s_no.h, 1)
                else:
                    e.tensor_tensor(tmp[:], ps_c[:], bc[:], ALU.mult)
                    e.drain()
                    e.scalar_tensor_tensor(outT[:, sl], olinT[:, sl], float(wlin), tmp[:],
                                           ALU.mult, ALU.add).then_inc(s_no.h, 1)
            p.on("vector", dve_n2)
            s_no.bump()

    if NB > 0:
        p.on("tensor", pe_s[0])
    if NB > 1:
        p.on("tensor", pe_s[1])
    for j, (qb, bi, nbt, batch) in enumerate(batches):
        p.on("tensor", pe_cd[j])
        if bi == nbt - 1:
            p.on("tensor", pe_nm[qb])
        if j + 2 < NB:
            p.on("tensor", pe_s[j + 2])

    return s_no, s_no.n



def emit_transpose_fm_tm(p: Prog, srcT, dst_tm, n, ident, deps, tag):
    ps = [p.ps(f"tp_{tag}_{b}", [128, 128], BF16) for b in range(2)]
    s_t = p.sem("tp_t")
    s_e = p.sem("tp_e")
    base_e = s_e.n
    for c in range(n):
        b = c % 2

        def mm(e, c=c, b=b, ev_need=base_e + max(0, c - 1), first=(c == 0)):
            if first:
                for (sm, v) in deps:
                    e.wait_ge(sm.h, v)
            e.wait_ge(s_e.h, ev_need)
            e.transpose(ps[b][:], srcT[:, c * 128:(c + 1) * 128], ident[:]).then_inc(s_t.h, 1)
        p.on("tensor", mm)
        t_v = s_t.bump()

        def ev(e, c=c, b=b, t_v=t_v):
            e.wait_ge(s_t.h, t_v)
            e.tensor_copy(dst_tm[:, c, :], ps[b][:]).then_inc(s_e.h, 1)
        p.on("vector", ev)
        s_e.bump()
    return s_e, s_e.n



def emit_out_proj(p: Prog, mixT_loc, w_dram, x_src, x_sb, deps, tag):
    ctxT = p.sb(f"op_ctx_{tag}", [128, NDC, TOK], BF16)
    w_sb = p.sb(f"op_w_{tag}", [128, NDC, D], BF16)
    ps_o = [p.ps(f"op_ps_{tag}_{b}", [128, D]) for b in range(2)]

    s_ld = p.sem("op_ld")
    s_w = p.sem("op_w")
    s_mm = p.sem("op_mm")
    s_res = p.sem("op_res")

    def loads(e):
        for (sm, v) in deps:
            e.wait_ge(sm.h, v)
        if x_src is not None:
            e.dma_start(out=x_sb[:], in_=x_src.rearrange("(t p) d -> p t d", p=128)).then_inc(s_ld.h, 16)
        for j in range(NDC):
            e.dma_start(out=ctxT[:, j, :],
                        in_=mixT_loc[j * 128:(j + 1) * 128, :]).then_inc(s_ld.h, 16)
    p.on("sync", loads)
    ld_v = s_ld.bump(16 * (NDC + (1 if x_src is not None else 0)))

    for grp in range(4):
        def wload(e, grp=grp):
            if grp == 0:
                for (sm, v) in deps:
                    e.wait_ge(sm.h, v)
            src = w_dram[grp * 512:(grp + 1) * 512, :].rearrange("(a p) d -> p a d", p=128)
            e.dma_start(out=w_sb[:, grp * 4:(grp + 1) * 4, :], in_=src).then_inc(s_w.h, 16)
        p.on("gpsimd", wload)
        s_w.bump(16)
    w_all = s_w.n

    base_res = s_res.n
    for tt in range(NTT):
        bank = ps_o[tt % 2]

        def mm(e, tt=tt, bank=bank, res_need=base_res + max(0, tt - 1), first=(tt == 0)):
            if first:
                e.wait_ge(s_ld.h, ld_v)
                e.wait_ge(s_w.h, w_all)
            e.wait_ge(s_res.h, res_need)
            ins = None
            for j in range(NDC):
                for nch in range(4):
                    ins = e.matmul(bank[:, nch * 512:(nch + 1) * 512],
                                   ctxT[:, j, tt * 128:(tt + 1) * 128],
                                   w_sb[:, j, nch * 512:(nch + 1) * 512],
                                   start=(j == 0), stop=(j == NDC - 1))
            ins.then_inc(s_mm.h, 1)
        p.on("tensor", mm)
        mm_v = s_mm.bump()

        def res(e, tt=tt, bank=bank, mm_v=mm_v):
            e.wait_ge(s_mm.h, mm_v)
            e.tensor_tensor(x_sb[:, tt, :], x_sb[:, tt, :], bank[:], ALU.add).then_inc(s_res.h, 1)
        p.on("vector", res)
        s_res.bump()

    return s_res, s_res.n



def emit_aux_load(p: Prog, dram_ap, shape, dtype, tag):
    sb = p.sb(f"auxl_{tag}", shape, dtype)
    s = p.sem("auxl")

    def lf(e):
        e.dma_start(out=sb[:], in_=dram_ap).then_inc(s.h, 16)
    p.on("sync", lf)
    return sb, (s, s.bump(16))


def emit_attn_a(p: Prog, d, aux, x_sb, xdep, tag="a"):
    with p.scope():
        qT = p.sb(f"qT_{tag}", [128, 2, S], BF16)
        kT = p.sb(f"kT_{tag}", [128, 1, S], BF16)
        vT = p.sb(f"vT_{tag}", [128, 1, S], BF16)

        with p.scope():
            t_bf = p.sb(f"t_{tag}", [128, NTT, D], BF16)
            s_ln, lnv = emit_ln(p, x_sb, t_bf, [xdep], f"{tag}")
            emit_allgather_tm(p, t_bf, d[f"cc_t_{tag}_in"], d[f"cc_t_{tag}"],
                              [(s_ln, lnv)], f"{tag}t")
            bar0 = p.barrier()

        with p.scope():
            specs = [
                (d[f"wq_sl_{tag}"], 0, 2, qT),
                (d[f"wk_sl_{tag}"], 0, 1, kT),
                (d[f"wv_sl_{tag}"], 0, 1, vT),
            ]
            emit_attn_proj(p, d[f"cc_t_{tag}"], specs, [bar0], f"{tag}")
            bar1 = p.barrier()

        with p.scope():
            v_tm = p.sb(f"vtm_{tag}", [128, NCH, 128], BF16)
            mask4, m4dep = emit_aux_load(p, d["mask4"], [128, 4, 512], BF16, f"m4{tag}",
                                         deps=[bar1])
            with p.scope():
                emit_transpose_fm_tm(p, vT[:, 0, :], v_tm, NCH, aux["ident"],
                                     [bar1], f"{tag}v")
                bar_tp = p.barrier(engines=("tensor", "vector"))
            s_mx = p.sem("mx")
            hdep = [bar1, bar_tp, m4dep]
            for h in range(2):
                with p.scope():
                    outT = p.sb(f"mixT_{tag}_{h}", [128, S], BF16)
                    s_fl, flv = emit_flash_head(p, qT[:, h, :], kT[:, 0, :], v_tm, outT,
                                                aux["ones_bf"], aux["ones1_f"], mask4,
                                                hdep, f"{tag}h{h}", ones_f=aux["ones_f"])

                    def mxo(e, h=h, outT=outT, flv=flv, s_fl=s_fl):
                        e.wait_ge(s_fl.h, flv)
                        for j in range(R):
                            e.dma_start(out=d[f"cc_mix_{tag}_in"][256 * j + 128 * h:256 * j + 128 * (h + 1), :],
                                        in_=outT[:, j * TOK:(j + 1) * TOK]).then_inc(s_mx.h, 16)
                    p.on("scalar", mxo)
                    s_mx.bump(16 * R)
                    bar_h = p.barrier(engines=("tensor", "scalar", "vector"))
                    hdep = [bar_h]

            def agm(e, mxv=s_mx.n):
                from concourse.aot_env import allow_ici_all_to_all
                e.wait_ge(s_mx.h, mxv)
                with allow_ici_all_to_all():
                    e.collective_compute(
                        "AllToAll", ALU.bypass,
                        replica_groups=[list(range(R))],
                        ins=[d[f"cc_mix_{tag}_in"]],
                        outs=[d[f"cc_mix_{tag}"]],
                    ).then_inc(s_mx.h, 1)
            p.on("gpsimd", agm)
            s_mx.bump(1)

        bar2 = p.barrier()

    with p.scope():
        s_res, resv = emit_out_proj(p, d[f"cc_mix_{tag}"], d[f"w_o_{tag}"], None,
                                    x_sb, [bar2], f"{tag}")
    return s_res, resv






def emit_ln_dram(p: Prog, x_sb, trb, trb_lo, deps, tag):
    p.constv(EPS_LN)
    p.constv(-0.5)
    sq = p.sb(f"lnd_sq_{tag}", [128, D], F32)
    ssq = p.sb(f"lnd_ssq_{tag}", [128, 1], F32)
    sx = p.sb(f"lnd_sx_{tag}", [128, 1], F32)
    mean = p.sb(f"lnd_mean_{tag}", [128, 1], F32)
    var = p.sb(f"lnd_var_{tag}", [128, 1], F32)
    rstd = p.sb(f"lnd_rstd_{tag}", [128, 1], F32)
    tf = p.sb(f"lnd_tf_{tag}", [128, D], F32)
    stage = [p.sb(f"lnd_st_{tag}_{b}", [128, D], BF16) for b in range(2)]
    stage_lo = [p.sb(f"lnd_sl_{tag}_{b}", [128, D], BF16) for b in range(2)] if trb_lo is not None else None
    ndma = 2 if trb_lo is not None else 1

    s_sq = p.sem("lnd_sq")
    s_st = p.sem("lnd_st")
    s_rs = p.sem("lnd_rs")
    s_done = p.sem("lnd_done")
    s_tb = p.sem("lnd_tb")
    base_tb = s_tb.n

    trb_r = trb.rearrange("(t p) x -> t p x", p=128)
    trb_lo_r = trb_lo.rearrange("(t p) x -> t p x", p=128) if trb_lo is not None else None

    for i in range(NTT):
        xt = x_sb[:, i, :]
        st_war = s_st.n

        def act1(e, xt=xt, i=i, st_war=st_war):
            if i == 0:
                for (sm, v) in deps:
                    e.wait_ge(sm.h, v)
            else:
                e.wait_ge(s_st.h, st_war)
            e.activation(sq[:], xt, AF.Square, accum_out=ssq[:]).then_inc(s_sq.h, 1)
        p.on("scalar", act1)
        sq_v = s_sq.bump()

        def dve1(e, xt=xt, sq_v=sq_v, i=i):
            if i == 0:
                for (sm, v) in deps:
                    e.wait_ge(sm.h, v)
            e.reduce_sum(sx[:], xt, axis=AX.X)
            e.drain()
            e.wait_ge(s_sq.h, sq_v)
            e.tensor_scalar_mul(mean[:], sx[:], 1.0 / D)
            e.drain()
            e.tensor_tensor(sx[:], mean[:], mean[:], ALU.mult)
            e.tensor_scalar_mul(var[:], ssq[:], 1.0 / D)
            e.drain()
            e.tensor_tensor(var[:], var[:], sx[:], ALU.subtract).then_inc(s_st.h, 1)
        p.on("vector", dve1)
        st_v = s_st.bump()

        def act2(e, st_v=st_v):
            e.wait_ge(s_st.h, st_v)
            e.activation(var[:], var[:], AF.Ln, bias=EPS_LN)
            e.drain()
            e.activation(rstd[:], var[:], AF.Exp, scale=-0.5).then_inc(s_rs.h, 1)
        p.on("scalar", act2)
        rs_v = s_rs.bump()

        def dve2(e, xt=xt, i=i, rs_v=rs_v,
                 tb_war=base_tb + i * 16 * ndma):
            e.wait_ge(s_rs.h, rs_v)
            e.wait_ge(s_tb.h, tb_war)
            b = i % 2
            if trb_lo is not None:
                e.tensor_scalar(tf[:], xt, mean[:], rstd[:], ALU.subtract, ALU.mult)
                e.drain()
                e.tensor_copy(stage[b][:], tf[:])
                e.drain()
                e.tensor_tensor(stage_lo[b][:], tf[:], stage[b][:],
                                ALU.subtract).then_inc(s_done.h, 1)
            else:
                e.tensor_scalar(stage[b][:], xt, mean[:], rstd[:],
                                ALU.subtract, ALU.mult).then_inc(s_done.h, 1)
        p.on("vector", dve2)
        dn_v = s_done.bump()

        def tdma(e, i=i, dn_v=dn_v):
            e.wait_ge(s_done.h, dn_v)
            b = i % 2
            e.dma_start(out=trb_r[i], in_=stage[b][:]).then_inc(s_tb.h, 16)
            if trb_lo is not None:
                e.dma_start(out=trb_lo_r[i], in_=stage_lo[b][:]).then_inc(s_tb.h, 16)
        p.on("scalar", tdma)
        s_tb.bump(16 * ndma)

    return s_tb, s_tb.n


def emit_ffn(p: Prog, d, aux, x_sb, deps, tag, upto="full", dbg=None):
    BIG = 1.0e9
    NM_E = F_EXP // 128

    tT = p.sb(f"ffn_tT_{tag}", [128, NDC, TOK], BF16)
    wrows = p.sb(f"ffn_wr_{tag}", [1, N_EXP, TOK], BF16)
    s_tb = p.sem("ffn_tb")
    s_tt = p.sem("ffn_tt")
    s_rt = p.sem("ffn_rt")

    with p.scope():
        s_tbl, tbv = emit_ln_dram(p, x_sb, d[f"trb_{tag}"], d[f"trb_lo_{tag}"],
                                  deps, f"{tag}")

        def tbfwd(e, tbv=tbv):
            e.wait_ge(s_tbl.h, tbv)
            e.sem_inc(s_tb.h, 1)
        p.on("scalar", tbfwd)
        tbv = s_tb.bump(1)
        p.barrier(engines=("vector", "scalar"))

    def treads(e, tbv=tbv):
        e.wait_ge(s_tb.h, tbv)
        for j in range(NDC):
            e.dma_start_transpose(tT[:, j, :],
                                  d[f"trb_{tag}"][:, j * 128:(j + 1) * 128]).then_inc(s_tt.h, 16)
    p.on("sync", treads)
    ttv = s_tt.bump(16 * NDC)

    with p.scope():
        tT_lo = p.sb(f"ffn_tTlo_{tag}", [128, NDC, TOK], BF16)

        def treads_lo(e, tbv=tbv):
            e.wait_ge(s_tb.h, tbv)
            for j in range(NDC):
                e.dma_start_transpose(tT_lo[:, j, :],
                                      d[f"trb_lo_{tag}"][:, j * 128:(j + 1) * 128]).then_inc(s_tt.h, 16)
        p.on("sync", treads_lo)
        ttv_lo = s_tt.bump(16 * NDC)

        gw_f = p.sb(f"ffn_gwf_{tag}", [128, NDC, 8], F32)
        gw_hi = p.sb(f"ffn_gwh_{tag}", [128, NDC, 8], BF16)
        gw_lo = p.sb(f"ffn_gwl_{tag}", [128, NDC, 8], BF16)

        def gwload(e):
            e.dma_start(out=gw_f[:], in_=d["gwT"].rearrange("(a p) n -> p a n", p=128)).then_inc(s_rt.h, 16)
        p.on("sync", gwload)
        rt_gw = s_rt.bump(16)

        def gwsplit(e, rt_gw=rt_gw):
            e.wait_ge(s_rt.h, rt_gw)
            e.tensor_copy(gw_hi[:], gw_f[:])
            e.drain()
            e.tensor_tensor(gw_lo[:], gw_f[:], gw_hi[:], ALU.subtract).then_inc(s_rt.h, 1)
        p.on("vector", gwsplit)
        rt_sp = s_rt.bump()

        ps_lg = p.ps(f"ffn_pslg_{tag}", [128, 8])
        ps_wr = p.ps(f"ffn_pswr_{tag}", [16, 128], BF16)
        lg = p.sb(f"ffn_lg_{tag}", [128, 8], F32)
        elg = p.sb(f"ffn_elg_{tag}", [128, 8], F32)
        ssum = p.sb(f"ffn_ssum_{tag}", [128, 1], F32)
        sc = p.sb(f"ffn_sc_{tag}", [128, 8], F32)
        gs = p.sb(f"ffn_gs_{tag}", [128, 4], F32)
        gx = p.sb(f"ffn_gx_{tag}", [128, 4], F32)
        mcol = p.sb(f"ffn_mcol_{tag}", [128, 1], F32)
        sm8 = p.sb(f"ffn_sm8_{tag}", [128, 8], F32)
        msk = p.sb(f"ffn_msk_{tag}", [128, 8], F32)
        sel = p.sb(f"ffn_sel_{tag}", [128, 8], F32)
        wm = p.sb(f"ffn_wm_{tag}", [128, 8], F32)
        wm_pad = p.sb(f"ffn_wmpad_{tag}", [128, 128], BF16)
        s_lg = p.sem("ffn_lg")
        s_wr = p.sem("ffn_wrow")

        for tt in range(NTT):
            def mml(e, tt=tt, first=(tt == 0), rt_sp=rt_sp, ttv=ttv, ttv_lo=ttv_lo,
                    lg_done=s_lg.n):
                if first:
                    e.wait_ge(s_rt.h, rt_sp)
                    e.wait_ge(s_tt.h, ttv_lo)
                else:
                    e.wait_ge(s_lg.h, lg_done)
                ins = None
                for j in range(NDC):
                    sl = tT[:, j, tt * 128:(tt + 1) * 128]
                    sl_lo = tT_lo[:, j, tt * 128:(tt + 1) * 128]
                    ins = e.matmul(ps_lg[:], sl, gw_hi[:, j, :], start=(j == 0), stop=False)
                    ins = e.matmul(ps_lg[:], sl, gw_lo[:, j, :], start=False, stop=False)
                    ins = e.matmul(ps_lg[:], sl_lo, gw_hi[:, j, :], start=False,
                                   stop=(j == NDC - 1))
                ins.then_inc(s_lg.h, 1)
            p.on("tensor", mml)
            lg_v = s_lg.bump()

            def route(e, lg_v=lg_v):
                e.wait_ge(s_lg.h, lg_v)
                e.tensor_copy(lg[:], ps_lg[:])
                e.drain()
                e.engine_nop().then_inc(s_lg.h, 1)
            p.on("vector", route)
            lg_cp = s_lg.bump()

            def act_exp(e, lg_cp=lg_cp):
                e.wait_ge(s_lg.h, lg_cp)
                e.activation(elg[:], lg[:], AF.Exp, accum_out=ssum[:]).then_inc(s_rt.h, 1)
            p.on("scalar", act_exp)
            rt_e = s_rt.bump()

            def route2(e, rt_e=rt_e):
                e.wait_ge(s_rt.h, rt_e)
                e.reciprocal(ssum[:], ssum[:])
                e.drain()
                e.tensor_scalar_mul(sc[:], elg[:], ssum[:])
                e.drain()
                e.tensor_tensor(gs[:], sc[:, 0:8:2], sc[:, 1:8:2], ALU.max)
                e.drain()
                e.reduce_max(mcol[:], gs[:], axis=AX.X)
                e.drain()
                e.tensor_scalar(gx[:], gs[:], mcol[:], None, ALU.is_ge)
                e.drain()
                e.scalar_tensor_tensor(gx[:], gx[:], -BIG, gs[:], ALU.mult, ALU.add)
                e.drain()
                e.reduce_max(mcol[:], gx[:], axis=AX.X)
                e.drain()
                e.tensor_scalar(gx[:], gs[:], mcol[:], None, ALU.is_ge)
                e.drain()
                e.tensor_copy(sm8[:, 0:8:2], gx[:])
                e.drain()
                e.tensor_copy(sm8[:, 1:8:2], gx[:])
                e.drain()
                e.tensor_tensor(msk[:], sc[:], sm8[:], ALU.mult)
                e.drain()
                e.reduce_max(mcol[:], msk[:], axis=AX.X)
                e.drain()
                e.tensor_scalar(sel[:], msk[:], mcol[:], None, ALU.is_ge)
                e.drain()
                e.scalar_tensor_tensor(sel[:], sel[:], -BIG, msk[:], ALU.mult, ALU.add)
                e.drain()
                e.reduce_max(mcol[:], sel[:], axis=AX.X)
                e.drain()
                e.tensor_scalar(sel[:], msk[:], mcol[:], None, ALU.is_ge)
                e.drain()
                e.tensor_tensor(wm[:], sel[:], msk[:], ALU.mult)
                e.drain()
                e.memset(wm_pad[:], 0.0)
                e.drain()
                e.tensor_copy(wm_pad[:, 0:128:16], wm[:]).then_inc(s_rt.h, 1)
            p.on("vector", route2)
            rt_w = s_rt.bump()

            if upto == "logits":
                continue
            for n in range(N_EXP):
                def wrt(e, n=n, rt_w=rt_w, wr_prev=s_wr.n):
                    e.wait_ge(s_rt.h, rt_w)
                    if wr_prev > 0:
                        e.wait_ge(s_wr.h, wr_prev)
                    e.transpose(ps_wr[:], wm_pad[:, 16 * n:16 * (n + 1)],
                                aux["ident"][:]).then_inc(s_rt.h, 1)
                p.on("tensor", wrt)
                rt_t = s_rt.bump()

                def wre(e, n=n, tt=tt, rt_t=rt_t):
                    e.wait_ge(s_rt.h, rt_t)
                    e.tensor_copy(wrows[0:1, n, tt * 128:(tt + 1) * 128],
                                  ps_wr[0:1, :]).then_inc(s_wr.h, 1)
                p.on("vector", wre)
                s_wr.bump()
            if dbg is not None and "wm_to_x" in dbg:
                def wmdump(e, tt=tt, rt_w=rt_w):
                    e.wait_ge(s_rt.h, rt_w)
                    e.tensor_copy(x_sb[:, tt, 0:8], wm[:])
                    e.drain()
                    e.tensor_copy(x_sb[:, tt, 8:16], sc[:])
                    e.drain()
                    e.tensor_copy(x_sb[:, tt, 16:20], gs[:])
                    e.drain()
                    e.tensor_copy(x_sb[:, tt, 20:24], gx[:])
                    e.drain()
                    e.tensor_copy(x_sb[:, tt, 24:32], sm8[:]).then_inc(s_rt.h, 1)
                p.on("vector", wmdump)
                s_rt.bump(1)
        wr_all = s_wr.n
        if upto in ("router", "logits"):
            return s_rt, s_rt.n
        bar_b = p.barrier(engines=("tensor", "vector", "scalar"))

    with p.scope():
        NRING = 2
        wg_ring = [p.sb(f"ffn_wg_{tag}_{i}", [128, NDC, 128], BF16) for i in range(NRING)]
        wu_ring = [p.sb(f"ffn_wu_{tag}_{i}", [128, NDC, 128], BF16) for i in range(NRING)]
        wd_sb = p.sb(f"ffn_wd_{tag}", [128, NM_E, D], BF16)
        act_sb = p.sb(f"ffn_act_{tag}", [128, NM_E, TOK], BF16)
        bc_bf = p.sb(f"ffn_bc_{tag}", [128, TOK], BF16)
        sg_bf = [p.sb(f"ffn_sg_{tag}_{i}", [128, TOK], BF16) for i in range(2)]
        a1 = p.sb(f"ffn_a1_{tag}", [128, TOK], BF16)
        ps_g = [p.ps(f"ffn_psg_{tag}_{i}", [128, TOK]) for i in range(2)]
        ps_u = [p.ps(f"ffn_psu_{tag}_{i}", [128, TOK]) for i in range(2)]

        s_wlp = [p.sem("ffn_wl0"), p.sem("ffn_wl1")]
        s_mf = p.sem("ffn_mf")
        s_ae = p.sem("ffn_ae")
        s_av = p.sem("ffn_av")
        s_bc = p.sem("ffn_bc")
        s_dn = p.sem("ffn_dn")
        s_xa = p.sem("ffn_xa")
        s_dl = p.sem("ffn_dl")
        base_wlp = [s_wlp[0].n, s_wlp[1].n]
        base_mf, base_dn, base_dl = s_mf.n, s_dn.n, s_dl.n

        blocks = [("e", n) for n in range(N_EXP)] + [("s", h) for h in range(2)]
        SKIP_DOWN = False

        mseq = 0
        for bi, (kind, n) in enumerate(blocks):
            for m in range(NM_E):
                for (ring, wsel) in ((wg_ring, "g"), (wu_ring, "u")):
                    if kind == "e":
                        wtns = d["gate_wT"] if wsel == "g" else d["up_wT"]
                        src = wtns[n, :, m * 128:(m + 1) * 128]
                    else:
                        wtns = d["sh_gate_wT"] if wsel == "g" else d["sh_up_wT"]
                        c0 = n * F_EXP + m * 128
                        src = wtns[:, c0:c0 + 128]
                    src = src.rearrange("(a p) c -> p a c", p=128)

                    def wl(e, src=src, ring=ring, mseq=mseq, bar_b=bar_b,
                           sw=s_wlp[mseq % 2], need=base_mf + max(0, mseq - 1)):
                        e.wait_ge(bar_b[0].h, bar_b[1])
                        if need > base_mf:
                            e.wait_ge(s_mf.h, need)
                        e.dma_start(out=ring[mseq % NRING][:], in_=src).then_inc(sw.h, 16)
                    p.on("gpsimd", wl)
                    s_wlp[mseq % 2].bump(16)
                mseq += 1
            for k in range(NM_E):
                if kind == "e":
                    dsrc = d["down_wT"][n, k * 128:(k + 1) * 128, :]
                else:
                    r0 = n * F_EXP + k * 128
                    dsrc = d["sh_down_wT"][r0:r0 + 128, :]

                def dl(e, dsrc=dsrc, k=k, bi=bi, bar_b=bar_b,
                       dn_need=base_dn + bi * NTT):
                    e.wait_ge(bar_b[0].h, bar_b[1])
                    if bi > 0:
                        e.wait_ge(s_dn.h, dn_need)
                    e.dma_start(out=wd_sb[:, k, :], in_=dsrc).then_inc(s_dl.h, 16)
                p.on("gpsimd", dl)
                s_dl.bump(16)

        mseq = 0
        for bi, (kind, n) in enumerate(blocks):
            routed = (kind == "e")
            if routed:
                def bcast(e, n=n, wr_all=wr_all, bar_b=bar_b, bi=bi,
                          av_need=s_av.n, xa_need=s_xa.n):
                    e.wait_ge(s_wr.h, wr_all)
                    e.wait_ge(bar_b[0].h, bar_b[1])
                    if bi > 0:
                        e.wait_ge(s_av.h, av_need)
                        e.wait_ge(s_xa.h, xa_need)
                    for nck in range(TOK // 512):
                        e.matmul(ps_g[0][:, nck * 512:(nck + 1) * 512], aux["ones1_bf"][:],
                                 wrows[0:1, n, nck * 512:(nck + 1) * 512],
                                 start=True, stop=True).then_inc(s_bc.h, 1)
                p.on("tensor", bcast)
                bc_f = s_bc.bump(2)

                def bccp(e, bc_f=bc_f):
                    e.wait_ge(s_bc.h, bc_f)
                    e.activation(bc_bf[:], ps_g[0][:], AF.Copy).then_inc(s_bc.h, 1)
                p.on("scalar", bccp)
                bc_c = s_bc.bump()
            else:
                bc_c = s_bc.n

            for m in range(NM_E):
                mb = mseq % 2
                wg = wg_ring[mseq % NRING]
                wu = wu_ring[mseq % NRING]

                def mmg(e, mb=mb, wg=wg, wu=wu, sw=s_wlp[mseq % 2],
                        wl_need=base_wlp[mseq % 2] + 32 * (mseq // 2 + 1),
                        bc_c=bc_c, m=m, routed=routed, bi=bi,
                        av_need=max(0, s_av.n - 1), xa_need=s_xa.n if m <= 1 and bi > 0 else 0):
                    e.wait_ge(sw.h, wl_need)
                    if m == 0 and routed:
                        e.wait_ge(s_bc.h, bc_c)
                    if xa_need > 0:
                        e.wait_ge(s_xa.h, xa_need)
                    e.wait_ge(s_av.h, av_need)
                    ins = None
                    for j in range(NDC):
                        for nck in range(TOK // 512):
                            ins = e.matmul(ps_g[mb][:, nck * 512:(nck + 1) * 512],
                                           wg[:, j, :], tT[:, j, nck * 512:(nck + 1) * 512],
                                           start=(j == 0), stop=(j == NDC - 1))
                    for j in range(NDC):
                        for nck in range(TOK // 512):
                            ins = e.matmul(ps_u[mb][:, nck * 512:(nck + 1) * 512],
                                           wu[:, j, :], tT[:, j, nck * 512:(nck + 1) * 512],
                                           start=(j == 0), stop=(j == NDC - 1))
                    ins.then_inc(s_mf.h, 1)
                p.on("tensor", mmg)
                mf_v = s_mf.bump()

                def silu(e, mb=mb, mf_v=mf_v):
                    e.wait_ge(s_mf.h, mf_v)
                    e.activation(sg_bf[mb][:], ps_g[mb][:], AF.Silu).then_inc(s_ae.h, 1)
                p.on("scalar", silu)
                ae_v = s_ae.bump()

                def acte(e, m=m, mb=mb, ae_v=ae_v, routed=routed,
                         dn_need=base_dn + bi * NTT):
                    e.wait_ge(s_ae.h, ae_v)
                    if dn_need > base_dn:
                        e.wait_ge(s_dn.h, dn_need)
                    if routed:
                        e.tensor_tensor(a1[:], sg_bf[mb][:], ps_u[mb][:], ALU.mult)
                        e.drain()
                        e.tensor_tensor(act_sb[:, m, :], a1[:], bc_bf[:],
                                        ALU.mult).then_inc(s_av.h, 1)
                    else:
                        e.tensor_tensor(act_sb[:, m, :], sg_bf[mb][:], ps_u[mb][:],
                                        ALU.mult).then_inc(s_av.h, 1)
                p.on("vector", acte)
                s_av.bump()
                mseq += 1

            for tt in range(NTT if not SKIP_DOWN else 0):
                def mmd(e, tt=tt, av_all=s_av.n, dl_need=base_dl + 16 * NM_E * (bi + 1),
                        xa_need=s_xa.n):
                    e.wait_ge(s_av.h, av_all)
                    e.wait_ge(s_dl.h, dl_need)
                    if xa_need > 0:
                        e.wait_ge(s_xa.h, xa_need)
                    ins = None
                    if True:
                        for k in range(NM_E):
                            for nch in range(4):
                                bank = ps_g[nch // 2]
                                tgt = bank[:, (nch % 2) * 512:(nch % 2 + 1) * 512]
                                ins = e.matmul(tgt, act_sb[:, k, tt * 128:(tt + 1) * 128],
                                               wd_sb[:, k, nch * 512:(nch + 1) * 512],
                                               start=(k == 0), stop=(k == NM_E - 1))
                    ins.then_inc(s_dn.h, 1)
                p.on("tensor", mmd)
                dn_v = s_dn.bump()

                def xadd(e, tt=tt, dn_v=dn_v):
                    e.wait_ge(s_dn.h, dn_v)
                    e.tensor_tensor(x_sb[:, tt, 0:TOK], x_sb[:, tt, 0:TOK],
                                    ps_g[0][:], ALU.add)
                    e.drain()
                    e.tensor_tensor(x_sb[:, tt, TOK:D], x_sb[:, tt, TOK:D],
                                    ps_g[1][:], ALU.add).then_inc(s_xa.h, 1)
                p.on("vector", xadd)
                s_xa.bump()

    return s_xa, s_xa.n



def emit_delta_pair(p: Prog, heads, aux, deps, tag):
    p.constv(ATT_SCALE)
    GRP = 8
    NG = NCH // GRP
    GW = GRP * 128
    NH = len(heads)

    s_tf = p.sem("dl_tf")
    s_af = p.sem("dl_af")
    s_ae = p.sem("dl_ae")
    s_aa = p.sem("dl_aa")
    s_bf = p.sem("dl_bf")
    s_be = p.sem("dl_be")
    base_be0 = s_be.n

    ctxs = []
    for h in range(NH):
        c = dict(heads[h])
        c["qh_g"] = [p.sb(f"dl_qh_{tag}{h}_{b}", [128, GW], BF16) for b in range(2)]
        c["kt_g"] = [p.sb(f"dl_kt_{tag}{h}_{b}", [128, GW], BF16) for b in range(2)]
        c["kb_g"] = [p.sb(f"dl_kb_{tag}{h}_{b}", [128, GW], BF16) for b in range(2)]
        c["vb_g"] = [p.sb(f"dl_vb_{tag}{h}_{b}", [128, GW], BF16) for b in range(2)]
        c["vb_tm"] = p.sb(f"dl_vbtm_{tag}{h}", [128, 128], BF16)
        c["kb_tm"] = p.sb(f"dl_kbtm_{tag}{h}", [128, 128], BF16)
        c["Lm"] = p.sb(f"dl_L_{tag}{h}", [128, 128], BF16)
        c["LTm"] = p.sb(f"dl_LT_{tag}{h}", [128, 128], BF16)
        c["Pw"] = [p.sb(f"dl_pw_{tag}{h}_{i}", [128, 128], BF16) for i in range(5)]
        c["PwT"] = [p.sb(f"dl_pwt_{tag}{h}_{i}", [128, 128], BF16) for i in range(2)]
        c["Y"] = p.sb(f"dl_Y_{tag}{h}", [128, 128], BF16)
        c["TT"] = p.sb(f"dl_TT_{tag}{h}", [128, 128], BF16)
        c["u_sb"] = [p.sb(f"dl_u_{tag}{h}_{b}", [128, 128], BF16) for b in range(2)]
        c["wT_sb"] = [p.sb(f"dl_w_{tag}{h}_{b}", [128, 128], BF16) for b in range(2)]
        c["at_sb"] = [p.sb(f"dl_at_{tag}{h}_{b}", [128, 128], BF16) for b in range(2)]
        c["ktm_sb"] = [p.sb(f"dl_ktm_{tag}{h}_{b}", [128, 128], BF16) for b in range(2)]
        c["u2"] = p.sb(f"dl_u2_{tag}{h}", [128, 128], BF16)
        c["oc"] = p.sb(f"dl_oc_{tag}{h}", [128, 128], BF16)
        c["S_f"] = p.sb(f"dl_Sf_{tag}{h}", [128, 128], F32)
        c["S_b"] = p.sb(f"dl_Sb_{tag}{h}", [128, 128], BF16)
        c["ps_a"] = p.ps(f"dl_psa_{tag}{h}", [128, 256])
        c["ps_t"] = p.ps(f"dl_pst_{tag}{h}", [128, 128], BF16)
        c["ps_b"] = p.ps(f"dl_psb_{tag}{h}", [128, 256])
        c["grp_done"] = {}
        c["a_done"] = {}
        c["ae_own"] = 0
        c["aa_own"] = 0
        c["sb_ready"] = 0
        ctxs.append(c)
    eq = p.sb(f"dl_eq_{tag}", [128, GW], BF16)
    tmpg = p.sb(f"dl_tmpg_{tag}", [128, GW], F32)
    den = p.sb(f"dl_den_{tag}", [1, GW], F32)
    akg = p.sb(f"dl_ak_{tag}", [128, GW], BF16)
    e2g = p.sb(f"dl_e2_{tag}", [128, GW], F32)
    mng = p.sb(f"dl_mn_{tag}", [128, GW], F32)
    gcg = p.sb(f"dl_gc_{tag}", [128, GW], BF16)
    ps_r = p.ps(f"dl_psr_{tag}", [128, 512])

    def sinit(e, cs=tuple(c["S_f"] for c in ctxs) + tuple(c["S_b"] for c in ctxs)):
        for (sm, v) in deps:
            e.wait_ge(sm.h, v)
        for t in cs:
            e.memset(t[:], 0.0)
            e.drain()
        e.engine_nop().then_inc(s_be.h, 1)
    p.on("vector", sinit)
    s_be.bump()
    for c_ in ctxs:
        c_["sb_ready"] = s_be.n

    def emit_group(hc, g):
        b = g % 2
        qT, kT, vT = hc["qT"], hc["kT"], hc["vT"]
        gsl = slice(g * GW, (g + 1) * GW)
        war = base_be0 + 1 + max(0, (NH * 3 * GRP) * (g - 1))

        def t_act1(e, g=g, gsl=gsl, war=war, prev_grp=s_tf.n, qT=qT):
            if g == 0:
                for (sm, v) in deps:
                    e.wait_ge(sm.h, v)
            e.wait_ge(s_tf.h, prev_grp)
            e.wait_ge(s_be.h, war)
            e.activation(eq[:], qT[:, gsl], AF.Exp).then_inc(s_tf.h, 1)
        p.on("scalar", t_act1)
        v1 = s_tf.bump()

        def softmax_pipe(dst, scale_after, v_ready):
            vr = v_ready
            for ncl in range(2):
                def pe_row(e, ncl=ncl, vr=vr):
                    e.wait_ge(s_tf.h, vr)
                    e.matmul(ps_r[0:1, 0:512], aux["ones_bf"][:],
                             eq[:, ncl * 512:(ncl + 1) * 512],
                             start=True, stop=True).then_inc(s_tf.h, 1)
                p.on("tensor", pe_row)
                v_r = s_tf.bump()

                def dv_rec(e, ncl=ncl, v_r=v_r):
                    e.wait_ge(s_tf.h, v_r)
                    e.tensor_copy(den[0:1, ncl * 512:(ncl + 1) * 512], ps_r[0:1, 0:512])
                    e.drain()
                    e.reciprocal(den[0:1, ncl * 512:(ncl + 1) * 512],
                                 den[0:1, ncl * 512:(ncl + 1) * 512])
                    e.drain()
                    e.engine_nop().then_inc(s_tf.h, 1)
                p.on("vector", dv_rec)
                v_c = s_tf.bump()

                def pe_bc(e, ncl=ncl, v_c=v_c):
                    e.wait_ge(s_tf.h, v_c)
                    e.matmul(ps_r[:, 0:512], aux["ones1_f"][:],
                             den[0:1, ncl * 512:(ncl + 1) * 512],
                             start=True, stop=True).then_inc(s_tf.h, 1)
                p.on("tensor", pe_bc)
                v_b = s_tf.bump()

                def dv_fin(e, dst=dst, ncl=ncl, v_b=v_b, scale_after=scale_after):
                    e.wait_ge(s_tf.h, v_b)
                    csl = slice(ncl * 512, (ncl + 1) * 512)
                    e.tensor_tensor(tmpg[:, csl], eq[:, csl], ps_r[:, 0:512], ALU.mult)
                    e.drain()
                    if scale_after is None:
                        e.tensor_scalar(dst[:, csl], tmpg[:, csl], 1e-6, 1.0 - 1e-6,
                                        ALU.max, ALU.min).then_inc(s_tf.h, 1)
                    else:
                        e.tensor_scalar(tmpg[:, csl], tmpg[:, csl], 1e-6, 1.0 - 1e-6,
                                        ALU.max, ALU.min)
                        e.drain()
                        e.tensor_scalar_mul(dst[:, csl], tmpg[:, csl],
                                            float(scale_after)).then_inc(s_tf.h, 1)
                p.on("vector", dv_fin)
                vr = s_tf.bump()
            return vr

        vq = softmax_pipe(hc["qh_g"][b], ATT_SCALE, v1)

        def t_act2(e, gsl=gsl, vq=vq, kT=kT):
            e.wait_ge(s_tf.h, vq)
            e.activation(eq[:], kT[:, gsl], AF.Exp).then_inc(s_tf.h, 1)
        p.on("scalar", t_act2)
        v2 = s_tf.bump()
        vk = softmax_pipe(hc["kt_g"][b], None, v2)

        def t_gate_a(e, gsl=gsl, vk=vk, kT=kT):
            e.wait_ge(s_tf.h, vk)
            e.activation(akg[:], kT[:, gsl], AF.Abs)
            e.drain()
            e.activation(e2g[:], akg[:], AF.Exp, scale=-1.0)
            e.drain()
            e.activation(e2g[:], e2g[:], AF.Ln, bias=1.0).then_inc(s_tf.h, 1)
        p.on("scalar", t_gate_a)
        v4 = s_tf.bump()

        def t_gate_d2(e, gsl=gsl, b=b, v4=v4, hc=hc, vT=vT, kT=kT):
            e.wait_ge(s_tf.h, v4)
            e.tensor_scalar(mng[:], kT[:, gsl], 0.0, None, ALU.min)
            e.drain()
            e.tensor_tensor(mng[:], mng[:], e2g[:], ALU.subtract)
            e.drain()
            e.tensor_scalar(mng[:], mng[:], 1.0 / 16.0, -16.0, ALU.mult, ALU.max)
            e.drain()
            e.tensor_scalar(gcg[:], mng[:], 16.0, None, ALU.min)
            e.drain()
            e.tensor_tensor(hc["kb_g"][b][:], hc["kt_g"][b][:], gcg[:], ALU.mult)
            e.drain()
            e.tensor_tensor(hc["vb_g"][b][:], vT[:, gsl], gcg[:], ALU.mult).then_inc(s_tf.h, 1)
        p.on("vector", t_gate_d2)
        return s_tf.bump()

    def wait_evacs(e, ae_n, aa_n):
        e.wait_ge(s_ae.h, ae_n)
        e.wait_ge(s_aa.h, aa_n)

    def gen_A(hc, c):
        g = c // GRP
        b2 = c % 2
        bg = g % 2
        lsl = slice((c % GRP) * 128, (c % GRP + 1) * 128)
        qh, kt_, kb, vb = (hc["qh_g"][bg], hc["kt_g"][bg], hc["kb_g"][bg],
                          hc["vb_g"][bg])
        ps_a, ps_t = hc["ps_a"], hc["ps_t"]
        gd = hc["grp_done"][g]
        be_war = s_be.n

        for (srcbuf, dst) in ((vb, hc["vb_tm"]), (kb, hc["kb_tm"]),
                              (kt_, hc["ktm_sb"][b2])):
            def tp_pe(e, srcbuf=srcbuf, lsl=lsl, gd=gd, be_war=be_war,
                      ae_n=hc["ae_own"], aa_n=hc["aa_own"], first=(srcbuf is vb)):
                if first:
                    e.wait_ge(s_tf.h, gd)
                    if be_war > base_be0 + 1:
                        e.wait_ge(s_be.h, be_war)
                wait_evacs(e, ae_n, aa_n)
                e.transpose(ps_t[:], srcbuf[:, lsl], aux["ident"][:]).then_inc(s_af.h, 1)
            p.on("tensor", tp_pe)
            af_v = s_af.bump()

            def tp_cp(e, ps_t=ps_t, dst=dst, af_v=af_v):
                e.wait_ge(s_af.h, af_v)
                e.copy(dst[:], ps_t[:]).then_inc(s_aa.h, 1)
            p.on("scalar", tp_cp)
            hc["aa_own"] = s_aa.bump()
            yield

        def llt_pe(e, kb=kb, kt_=kt_, lsl=lsl, ae_n=hc["ae_own"], aa_n=hc["aa_own"]):
            wait_evacs(e, ae_n, aa_n)
            e.matmul(ps_a[:, 0:128], kb[:, lsl], kt_[:, lsl], start=True, stop=True)
            e.matmul(ps_a[:, 128:256], kt_[:, lsl], kb[:, lsl], start=True,
                     stop=True).then_inc(s_af.h, 1)
        p.on("tensor", llt_pe)
        af_v = s_af.bump()

        def llt_dv(e, hc=hc, af_v=af_v):
            e.wait_ge(s_af.h, af_v)
            e.tensor_tensor(hc["Lm"][:], ps_a[:, 0:128], aux["m_sl"][:], ALU.mult)
            e.tensor_tensor(hc["LTm"][:], ps_a[:, 128:256], aux["m_su"][:],
                            ALU.mult).then_inc(s_ae.h, 1)
        p.on("vector", llt_dv)
        hc["ae_own"] = s_ae.bump()
        yield

        def at_pe(e, kt_=kt_, qh=qh, lsl=lsl, ae_n=hc["ae_own"], aa_n=hc["aa_own"]):
            wait_evacs(e, ae_n, aa_n)
            e.matmul(ps_a[:, 0:128], kt_[:, lsl], qh[:, lsl], start=True,
                     stop=True).then_inc(s_af.h, 1)
        p.on("tensor", at_pe)
        af_v = s_af.bump()

        def at_dv(e, hc=hc, b2=b2, af_v=af_v):
            e.wait_ge(s_af.h, af_v)
            e.tensor_tensor(hc["at_sb"][b2][:], ps_a[:, 0:128], aux["m_iu"][:],
                            ALU.mult).then_inc(s_ae.h, 1)
        p.on("vector", at_dv)
        hc["ae_own"] = s_ae.bump()
        yield

        for lev in range(6):
            curL = hc["Lm"] if lev == 0 else hc["Pw"][lev - 1]
            curLT = hc["LTm"] if lev == 0 else hc["PwT"][(lev - 1) % 2]
            last = (lev == 5)

            def p_pe(e, curL=curL, curLT=curLT, last=last, ae_n=hc["ae_own"],
                     aa_n=hc["aa_own"]):
                wait_evacs(e, ae_n, aa_n)
                if not last:
                    e.matmul(ps_a[:, 0:128], curLT[:], curL[:], start=True, stop=True)
                e.matmul(ps_a[:, 128:256], curL[:], curLT[:], start=True,
                         stop=True).then_inc(s_af.h, 1)
            p.on("tensor", p_pe)
            af_v = s_af.bump()

            def p_cp(e, hc=hc, lev=lev, last=last, af_v=af_v):
                e.wait_ge(s_af.h, af_v)
                if not last:
                    e.copy(hc["Pw"][lev][:], ps_a[:, 0:128])
                e.copy(hc["PwT"][lev % 2][:], ps_a[:, 128:256]).then_inc(s_aa.h, 1)
            p.on("scalar", p_cp)
            hc["aa_own"] = s_aa.bump()
            yield

        def y_init(e, hc=hc, ae_n=hc["ae_own"], aa_n=hc["aa_own"]):
            wait_evacs(e, ae_n, aa_n)
            e.tensor_tensor(hc["Y"][:], hc["PwT"][1][:], aux["eye_bf"][:],
                            ALU.add).then_inc(s_ae.h, 1)
        p.on("vector", y_init)
        hc["ae_own"] = s_ae.bump()
        yield

        prod_lhs = [hc["Pw"][4], hc["Pw"][3], hc["Pw"][2], hc["Pw"][1],
                    hc["Pw"][0], hc["Lm"]]
        for pi, lhs in enumerate(prod_lhs):
            lastp = (pi == 5)

            def pr_pe(e, lhs=lhs, hc=hc, ae_n=hc["ae_own"], aa_n=hc["aa_own"]):
                wait_evacs(e, ae_n, aa_n)
                e.matmul(ps_a[:, 0:128], lhs[:], hc["Y"][:], start=True,
                         stop=True).then_inc(s_af.h, 1)
            p.on("tensor", pr_pe)
            af_v = s_af.bump()

            def pr_dv(e, hc=hc, lastp=lastp, af_v=af_v):
                e.wait_ge(s_af.h, af_v)
                if lastp:
                    e.tensor_tensor(hc["TT"][:], hc["Y"][:], ps_a[:, 0:128],
                                    ALU.subtract).then_inc(s_ae.h, 1)
                else:
                    e.tensor_tensor(hc["Y"][:], hc["Y"][:], ps_a[:, 0:128],
                                    ALU.add).then_inc(s_ae.h, 1)
            p.on("vector", pr_dv)
            hc["ae_own"] = s_ae.bump()
            yield

        def uw_pe(e, hc=hc, ae_n=hc["ae_own"], aa_n=hc["aa_own"]):
            wait_evacs(e, ae_n, aa_n)
            e.matmul(ps_a[:, 0:128], hc["TT"][:], hc["vb_tm"][:], start=True, stop=True)
            e.matmul(ps_a[:, 128:256], hc["kb_tm"][:], hc["TT"][:], start=True,
                     stop=True).then_inc(s_af.h, 1)
        p.on("tensor", uw_pe)
        af_v = s_af.bump()

        def uw_cp(e, hc=hc, b2=b2, af_v=af_v):
            e.wait_ge(s_af.h, af_v)
            e.copy(hc["u_sb"][b2][:], ps_a[:, 0:128])
            e.copy(hc["wT_sb"][b2][:], ps_a[:, 128:256]).then_inc(s_aa.h, 1)
        p.on("scalar", uw_cp)
        hc["aa_own"] = s_aa.bump()
        hc["a_done"][c] = (s_ae.n, s_aa.n)
        yield

    def gen_B(hc, c):
        b2 = c % 2
        csl = slice(c * 128, (c + 1) * 128)
        lsl = slice((c % GRP) * 128, (c % GRP + 1) * 128)
        g = c // GRP
        qh = hc["qh_g"][g % 2]
        ps_b2, ps_t = hc["ps_b"], hc["ps_t"]
        a_done = hc["a_done"][c]

        def b_pe1(e, hc=hc, b2=b2, a_done=a_done, be_n=hc["sb_ready"]):
            wait_evacs(e, a_done[0], a_done[1])
            e.wait_ge(s_be.h, be_n)
            e.matmul(ps_b2[:, 0:128], hc["wT_sb"][b2][:], hc["S_b"][:], start=True,
                     stop=True).then_inc(s_bf.h, 1)
        p.on("tensor", b_pe1)
        bf1 = s_bf.bump()

        def b_dv1(e, hc=hc, b2=b2, bf1=bf1):
            e.wait_ge(s_bf.h, bf1)
            e.tensor_tensor(hc["u2"][:], hc["u_sb"][b2][:], ps_b2[:, 0:128],
                            ALU.subtract).then_inc(s_be.h, 1)
        p.on("vector", b_dv1)
        be1 = s_be.bump()
        yield

        def b_pe2(e, hc=hc, b2=b2, lsl=lsl, qh=qh, be1=be1):
            e.wait_ge(s_be.h, be1)
            e.matmul(ps_b2[:, 128:256], qh[:, lsl], hc["S_b"][:], start=True, stop=False)
            e.matmul(ps_b2[:, 128:256], hc["at_sb"][b2][:], hc["u2"][:], start=False,
                     stop=True)
            e.matmul(ps_b2[:, 0:128], hc["ktm_sb"][b2][:], hc["u2"][:], start=True,
                     stop=True).then_inc(s_bf.h, 1)
        p.on("tensor", b_pe2)
        bf2 = s_bf.bump()

        def b_dv2(e, hc=hc, bf2=bf2):
            e.wait_ge(s_bf.h, bf2)
            e.tensor_copy(hc["oc"][:], ps_b2[:, 128:256])
            e.tensor_tensor(hc["S_f"][:], hc["S_f"][:], ps_b2[:, 0:128], ALU.add)
            e.drain()
            e.tensor_copy(hc["S_b"][:], hc["S_f"][:]).then_inc(s_be.h, 1)
        p.on("vector", b_dv2)
        be2 = s_be.bump()
        hc["sb_ready"] = be2
        yield

        def b_pe3(e, hc=hc, be2=be2, ae_n=hc["ae_own"], aa_n=hc["aa_own"]):
            e.wait_ge(s_be.h, be2)
            wait_evacs(e, ae_n, aa_n)
            e.transpose(ps_t[:], hc["oc"][:], aux["ident"][:]).then_inc(s_bf.h, 1)
        p.on("tensor", b_pe3)
        bf3 = s_bf.bump()

        def b_dv3(e, hc=hc, csl=csl, bf3=bf3):
            e.wait_ge(s_bf.h, bf3)
            e.tensor_copy(hc["olinT"][:, csl], ps_t[:]).then_inc(s_be.h, 1)
        p.on("vector", b_dv3)
        s_be.bump()
        yield

    def drive(gens):
        gens = [g for g in gens if g is not None]
        while gens:
            nxt = []
            for g in gens:
                try:
                    next(g)
                    nxt.append(g)
                except StopIteration:
                    pass
            gens = nxt

    for hc in ctxs:
        hc["grp_done"][0] = emit_group(hc, 0)
    drive([gen_A(hc, 0) for hc in ctxs])
    for c in range(NCH):
        if c + 1 < NCH:
            g_next = (c + 1) // GRP
            for hc in ctxs:
                if g_next not in hc["grp_done"]:
                    hc["grp_done"][g_next] = emit_group(hc, g_next)
            drive([gen_A(hc, c + 1) for hc in ctxs] + [gen_B(hc, c) for hc in ctxs])
        else:
            drive([gen_B(hc, c) for hc in ctxs])

    return s_be, s_be.n



def emit_aux_load(p: Prog, dram_ap, shape, dtype, tag, deps=()):
    sb = p.sb(f"auxl_{tag}", shape, dtype)
    s = p.sem("auxl")

    def lf(e, deps=tuple(deps)):
        for (sm, v) in deps:
            e.wait_ge(sm.h, v)
        e.dma_start(out=sb[:], in_=dram_ap).then_inc(s.h, 16)
    p.on("sync", lf)
    return sb, (s, s.bump(16))


def emit_attn_a(p: Prog, d, aux, x_sb, xdep, tag="a"):
    with p.scope():
        qT = p.sb(f"qT_{tag}", [128, 2, S], BF16)
        kT = p.sb(f"kT_{tag}", [128, 1, S], BF16)
        vT = p.sb(f"vT_{tag}", [128, 1, S], BF16)

        with p.scope():
            t_bf = p.sb(f"t_{tag}", [128, NTT, D], BF16)
            s_ln, lnv = emit_ln(p, x_sb, t_bf, [xdep], f"{tag}")
            emit_allgather_tm(p, t_bf, d[f"cc_t_{tag}_in"], d[f"cc_t_{tag}"],
                              [(s_ln, lnv)], f"{tag}t")
            bar0 = p.barrier()

        with p.scope():
            specs = [
                (d[f"wq_sl_{tag}"], 0, 2, qT),
                (d[f"wk_sl_{tag}"], 0, 1, kT),
                (d[f"wv_sl_{tag}"], 0, 1, vT),
            ]
            emit_attn_proj(p, d[f"cc_t_{tag}"], specs, [bar0], f"{tag}")
            bar1 = p.barrier()

        with p.scope():
            v_tm = p.sb(f"vtm_{tag}", [128, NCH, 128], BF16)
            mask4, m4dep = emit_aux_load(p, d["mask4"], [128, 4, 512], BF16, f"m4{tag}",
                                         deps=[bar1])
            with p.scope():
                emit_transpose_fm_tm(p, vT[:, 0, :], v_tm, NCH, aux["ident"],
                                     [bar1], f"{tag}v")
                bar_tp = p.barrier(engines=("tensor", "vector"))
            s_mx = p.sem("mx")
            hdep = [bar1, bar_tp, m4dep]
            for h in range(2):
                with p.scope():
                    outT = p.sb(f"mixT_{tag}_{h}", [128, S], BF16)
                    s_fl, flv = emit_flash_head(p, qT[:, h, :], kT[:, 0, :], v_tm, outT,
                                                aux["ones_bf"], aux["ones1_f"], mask4,
                                                hdep, f"{tag}h{h}", ones_f=aux["ones_f"])

                    def mxo(e, h=h, outT=outT, flv=flv, s_fl=s_fl):
                        e.wait_ge(s_fl.h, flv)
                        for j in range(R):
                            e.dma_start(out=d[f"cc_mix_{tag}_in"][256 * j + 128 * h:256 * j + 128 * (h + 1), :],
                                        in_=outT[:, j * TOK:(j + 1) * TOK]).then_inc(s_mx.h, 16)
                    p.on("scalar", mxo)
                    s_mx.bump(16 * R)
                    bar_h = p.barrier(engines=("tensor", "scalar", "vector"))
                    hdep = [bar_h]

            def agm(e, mxv=s_mx.n):
                from concourse.aot_env import allow_ici_all_to_all
                e.wait_ge(s_mx.h, mxv)
                with allow_ici_all_to_all():
                    e.collective_compute(
                        "AllToAll", ALU.bypass,
                        replica_groups=[list(range(R))],
                        ins=[d[f"cc_mix_{tag}_in"]],
                        outs=[d[f"cc_mix_{tag}"]],
                    ).then_inc(s_mx.h, 1)
            p.on("gpsimd", agm)
            s_mx.bump(1)

        bar2 = p.barrier()

    with p.scope():
        s_res, resv = emit_out_proj(p, d[f"cc_mix_{tag}"], d[f"w_o_{tag}"], None,
                                    x_sb, [bar2], f"{tag}")
    return s_res, resv






def emit_ln_dram(p: Prog, x_sb, trb, trb_lo, deps, tag):
    p.constv(EPS_LN)
    p.constv(-0.5)
    sq = p.sb(f"lnd_sq_{tag}", [128, D], F32)
    ssq = p.sb(f"lnd_ssq_{tag}", [128, 1], F32)
    sx = p.sb(f"lnd_sx_{tag}", [128, 1], F32)
    mean = p.sb(f"lnd_mean_{tag}", [128, 1], F32)
    var = p.sb(f"lnd_var_{tag}", [128, 1], F32)
    rstd = p.sb(f"lnd_rstd_{tag}", [128, 1], F32)
    tf = p.sb(f"lnd_tf_{tag}", [128, D], F32)
    stage = [p.sb(f"lnd_st_{tag}_{b}", [128, D], BF16) for b in range(2)]
    stage_lo = [p.sb(f"lnd_sl_{tag}_{b}", [128, D], BF16) for b in range(2)] if trb_lo is not None else None
    ndma = 2 if trb_lo is not None else 1

    s_sq = p.sem("lnd_sq")
    s_st = p.sem("lnd_st")
    s_rs = p.sem("lnd_rs")
    s_done = p.sem("lnd_done")
    s_tb = p.sem("lnd_tb")
    base_tb = s_tb.n

    trb_r = trb.rearrange("(t p) x -> t p x", p=128)
    trb_lo_r = trb_lo.rearrange("(t p) x -> t p x", p=128) if trb_lo is not None else None

    for i in range(NTT):
        xt = x_sb[:, i, :]
        st_war = s_st.n

        def act1(e, xt=xt, i=i, st_war=st_war):
            if i == 0:
                for (sm, v) in deps:
                    e.wait_ge(sm.h, v)
            else:
                e.wait_ge(s_st.h, st_war)
            e.activation(sq[:], xt, AF.Square, accum_out=ssq[:]).then_inc(s_sq.h, 1)
        p.on("scalar", act1)
        sq_v = s_sq.bump()

        def dve1(e, xt=xt, sq_v=sq_v, i=i):
            if i == 0:
                for (sm, v) in deps:
                    e.wait_ge(sm.h, v)
            e.reduce_sum(sx[:], xt, axis=AX.X)
            e.drain()
            e.wait_ge(s_sq.h, sq_v)
            e.tensor_scalar_mul(mean[:], sx[:], 1.0 / D)
            e.drain()
            e.tensor_tensor(sx[:], mean[:], mean[:], ALU.mult)
            e.tensor_scalar_mul(var[:], ssq[:], 1.0 / D)
            e.drain()
            e.tensor_tensor(var[:], var[:], sx[:], ALU.subtract).then_inc(s_st.h, 1)
        p.on("vector", dve1)
        st_v = s_st.bump()

        def act2(e, st_v=st_v):
            e.wait_ge(s_st.h, st_v)
            e.activation(var[:], var[:], AF.Ln, bias=EPS_LN)
            e.drain()
            e.activation(rstd[:], var[:], AF.Exp, scale=-0.5).then_inc(s_rs.h, 1)
        p.on("scalar", act2)
        rs_v = s_rs.bump()

        def dve2(e, xt=xt, i=i, rs_v=rs_v,
                 tb_war=base_tb + i * 16 * ndma):
            e.wait_ge(s_rs.h, rs_v)
            e.wait_ge(s_tb.h, tb_war)
            b = i % 2
            if trb_lo is not None:
                e.tensor_scalar(tf[:], xt, mean[:], rstd[:], ALU.subtract, ALU.mult)
                e.drain()
                e.tensor_copy(stage[b][:], tf[:])
                e.drain()
                e.tensor_tensor(stage_lo[b][:], tf[:], stage[b][:],
                                ALU.subtract).then_inc(s_done.h, 1)
            else:
                e.tensor_scalar(stage[b][:], xt, mean[:], rstd[:],
                                ALU.subtract, ALU.mult).then_inc(s_done.h, 1)
        p.on("vector", dve2)
        dn_v = s_done.bump()

        def tdma(e, i=i, dn_v=dn_v):
            e.wait_ge(s_done.h, dn_v)
            b = i % 2
            e.dma_start(out=trb_r[i], in_=stage[b][:]).then_inc(s_tb.h, 16)
            if trb_lo is not None:
                e.dma_start(out=trb_lo_r[i], in_=stage_lo[b][:]).then_inc(s_tb.h, 16)
        p.on("scalar", tdma)
        s_tb.bump(16 * ndma)

    return s_tb, s_tb.n


def emit_ffn(p: Prog, d, aux, x_sb, deps, tag, upto="full", dbg=None):
    BIG = 1.0e9
    NM_E = F_EXP // 128

    tT = p.sb(f"ffn_tT_{tag}", [128, NDC, TOK], BF16)
    wrows = p.sb(f"ffn_wr_{tag}", [1, N_EXP, TOK], BF16)
    s_tb = p.sem("ffn_tb")
    s_tt = p.sem("ffn_tt")
    s_rt = p.sem("ffn_rt")

    with p.scope():
        s_tbl, tbv = emit_ln_dram(p, x_sb, d[f"trb_{tag}"], d[f"trb_lo_{tag}"],
                                  deps, f"{tag}")

        def tbfwd(e, tbv=tbv):
            e.wait_ge(s_tbl.h, tbv)
            e.sem_inc(s_tb.h, 1)
        p.on("scalar", tbfwd)
        tbv = s_tb.bump(1)
        p.barrier(engines=("vector", "scalar"))

    def treads(e, tbv=tbv):
        e.wait_ge(s_tb.h, tbv)
        for j in range(NDC):
            e.dma_start_transpose(tT[:, j, :],
                                  d[f"trb_{tag}"][:, j * 128:(j + 1) * 128]).then_inc(s_tt.h, 16)
    p.on("sync", treads)
    ttv = s_tt.bump(16 * NDC)

    with p.scope():
        tT_lo = p.sb(f"ffn_tTlo_{tag}", [128, NDC, TOK], BF16)

        def treads_lo(e, tbv=tbv):
            e.wait_ge(s_tb.h, tbv)
            for j in range(NDC):
                e.dma_start_transpose(tT_lo[:, j, :],
                                      d[f"trb_lo_{tag}"][:, j * 128:(j + 1) * 128]).then_inc(s_tt.h, 16)
        p.on("sync", treads_lo)
        ttv_lo = s_tt.bump(16 * NDC)

        gw_f = p.sb(f"ffn_gwf_{tag}", [128, NDC, 8], F32)
        gw_hi = p.sb(f"ffn_gwh_{tag}", [128, NDC, 8], BF16)
        gw_lo = p.sb(f"ffn_gwl_{tag}", [128, NDC, 8], BF16)

        def gwload(e):
            e.dma_start(out=gw_f[:], in_=d["gwT"].rearrange("(a p) n -> p a n", p=128)).then_inc(s_rt.h, 16)
        p.on("sync", gwload)
        rt_gw = s_rt.bump(16)

        def gwsplit(e, rt_gw=rt_gw):
            e.wait_ge(s_rt.h, rt_gw)
            e.tensor_copy(gw_hi[:], gw_f[:])
            e.drain()
            e.tensor_tensor(gw_lo[:], gw_f[:], gw_hi[:], ALU.subtract).then_inc(s_rt.h, 1)
        p.on("vector", gwsplit)
        rt_sp = s_rt.bump()

        ps_lg = p.ps(f"ffn_pslg_{tag}", [128, 8])
        ps_wr = p.ps(f"ffn_pswr_{tag}", [16, 128], BF16)
        lg = p.sb(f"ffn_lg_{tag}", [128, 8], F32)
        elg = p.sb(f"ffn_elg_{tag}", [128, 8], F32)
        ssum = p.sb(f"ffn_ssum_{tag}", [128, 1], F32)
        sc = p.sb(f"ffn_sc_{tag}", [128, 8], F32)
        gs = p.sb(f"ffn_gs_{tag}", [128, 4], F32)
        gx = p.sb(f"ffn_gx_{tag}", [128, 4], F32)
        mcol = p.sb(f"ffn_mcol_{tag}", [128, 1], F32)
        sm8 = p.sb(f"ffn_sm8_{tag}", [128, 8], F32)
        msk = p.sb(f"ffn_msk_{tag}", [128, 8], F32)
        sel = p.sb(f"ffn_sel_{tag}", [128, 8], F32)
        wm = p.sb(f"ffn_wm_{tag}", [128, 8], F32)
        wm_pad = p.sb(f"ffn_wmpad_{tag}", [128, 128], BF16)
        s_lg = p.sem("ffn_lg")
        s_wr = p.sem("ffn_wrow")

        for tt in range(NTT):
            def mml(e, tt=tt, first=(tt == 0), rt_sp=rt_sp, ttv=ttv, ttv_lo=ttv_lo,
                    lg_done=s_lg.n):
                if first:
                    e.wait_ge(s_rt.h, rt_sp)
                    e.wait_ge(s_tt.h, ttv_lo)
                else:
                    e.wait_ge(s_lg.h, lg_done)
                ins = None
                for j in range(NDC):
                    sl = tT[:, j, tt * 128:(tt + 1) * 128]
                    sl_lo = tT_lo[:, j, tt * 128:(tt + 1) * 128]
                    ins = e.matmul(ps_lg[:], sl, gw_hi[:, j, :], start=(j == 0), stop=False)
                    ins = e.matmul(ps_lg[:], sl, gw_lo[:, j, :], start=False, stop=False)
                    ins = e.matmul(ps_lg[:], sl_lo, gw_hi[:, j, :], start=False,
                                   stop=(j == NDC - 1))
                ins.then_inc(s_lg.h, 1)
            p.on("tensor", mml)
            lg_v = s_lg.bump()

            def route(e, lg_v=lg_v):
                e.wait_ge(s_lg.h, lg_v)
                e.tensor_copy(lg[:], ps_lg[:])
                e.drain()
                e.engine_nop().then_inc(s_lg.h, 1)
            p.on("vector", route)
            lg_cp = s_lg.bump()

            def act_exp(e, lg_cp=lg_cp):
                e.wait_ge(s_lg.h, lg_cp)
                e.activation(elg[:], lg[:], AF.Exp, accum_out=ssum[:]).then_inc(s_rt.h, 1)
            p.on("scalar", act_exp)
            rt_e = s_rt.bump()

            def route2(e, rt_e=rt_e):
                e.wait_ge(s_rt.h, rt_e)
                e.reciprocal(ssum[:], ssum[:])
                e.drain()
                e.tensor_scalar_mul(sc[:], elg[:], ssum[:])
                e.drain()
                e.tensor_tensor(gs[:], sc[:, 0:8:2], sc[:, 1:8:2], ALU.max)
                e.drain()
                e.reduce_max(mcol[:], gs[:], axis=AX.X)
                e.drain()
                e.tensor_scalar(gx[:], gs[:], mcol[:], None, ALU.is_ge)
                e.drain()
                e.scalar_tensor_tensor(gx[:], gx[:], -BIG, gs[:], ALU.mult, ALU.add)
                e.drain()
                e.reduce_max(mcol[:], gx[:], axis=AX.X)
                e.drain()
                e.tensor_scalar(gx[:], gs[:], mcol[:], None, ALU.is_ge)
                e.drain()
                e.tensor_copy(sm8[:, 0:8:2], gx[:])
                e.drain()
                e.tensor_copy(sm8[:, 1:8:2], gx[:])
                e.drain()
                e.tensor_tensor(msk[:], sc[:], sm8[:], ALU.mult)
                e.drain()
                e.reduce_max(mcol[:], msk[:], axis=AX.X)
                e.drain()
                e.tensor_scalar(sel[:], msk[:], mcol[:], None, ALU.is_ge)
                e.drain()
                e.scalar_tensor_tensor(sel[:], sel[:], -BIG, msk[:], ALU.mult, ALU.add)
                e.drain()
                e.reduce_max(mcol[:], sel[:], axis=AX.X)
                e.drain()
                e.tensor_scalar(sel[:], msk[:], mcol[:], None, ALU.is_ge)
                e.drain()
                e.tensor_tensor(wm[:], sel[:], msk[:], ALU.mult)
                e.drain()
                e.memset(wm_pad[:], 0.0)
                e.drain()
                e.tensor_copy(wm_pad[:, 0:128:16], wm[:]).then_inc(s_rt.h, 1)
            p.on("vector", route2)
            rt_w = s_rt.bump()

            if upto == "logits":
                continue
            for n in range(N_EXP):
                def wrt(e, n=n, rt_w=rt_w, wr_prev=s_wr.n):
                    e.wait_ge(s_rt.h, rt_w)
                    if wr_prev > 0:
                        e.wait_ge(s_wr.h, wr_prev)
                    e.transpose(ps_wr[:], wm_pad[:, 16 * n:16 * (n + 1)],
                                aux["ident"][:]).then_inc(s_rt.h, 1)
                p.on("tensor", wrt)
                rt_t = s_rt.bump()

                def wre(e, n=n, tt=tt, rt_t=rt_t):
                    e.wait_ge(s_rt.h, rt_t)
                    e.tensor_copy(wrows[0:1, n, tt * 128:(tt + 1) * 128],
                                  ps_wr[0:1, :]).then_inc(s_wr.h, 1)
                p.on("vector", wre)
                s_wr.bump()
            if dbg is not None and "wm_to_x" in dbg:
                def wmdump(e, tt=tt, rt_w=rt_w):
                    e.wait_ge(s_rt.h, rt_w)
                    e.tensor_copy(x_sb[:, tt, 0:8], wm[:])
                    e.drain()
                    e.tensor_copy(x_sb[:, tt, 8:16], sc[:])
                    e.drain()
                    e.tensor_copy(x_sb[:, tt, 16:20], gs[:])
                    e.drain()
                    e.tensor_copy(x_sb[:, tt, 20:24], gx[:])
                    e.drain()
                    e.tensor_copy(x_sb[:, tt, 24:32], sm8[:]).then_inc(s_rt.h, 1)
                p.on("vector", wmdump)
                s_rt.bump(1)
        wr_all = s_wr.n
        if upto in ("router", "logits"):
            return s_rt, s_rt.n
        bar_b = p.barrier(engines=("tensor", "vector", "scalar"))

    with p.scope():
        NRING = 2
        wg_ring = [p.sb(f"ffn_wg_{tag}_{i}", [128, NDC, 128], BF16) for i in range(NRING)]
        wu_ring = [p.sb(f"ffn_wu_{tag}_{i}", [128, NDC, 128], BF16) for i in range(NRING)]
        wd_sb = p.sb(f"ffn_wd_{tag}", [128, NM_E, D], BF16)
        act_sb = p.sb(f"ffn_act_{tag}", [128, NM_E, TOK], BF16)
        bc_bf = p.sb(f"ffn_bc_{tag}", [128, TOK], BF16)
        sg_bf = [p.sb(f"ffn_sg_{tag}_{i}", [128, TOK], BF16) for i in range(2)]
        a1 = p.sb(f"ffn_a1_{tag}", [128, TOK], BF16)
        ps_g = [p.ps(f"ffn_psg_{tag}_{i}", [128, TOK]) for i in range(2)]
        ps_u = [p.ps(f"ffn_psu_{tag}_{i}", [128, TOK]) for i in range(2)]

        s_wlp = [p.sem("ffn_wl0"), p.sem("ffn_wl1")]
        s_mf = p.sem("ffn_mf")
        s_ae = p.sem("ffn_ae")
        s_av = p.sem("ffn_av")
        s_bc = p.sem("ffn_bc")
        s_dn = p.sem("ffn_dn")
        s_xa = p.sem("ffn_xa")
        s_dl = p.sem("ffn_dl")
        base_wlp = [s_wlp[0].n, s_wlp[1].n]
        base_mf, base_dn, base_dl = s_mf.n, s_dn.n, s_dl.n

        blocks = [("e", n) for n in range(N_EXP)] + [("s", h) for h in range(2)]
        SKIP_DOWN = False

        mseq = 0
        for bi, (kind, n) in enumerate(blocks):
            for m in range(NM_E):
                for (ring, wsel) in ((wg_ring, "g"), (wu_ring, "u")):
                    if kind == "e":
                        wtns = d["gate_wT"] if wsel == "g" else d["up_wT"]
                        src = wtns[n, :, m * 128:(m + 1) * 128]
                    else:
                        wtns = d["sh_gate_wT"] if wsel == "g" else d["sh_up_wT"]
                        c0 = n * F_EXP + m * 128
                        src = wtns[:, c0:c0 + 128]
                    src = src.rearrange("(a p) c -> p a c", p=128)

                    def wl(e, src=src, ring=ring, mseq=mseq, bar_b=bar_b,
                           sw=s_wlp[mseq % 2], need=base_mf + max(0, mseq - 1)):
                        e.wait_ge(bar_b[0].h, bar_b[1])
                        if need > base_mf:
                            e.wait_ge(s_mf.h, need)
                        e.dma_start(out=ring[mseq % NRING][:], in_=src).then_inc(sw.h, 16)
                    p.on("gpsimd", wl)
                    s_wlp[mseq % 2].bump(16)
                mseq += 1
            for k in range(NM_E):
                if kind == "e":
                    dsrc = d["down_wT"][n, k * 128:(k + 1) * 128, :]
                else:
                    r0 = n * F_EXP + k * 128
                    dsrc = d["sh_down_wT"][r0:r0 + 128, :]

                def dl(e, dsrc=dsrc, k=k, bi=bi, bar_b=bar_b,
                       dn_need=base_dn + bi * NTT):
                    e.wait_ge(bar_b[0].h, bar_b[1])
                    if bi > 0:
                        e.wait_ge(s_dn.h, dn_need)
                    e.dma_start(out=wd_sb[:, k, :], in_=dsrc).then_inc(s_dl.h, 16)
                p.on("gpsimd", dl)
                s_dl.bump(16)

        mseq = 0
        for bi, (kind, n) in enumerate(blocks):
            routed = (kind == "e")
            if routed:
                def bcast(e, n=n, wr_all=wr_all, bar_b=bar_b, bi=bi,
                          av_need=s_av.n, xa_need=s_xa.n):
                    e.wait_ge(s_wr.h, wr_all)
                    e.wait_ge(bar_b[0].h, bar_b[1])
                    if bi > 0:
                        e.wait_ge(s_av.h, av_need)
                        e.wait_ge(s_xa.h, xa_need)
                    for nck in range(TOK // 512):
                        e.matmul(ps_g[0][:, nck * 512:(nck + 1) * 512], aux["ones1_bf"][:],
                                 wrows[0:1, n, nck * 512:(nck + 1) * 512],
                                 start=True, stop=True).then_inc(s_bc.h, 1)
                p.on("tensor", bcast)
                bc_f = s_bc.bump(2)

                def bccp(e, bc_f=bc_f):
                    e.wait_ge(s_bc.h, bc_f)
                    e.activation(bc_bf[:], ps_g[0][:], AF.Copy).then_inc(s_bc.h, 1)
                p.on("scalar", bccp)
                bc_c = s_bc.bump()
            else:
                bc_c = s_bc.n

            for m in range(NM_E):
                mb = mseq % 2
                wg = wg_ring[mseq % NRING]
                wu = wu_ring[mseq % NRING]

                def mmg(e, mb=mb, wg=wg, wu=wu, sw=s_wlp[mseq % 2],
                        wl_need=base_wlp[mseq % 2] + 32 * (mseq // 2 + 1),
                        bc_c=bc_c, m=m, routed=routed, bi=bi,
                        av_need=max(0, s_av.n - 1), xa_need=s_xa.n if m <= 1 and bi > 0 else 0):
                    e.wait_ge(sw.h, wl_need)
                    if m == 0 and routed:
                        e.wait_ge(s_bc.h, bc_c)
                    if xa_need > 0:
                        e.wait_ge(s_xa.h, xa_need)
                    e.wait_ge(s_av.h, av_need)
                    ins = None
                    for j in range(NDC):
                        for nck in range(TOK // 512):
                            ins = e.matmul(ps_g[mb][:, nck * 512:(nck + 1) * 512],
                                           wg[:, j, :], tT[:, j, nck * 512:(nck + 1) * 512],
                                           start=(j == 0), stop=(j == NDC - 1))
                    for j in range(NDC):
                        for nck in range(TOK // 512):
                            ins = e.matmul(ps_u[mb][:, nck * 512:(nck + 1) * 512],
                                           wu[:, j, :], tT[:, j, nck * 512:(nck + 1) * 512],
                                           start=(j == 0), stop=(j == NDC - 1))
                    ins.then_inc(s_mf.h, 1)
                p.on("tensor", mmg)
                mf_v = s_mf.bump()

                def silu(e, mb=mb, mf_v=mf_v):
                    e.wait_ge(s_mf.h, mf_v)
                    e.activation(sg_bf[mb][:], ps_g[mb][:], AF.Silu).then_inc(s_ae.h, 1)
                p.on("scalar", silu)
                ae_v = s_ae.bump()

                def acte(e, m=m, mb=mb, ae_v=ae_v, routed=routed,
                         dn_need=base_dn + bi * NTT):
                    e.wait_ge(s_ae.h, ae_v)
                    if dn_need > base_dn:
                        e.wait_ge(s_dn.h, dn_need)
                    if routed:
                        e.tensor_tensor(a1[:], sg_bf[mb][:], ps_u[mb][:], ALU.mult)
                        e.drain()
                        e.tensor_tensor(act_sb[:, m, :], a1[:], bc_bf[:],
                                        ALU.mult).then_inc(s_av.h, 1)
                    else:
                        e.tensor_tensor(act_sb[:, m, :], sg_bf[mb][:], ps_u[mb][:],
                                        ALU.mult).then_inc(s_av.h, 1)
                p.on("vector", acte)
                s_av.bump()
                mseq += 1

            for tt in range(NTT if not SKIP_DOWN else 0):
                def mmd(e, tt=tt, av_all=s_av.n, dl_need=base_dl + 16 * NM_E * (bi + 1),
                        xa_need=s_xa.n):
                    e.wait_ge(s_av.h, av_all)
                    e.wait_ge(s_dl.h, dl_need)
                    if xa_need > 0:
                        e.wait_ge(s_xa.h, xa_need)
                    ins = None
                    if True:
                        for k in range(NM_E):
                            for nch in range(4):
                                bank = ps_g[nch // 2]
                                tgt = bank[:, (nch % 2) * 512:(nch % 2 + 1) * 512]
                                ins = e.matmul(tgt, act_sb[:, k, tt * 128:(tt + 1) * 128],
                                               wd_sb[:, k, nch * 512:(nch + 1) * 512],
                                               start=(k == 0), stop=(k == NM_E - 1))
                    ins.then_inc(s_dn.h, 1)
                p.on("tensor", mmd)
                dn_v = s_dn.bump()

                def xadd(e, tt=tt, dn_v=dn_v):
                    e.wait_ge(s_dn.h, dn_v)
                    e.tensor_tensor(x_sb[:, tt, 0:TOK], x_sb[:, tt, 0:TOK],
                                    ps_g[0][:], ALU.add)
                    e.drain()
                    e.tensor_tensor(x_sb[:, tt, TOK:D], x_sb[:, tt, TOK:D],
                                    ps_g[1][:], ALU.add).then_inc(s_xa.h, 1)
                p.on("vector", xadd)
                s_xa.bump()

    return s_xa, s_xa.n



def emit_delta_head(p: Prog, qT, kT, vT, olinT, aux, deps, tag):
    p.constv(ATT_SCALE)
    base_be0 = p.sem("dl_be").n
    GRP = 8
    NG = NCH // GRP
    GW = GRP * 128

    qh_g = [p.sb(f"dl_qh_{tag}_{b}", [128, GW], BF16) for b in range(2)]
    kt_g = [p.sb(f"dl_kt_{tag}_{b}", [128, GW], BF16) for b in range(2)]
    kb_g = [p.sb(f"dl_kb_{tag}_{b}", [128, GW], BF16) for b in range(2)]
    vb_g = [p.sb(f"dl_vb_{tag}_{b}", [128, GW], BF16) for b in range(2)]
    eq = p.sb(f"dl_eq_{tag}", [128, GW], BF16)
    tmpg = p.sb(f"dl_tmpg_{tag}", [128, GW], F32)
    den = p.sb(f"dl_den_{tag}", [1, GW], F32)
    akg = p.sb(f"dl_ak_{tag}", [128, GW], BF16)
    e2g = p.sb(f"dl_e2_{tag}", [128, GW], F32)
    mng = p.sb(f"dl_mn_{tag}", [128, GW], F32)
    gcg = p.sb(f"dl_gc_{tag}", [128, GW], BF16)

    vb_tm = p.sb(f"dl_vbtm_{tag}", [128, 128], BF16)
    kb_tm = p.sb(f"dl_kbtm_{tag}", [128, 128], BF16)
    Lm = p.sb(f"dl_L_{tag}", [128, 128], BF16)
    LTm = p.sb(f"dl_LT_{tag}", [128, 128], BF16)
    Pw = [p.sb(f"dl_pw_{tag}_{i}", [128, 128], BF16) for i in range(5)]
    PwT = [p.sb(f"dl_pwt_{tag}_{i}", [128, 128], BF16) for i in range(2)]
    Y = p.sb(f"dl_Y_{tag}", [128, 128], BF16)
    TT = p.sb(f"dl_TT_{tag}", [128, 128], BF16)
    u_sb = [p.sb(f"dl_u_{tag}_{b}", [128, 128], BF16) for b in range(2)]
    wT_sb = [p.sb(f"dl_w_{tag}_{b}", [128, 128], BF16) for b in range(2)]
    at_sb = [p.sb(f"dl_at_{tag}_{b}", [128, 128], BF16) for b in range(2)]
    ktm_sb = [p.sb(f"dl_ktm_{tag}_{b}", [128, 128], BF16) for b in range(2)]
    u2 = p.sb(f"dl_u2_{tag}", [128, 128], BF16)
    oc = p.sb(f"dl_oc_{tag}", [128, 128], BF16)
    S_f = p.sb(f"dl_Sf_{tag}", [128, 128], F32)
    S_b = p.sb(f"dl_Sb_{tag}", [128, 128], BF16)

    ps_a = [p.ps(f"dl_psa_{tag}_{i}", [128, 128]) for i in range(2)]
    ps_r = p.ps(f"dl_psr_{tag}", [128, 512])
    ps_t = p.ps(f"dl_pst_{tag}", [128, 128], BF16)
    ps_b = [p.ps(f"dl_psb_{tag}_{i}", [128, 128]) for i in range(2)]

    s_tf = p.sem("dl_tf")
    s_af = p.sem("dl_af")
    s_ae = p.sem("dl_ae")
    s_bf = p.sem("dl_bf")
    s_be = p.sem("dl_be")

    def sinit(e):
        for (sm, v) in deps:
            e.wait_ge(sm.h, v)
        e.memset(S_f[:], 0.0)
        e.drain()
        e.memset(S_b[:], 0.0)
        e.drain()
        e.engine_nop().then_inc(s_be.h, 1)
    p.on("vector", sinit)
    s_be.bump()

    def emit_group(g):
        b = g % 2
        gsl = slice(g * GW, (g + 1) * GW)
        war = base_be0 + 1 + max(0, 24 * (g - 1))

        def t_act1(e, g=g, gsl=gsl, war=war, prev_grp=s_tf.n):
            if g == 0:
                for (sm, v) in deps:
                    e.wait_ge(sm.h, v)
            e.wait_ge(s_tf.h, prev_grp)
            e.wait_ge(s_be.h, war)
            e.activation(eq[:], qT[:, gsl], AF.Exp).then_inc(s_tf.h, 1)
        p.on("scalar", t_act1)
        v1 = s_tf.bump()

        def softmax_pipe(src_fm, dst, scale_after, v_ready):
            vr = v_ready
            for ncl in range(2):
                def pe_row(e, src_fm=src_fm, ncl=ncl, vr=vr, ae_w=s_tf.n):
                    e.wait_ge(s_tf.h, vr)
                    e.matmul(ps_r[0:1, 0:512], aux["ones_bf"][:],
                             eq[:, ncl * 512:(ncl + 1) * 512],
                             start=True, stop=True).then_inc(s_tf.h, 1)
                p.on("tensor", pe_row)
                v_r = s_tf.bump()

                def dv_rec(e, ncl=ncl, v_r=v_r):
                    e.wait_ge(s_tf.h, v_r)
                    e.tensor_copy(den[0:1, ncl * 512:(ncl + 1) * 512], ps_r[0:1, 0:512])
                    e.drain()
                    e.reciprocal(den[0:1, ncl * 512:(ncl + 1) * 512],
                                 den[0:1, ncl * 512:(ncl + 1) * 512])
                    e.drain()
                    e.engine_nop().then_inc(s_tf.h, 1)
                p.on("vector", dv_rec)
                v_c = s_tf.bump()

                def pe_bc(e, ncl=ncl, v_c=v_c):
                    e.wait_ge(s_tf.h, v_c)
                    e.matmul(ps_r[:, 0:512], aux["ones1_f"][:],
                             den[0:1, ncl * 512:(ncl + 1) * 512],
                             start=True, stop=True).then_inc(s_tf.h, 1)
                p.on("tensor", pe_bc)
                v_b = s_tf.bump()

                def dv_fin(e, dst=dst, ncl=ncl, v_b=v_b, scale_after=scale_after):
                    e.wait_ge(s_tf.h, v_b)
                    csl = slice(ncl * 512, (ncl + 1) * 512)
                    e.tensor_tensor(tmpg[:, csl], eq[:, csl], ps_r[:, 0:512], ALU.mult)
                    e.drain()
                    if scale_after is None:
                        e.tensor_scalar(dst[:, csl], tmpg[:, csl], 1e-6, 1.0 - 1e-6,
                                        ALU.max, ALU.min).then_inc(s_tf.h, 1)
                    else:
                        e.tensor_scalar(tmpg[:, csl], tmpg[:, csl], 1e-6, 1.0 - 1e-6,
                                        ALU.max, ALU.min)
                        e.drain()
                        e.tensor_scalar_mul(dst[:, csl], tmpg[:, csl],
                                            float(scale_after)).then_inc(s_tf.h, 1)
                p.on("vector", dv_fin)
                vr = s_tf.bump()
            return vr

        vq = softmax_pipe(None, qh_g[b], ATT_SCALE, v1)

        def t_act2(e, gsl=gsl, vq=vq):
            e.wait_ge(s_tf.h, vq)
            e.activation(eq[:], kT[:, gsl], AF.Exp).then_inc(s_tf.h, 1)
        p.on("scalar", t_act2)
        v2 = s_tf.bump()
        vk = softmax_pipe(None, kt_g[b], None, v2)

        def t_gate_a(e, gsl=gsl, vk=vk):
            e.wait_ge(s_tf.h, vk)
            e.activation(akg[:], kT[:, gsl], AF.Abs)
            e.drain()
            e.activation(e2g[:], akg[:], AF.Exp, scale=-1.0)
            e.drain()
            e.activation(e2g[:], e2g[:], AF.Ln, bias=1.0).then_inc(s_tf.h, 1)
        p.on("scalar", t_gate_a)
        v4 = s_tf.bump()

        def t_gate_d2(e, gsl=gsl, b=b, v4=v4):
            e.wait_ge(s_tf.h, v4)
            e.tensor_scalar(mng[:], kT[:, gsl], 0.0, None, ALU.min)
            e.drain()
            e.tensor_tensor(mng[:], mng[:], e2g[:], ALU.subtract)
            e.drain()
            e.tensor_scalar(mng[:], mng[:], 1.0 / 16.0, -16.0, ALU.mult, ALU.max)
            e.drain()
            e.tensor_scalar(gcg[:], mng[:], 16.0, None, ALU.min)
            e.drain()
            e.tensor_tensor(kb_g[b][:], kt_g[b][:], gcg[:], ALU.mult)
            e.drain()
            e.tensor_tensor(vb_g[b][:], vT[:, gsl], gcg[:], ALU.mult).then_inc(s_tf.h, 1)
        p.on("vector", t_gate_d2)
        return s_tf.bump()

    grp_done = {}
    def emit_A(c):
        g = c // GRP
        b2 = c % 2
        bg = g % 2
        csl = slice((c % GRP) * 128, (c % GRP + 1) * 128)
        qh, kt_, kb, vb = qh_g[bg], kt_g[bg], kb_g[bg], vb_g[bg]

        be_war = s_be.n

        def a_pe(e, c=c, csl=csl, qh=qh, kt_=kt_, kb=kb, vb=vb, b2=b2,
                 gd=grp_done[g], be_war=s_be.n, ae_prev=s_ae.n):
            e.wait_ge(s_tf.h, gd)
            if c >= 2:
                e.wait_ge(s_be.h, be_war)
            if c >= 1:
                e.wait_ge(s_ae.h, ae_prev)
            e.transpose(ps_t[:], vb[:, csl], aux["ident"][:]).then_inc(s_af.h, 1)
            e.wait_ge(s_ae.h, ae_prev + 1)
            e.transpose(ps_t[:], kb[:, csl], aux["ident"][:]).then_inc(s_af.h, 1)
            e.wait_ge(s_ae.h, ae_prev + 2)
            e.transpose(ps_t[:], kt_[:, csl], aux["ident"][:]).then_inc(s_af.h, 1)
            e.matmul(ps_a[0][:], kb[:, csl], kt_[:, csl], start=True, stop=True).then_inc(s_af.h, 1)
            e.matmul(ps_a[1][:], kt_[:, csl], kb[:, csl], start=True, stop=True).then_inc(s_af.h, 1)
            e.wait_ge(s_ae.h, ae_prev + 5)
            e.matmul(ps_a[0][:], kt_[:, csl], qh[:, csl], start=True, stop=True).then_inc(s_af.h, 1)
        p.on("tensor", a_pe)
        ae_prev = s_ae.n
        s_af.bump(6)

        def a_dv(e, c=c, b2=b2, af0=ae_prev * 0 + s_af.n - 6):
            e.wait_ge(s_af.h, af0 + 1)
            e.tensor_copy(vb_tm[:], ps_t[:]).then_inc(s_ae.h, 1)
            e.wait_ge(s_af.h, af0 + 2)
            e.tensor_copy(kb_tm[:], ps_t[:]).then_inc(s_ae.h, 1)
            e.wait_ge(s_af.h, af0 + 3)
            e.tensor_copy(ktm_sb[b2][:], ps_t[:]).then_inc(s_ae.h, 1)
            e.wait_ge(s_af.h, af0 + 4)
            e.tensor_tensor(Lm[:], ps_a[0][:], aux["m_sl"][:], ALU.mult).then_inc(s_ae.h, 1)
            e.wait_ge(s_af.h, af0 + 5)
            e.tensor_tensor(LTm[:], ps_a[1][:], aux["m_su"][:], ALU.mult).then_inc(s_ae.h, 1)
            e.wait_ge(s_af.h, af0 + 6)
            e.tensor_tensor(at_sb[b2][:], ps_a[0][:], aux["m_iu"][:], ALU.mult).then_inc(s_ae.h, 1)
        p.on("vector", a_dv)
        s_ae.bump(6)

        for lev in range(6):
            curL = Lm if lev == 0 else Pw[lev - 1]
            curLT = LTm if lev == 0 else PwT[(lev - 1) % 2]
            last = (lev == 5)

            def p_pe(e, curL=curL, curLT=curLT, last=last, ae_n=hc["ae_own"]):
                e.wait_ge(s_ae.h, ae_n)
                if not last:
                    e.matmul(ps_a[0][:], curLT[:], curL[:], start=True, stop=True)
                e.matmul(ps_a[1][:], curL[:], curLT[:], start=True,
                         stop=True).then_inc(s_af.h, 1)
            p.on("tensor", p_pe)
            af_v = s_af.bump()

            def p_dv(e, lev=lev, last=last, af_v=af_v):
                e.wait_ge(s_af.h, af_v)
                if not last:
                    e.tensor_copy(Pw[lev][:], ps_a[0][:])
                    e.drain()
                e.tensor_copy(PwT[lev % 2][:], ps_a[1][:]).then_inc(s_ae.h, 1)
            p.on("vector", p_dv)
            s_ae.bump()

        def y_init(e, ae_n=s_ae.n):
            e.wait_ge(s_ae.h, ae_n)
            e.tensor_tensor(Y[:], PwT[5 % 2][:], aux["eye_bf"][:], ALU.add).then_inc(s_ae.h, 1)
        p.on("vector", y_init)
        s_ae.bump()

        prod_lhs = [Pw[4], Pw[3], Pw[2], Pw[1], Pw[0], Lm]
        for pi, lhs in enumerate(prod_lhs):
            lastp = (pi == 5)

            def pr_pe(e, lhs=lhs, ae_n=s_ae.n):
                e.wait_ge(s_ae.h, ae_n)
                e.matmul(ps_a[0][:], lhs[:], Y[:], start=True, stop=True).then_inc(s_af.h, 1)
            p.on("tensor", pr_pe)
            af_v = s_af.bump()

            def pr_dv(e, lastp=lastp, af_v=af_v):
                e.wait_ge(s_af.h, af_v)
                if lastp:
                    e.tensor_tensor(TT[:], Y[:], ps_a[0][:], ALU.subtract).then_inc(s_ae.h, 1)
                else:
                    e.tensor_tensor(Y[:], Y[:], ps_a[0][:], ALU.add).then_inc(s_ae.h, 1)
            p.on("vector", pr_dv)
            s_ae.bump()

        def uw_pe(e, ae_n=s_ae.n):
            e.wait_ge(s_ae.h, ae_n)
            e.matmul(ps_a[0][:], TT[:], vb_tm[:], start=True, stop=True)
            e.matmul(ps_a[1][:], kb_tm[:], TT[:], start=True, stop=True).then_inc(s_af.h, 1)
        p.on("tensor", uw_pe)
        af_v = s_af.bump()

        def uw_dv(e, b2=b2, af_v=af_v):
            e.wait_ge(s_af.h, af_v)
            e.tensor_copy(u_sb[b2][:], ps_a[0][:])
            e.drain()
            e.tensor_copy(wT_sb[b2][:], ps_a[1][:]).then_inc(s_ae.h, 1)
        p.on("vector", uw_dv)
        s_ae.bump()
        return s_ae.n

    def emit_B(c, a_done):
        b2 = c % 2
        csl = slice(c * 128, (c + 1) * 128)
        lsl = slice((c % GRP) * 128, (c % GRP + 1) * 128)
        g = c // GRP
        bg = g % 2
        qh = qh_g[bg]

        def b_pe1(e, b2=b2, a_done=a_done, be_n=s_be.n):
            e.wait_ge(s_ae.h, a_done)
            e.wait_ge(s_be.h, be_n)
            e.matmul(ps_b[0][:], wT_sb[b2][:], S_b[:], start=True, stop=True).then_inc(s_bf.h, 1)
        p.on("tensor", b_pe1)
        bf1 = s_bf.bump()

        def b_dv1(e, b2=b2, bf1=bf1):
            e.wait_ge(s_bf.h, bf1)
            e.tensor_tensor(u2[:], u_sb[b2][:], ps_b[0][:], ALU.subtract).then_inc(s_be.h, 1)
        p.on("vector", b_dv1)
        be1 = s_be.bump()

        def b_pe2(e, b2=b2, lsl=lsl, qh=qh, be1=be1):
            e.wait_ge(s_be.h, be1)
            e.matmul(ps_b[1][:], qh[:, lsl], S_b[:], start=True, stop=False)
            e.matmul(ps_b[1][:], at_sb[b2][:], u2[:], start=False, stop=True)
            e.matmul(ps_b[0][:], ktm_sb[b2][:], u2[:], start=True,
                     stop=True).then_inc(s_bf.h, 1)
        p.on("tensor", b_pe2)
        bf2 = s_bf.bump()

        def b_dv2(e, bf2=bf2):
            e.wait_ge(s_bf.h, bf2)
            e.tensor_copy(oc[:], ps_b[1][:])
            e.drain()
            e.tensor_tensor(S_f[:], S_f[:], ps_b[0][:], ALU.add)
            e.drain()
            e.tensor_copy(S_b[:], S_f[:]).then_inc(s_be.h, 1)
        p.on("vector", b_dv2)
        be2 = s_be.bump()

        def b_pe3(e, be2=be2, ae_n=s_ae.n):
            e.wait_ge(s_be.h, be2)
            e.transpose(ps_t[:], oc[:], aux["ident"][:]).then_inc(s_bf.h, 1)
        p.on("tensor", b_pe3)
        bf3 = s_bf.bump()

        def b_dv3(e, csl=csl, bf3=bf3):
            e.wait_ge(s_bf.h, bf3)
            e.tensor_copy(olinT[:, csl], ps_t[:]).then_inc(s_be.h, 1)
        p.on("vector", b_dv3)
        s_be.bump()

    a_done_of = {}
    grp_done[0] = emit_group(0)
    a_done_of[0] = emit_A(0)
    for c in range(NCH):
        g_next = (c + 1) // GRP
        if c + 1 < NCH:
            if g_next not in grp_done:
                grp_done[g_next] = emit_group(g_next)
            a_done_of[c + 1] = emit_A(c + 1)
        emit_B(c, a_done_of[c])

    return s_be, s_be.n



def emit_attn_b_core(p: Prog, d, aux, deps, tag="b"):
    with p.scope():
        qT = p.sb(f"qT_{tag}", [128, 2, S], BF16)
        kT = p.sb(f"kT_{tag}", [128, 2, S], BF16)
        vT = p.sb(f"vT_{tag}", [128, 2, S], BF16)
        with p.scope():
            specs = [
                (d[f"wq_sl_{tag}"], 0, 2, qT),
                (d[f"wk_sl_{tag}"], 0, 2, kT),
                (d[f"wv_sl_{tag}"], 0, 2, vT),
            ]
            emit_attn_proj(p, d[f"cc_t_{tag}"], specs, deps, f"{tag}")
            bar1 = p.barrier()

        s_mx = p.sem("mx")
        mask4, m4dep = emit_aux_load(p, d["mask4"], [128, 4, 512], BF16, f"m4{tag}",
                                     deps=deps)
        olins = [p.sb(f"olinT_{tag}_{h}", [128, S], BF16) for h in range(2)]
        with p.scope():
            heads = [dict(qT=qT[:, h, :], kT=kT[:, h, :], vT=vT[:, h, :],
                          olinT=olins[h]) for h in range(2)]
            emit_delta_pair(p, heads, aux, [bar1], f"{tag}")
            bar_d = p.barrier(engines=("tensor", "vector", "scalar"))

        hdep = [bar_d, m4dep]
        for h in range(2):
            with p.scope():
                v_tm = p.sb(f"vtm_{tag}_{h}", [128, NCH, 128], BF16)
                with p.scope():
                    emit_transpose_fm_tm(p, vT[:, h, :], v_tm, NCH,
                                         aux["ident"], hdep, f"{tag}v{h}")
                    bar_tp = p.barrier(engines=("tensor", "vector"))
                outT = p.sb(f"mixT_{tag}_{h}", [128, S], BF16)
                s_fl, flv = emit_flash_head(p, qT[:, h, :], kT[:, h, :], v_tm, outT,
                                            aux["ones_bf"], aux["ones1_f"], mask4,
                                            hdep + [bar_tp], f"{tag}h{h}",
                                            olinT=olins[h], wbase=0.5, wlin=0.5,
                                            ones_f=aux["ones_f"])

                def mxo(e, h=h, outT=outT, flv=flv, s_fl=s_fl):
                    e.wait_ge(s_fl.h, flv)
                    for j in range(R):
                        e.dma_start(out=d[f"cc_mix_{tag}_in"][256 * j + 128 * h:256 * j + 128 * (h + 1), :],
                                    in_=outT[:, j * TOK:(j + 1) * TOK]).then_inc(s_mx.h, 16)
                p.on("scalar", mxo)
                s_mx.bump(16 * R)
                bar_h = p.barrier(engines=("tensor", "scalar", "vector"))
                hdep = [bar_h, m4dep]

        def agm(e, mxv=s_mx.n):
            from concourse.aot_env import allow_ici_all_to_all
            e.wait_ge(s_mx.h, mxv)
            with allow_ici_all_to_all():
                e.collective_compute(
                    "AllToAll", ALU.bypass,
                    replica_groups=[list(range(R))],
                    ins=[d[f"cc_mix_{tag}_in"]],
                    outs=[d[f"cc_mix_{tag}"]],
                ).then_inc(s_mx.h, 1)
        p.on("gpsimd", agm)
        mxv2 = s_mx.bump(1)
    return s_mx, mxv2



AUX_SPECS = [("ident", [128, 128], "bf16"), ("ones_bf", [128, 1], "bf16"),
             ("ones1_f", [1, 128], "f32"), ("ones1_bf", [1, 128], "bf16"),
             ("ones_f", [128, 1], "f32"), ("m_sl", [128, 128], "bf16"),
             ("m_su", [128, 128], "bf16"), ("m_iu", [128, 128], "bf16")]


def build_full(nc):
    import contextlib
    d = {}

    def din(name, shape, dt):
        d[name] = nc.dram_tensor(name, shape, dt, kind="ExternalInput").ap()

    din("x", [TOK, D], F32)
    din("consts", [128, MAX_CONSTS], F32)
    for nm, shape, dt in AUX_SPECS:
        din(nm, shape, BF16 if dt == "bf16" else F32)
    din("mask4", [128, 4, 512], BF16)
    din("wq_sl_a", [D, 256], F32)
    din("wk_sl_a", [D, 128], F32)
    din("wv_sl_a", [D, 128], F32)
    din("w_o_a", [D, D], F32)
    din("wq_sl_b", [D, 256], F32)
    din("wk_sl_b", [D, 256], F32)
    din("wv_sl_b", [D, 256], F32)
    din("w_o_b", [D, D], F32)
    din("gwT", [D, 8], F32)
    din("gate_wT", [N_EXP, D, F_EXP], F32)
    din("up_wT", [N_EXP, D, F_EXP], F32)
    din("down_wT", [N_EXP, F_EXP, D], F32)
    din("sh_gate_wT", [D, F_SH], F32)
    din("sh_up_wT", [D, F_SH], F32)
    din("sh_down_wT", [F_SH, D], F32)
    d["out"] = nc.dram_tensor("out", [TOK, D], F32, kind="ExternalOutput").ap()
    d["cc_t_a_in"] = nc.dram_tensor("cc_t_a_in", [TOK, D], BF16).ap()
    d["cc_t_a"] = nc.dram_tensor("cc_t_a", [S, D], BF16).ap()
    d["cc_mix_a_in"] = nc.dram_tensor("cc_mix_a_in", [D, TOK], BF16).ap()
    d["cc_mix_a"] = nc.dram_tensor("cc_mix_a", [D, TOK], BF16).ap()
    d["cc_t_b_in"] = nc.dram_tensor("cc_t_b_in", [TOK, D], BF16).ap()
    d["cc_t_b"] = nc.dram_tensor("cc_t_b", [S, D], BF16).ap()
    d["cc_mix_b_in"] = nc.dram_tensor("cc_mix_b_in", [D, TOK], BF16).ap()
    d["cc_mix_b"] = nc.dram_tensor("cc_mix_b", [D, TOK], BF16).ap()
    for t in ("f1", "f2"):
        d[f"trb_{t}"] = nc.dram_tensor(f"trb_{t}", [TOK, D], BF16).ap()
        d[f"trb_lo_{t}"] = nc.dram_tensor(f"trb_lo_{t}", [TOK, D], BF16).ap()
    d["xs_dram"] = nc.dram_tensor("xs_dram", [TOK, D], F32).ap()

    es = contextlib.ExitStack()
    with es:
        p = Prog(nc, es)
        p.init_consts(d["consts"])
        aux = {}
        s_aux = p.sem("aux_ld")
        for nm, shape, dt in AUX_SPECS:
            aux[nm] = p.sb(nm + "_s", shape, BF16 if dt == "bf16" else F32)

            def lf(e, nm=nm):
                e.dma_start(out=aux[nm][:], in_=d[nm]).then_inc(s_aux.h, 16)
            p.on("sync", lf)
            s_aux.bump(16)
        aux["eye_bf"] = aux["ident"]
        for eng in ("tensor", "vector", "scalar", "gpsimd"):
            p.wait(eng, s_aux, s_aux.n)

        st = p.sem("store")
        UPTO = "full"

        def early_store(x_buf, dep):
            def stf(e, dep=dep, stv=st.n + 16):
                e.wait_ge(dep[0].h, dep[1])
                e.dma_start(out=d["out"].rearrange("(t p) x -> p t x", p=128),
                            in_=x_buf[:]).then_inc(st.h, 16)
                e.wait_ge(st.h, stv)
            p.on("sync", stf)
            st.bump(16)

        with p.scope():
            x_sb = p.sb("x_sb", [128, NTT, D], F32)
            ld, ldv = emit_load_x(p, d["x"], x_sb)
            sra, vra = emit_attn_a(p, d, aux, x_sb, (ld, ldv), tag="a")
            bar_a = p.barrier()
            if UPTO == "attn_a":
                early_store(x_sb, bar_a)
                p.finalize()
                return d, p
            emit_ffn(p, d, aux, x_sb, [bar_a], "f1")
            bar_f1 = p.barrier()
            if UPTO == "ffn1":
                early_store(x_sb, bar_f1)
                p.finalize()
                return d, p
            with p.scope():
                t_bf3 = p.sb("t_b3", [128, NTT, D], BF16)
                s_ln3, ln3v = emit_ln(p, x_sb, t_bf3, [bar_f1], "t3")
                emit_allgather_tm(p, t_bf3, d["cc_t_b_in"], d["cc_t_b"],
                                  [(s_ln3, ln3v)], "bt")

                def spill(e, ln3v=ln3v):
                    e.wait_ge(s_ln3.h, ln3v)
                    e.dma_start(out=d["xs_dram"].rearrange("(t p) x -> p t x", p=128),
                                in_=x_sb[:]).then_inc(st.h, 16)
                p.on("sync", spill)
                st.bump(16)
                bar_x = p.barrier()

        if UPTO == "ln3":
            with p.scope():
                x_sb3 = p.sb("x_sb3", [128, NTT, D], F32)
                sld = p.sem("xreload")

                def rl(e, bx=bar_x):
                    e.wait_ge(bx[0].h, bx[1])
                    e.dma_start(out=x_sb3[:], in_=d["xs_dram"].rearrange("(t p) x -> p t x", p=128)).then_inc(sld.h, 16)
                p.on("sync", rl)
                early_store(x_sb3, (sld, sld.bump(16)))
                p.finalize()
                return d, p

        with p.scope():
            emit_attn_b_core(p, d, aux, [bar_x], "b")
            bar_b = p.barrier()

        if UPTO == "attn_b":
            with p.scope():
                x_sb3 = p.sb("x_sb3", [128, NTT, D], F32)
                sld = p.sem("xreload")

                def rl(e, bx=bar_b):
                    e.wait_ge(bx[0].h, bx[1])
                    e.dma_start(out=x_sb3[:], in_=d["xs_dram"].rearrange("(t p) x -> p t x", p=128)).then_inc(sld.h, 16)
                p.on("sync", rl)
                early_store(x_sb3, (sld, sld.bump(16)))
                p.finalize()
                return d, p

        with p.scope():
            x_sb2 = p.sb("x_sb2", [128, NTT, D], F32)
            with p.scope():
                s_rb, v_rb = emit_out_proj(p, d["cc_mix_b"], d["w_o_b"], d["xs_dram"], x_sb2,
                                           [bar_b], "bo")
                bar_bo = p.barrier()
            if UPTO == "outproj":
                early_store(x_sb2, bar_bo)
                p.finalize()
                return d, p
            s_xa2, xa2v = emit_ffn(p, d, aux, x_sb2, [bar_bo], "f2")

            def store(e, xa2v=xa2v, stv=st.n + 16):
                e.wait_ge(s_xa2.h, xa2v)
                e.dma_start(out=d["out"].rearrange("(t p) x -> p t x", p=128),
                            in_=x_sb2[:]).then_inc(st.h, 16)
                e.wait_ge(st.h, stv)
            p.on("sync", store)
            st.bump(16)
        p.finalize()
    return d, p


def make_in_maps(inputs, const_vals):
    import ml_dtypes
    x_full = np.asarray(inputs["L0_attn_a_x"][0], np.float32)
    w_q = np.asarray(inputs["L0_attn_a_w_q"], np.float32)
    w_kv = np.asarray(inputs["L0_attn_a_w_kv"], np.float32)
    w_dense = np.ascontiguousarray(np.asarray(inputs["L0_attn_a_w_dense"], np.float32))
    Wq = np.asarray(inputs["L0_attn_b_Wq"], np.float32)
    Wk = np.asarray(inputs["L0_attn_b_Wk"], np.float32)
    Wv = np.asarray(inputs["L0_attn_b_Wv"], np.float32)
    Wo = np.ascontiguousarray(np.asarray(inputs["L0_attn_b_Wo"], np.float32))
    gw = np.asarray(inputs["L0_ffn_gate_weight"], np.float32)
    gate_w = np.asarray(inputs["L0_ffn_gate_w"], np.float32)
    up_w = np.asarray(inputs["L0_ffn_up_w"], np.float32)
    down_w = np.asarray(inputs["L0_ffn_down_w"], np.float32)
    sh_gate = np.asarray(inputs["L0_ffn_sh_gate_w"], np.float32)
    sh_up = np.asarray(inputs["L0_ffn_sh_up_w"], np.float32)
    sh_down = np.asarray(inputs["L0_ffn_sh_down_w"], np.float32)

    cv = np.zeros((128, MAX_CONSTS), np.float32)
    cv[:, :len(const_vals)] = np.array(const_vals, np.float32)[None, :]
    ident = np.eye(128, dtype=ml_dtypes.bfloat16)
    ones_bf = np.ones((128, 1), ml_dtypes.bfloat16)
    ones1_f = np.ones((1, 128), np.float32)
    ones1_bf = np.ones((1, 128), ml_dtypes.bfloat16)
    ones_f = np.ones((128, 1), np.float32)
    mask4 = np.zeros((128, 4, 512), np.float32)
    for o in range(4):
        for kk in range(128):
            mask4[kk, o, :] = (np.arange(512) >= 128 * o + kk)
    mask4 = mask4.astype(ml_dtypes.bfloat16)
    m_sl = np.tril(np.ones((128, 128), np.float32), -1).astype(ml_dtypes.bfloat16)
    m_su = np.triu(np.ones((128, 128), np.float32), 1).astype(ml_dtypes.bfloat16)
    m_iu = np.triu(np.ones((128, 128), np.float32)).astype(ml_dtypes.bfloat16)

    shared = {
        "consts": cv, "ident": ident, "ones_bf": ones_bf, "ones1_f": ones1_f,
        "ones1_bf": ones1_bf, "ones_f": ones_f, "mask4": mask4, "m_sl": m_sl,
        "m_su": m_su, "m_iu": m_iu,
        "w_o_a": w_dense, "w_o_b": Wo,
        "gwT": np.ascontiguousarray(gw.T),
        "gate_wT": np.ascontiguousarray(gate_w.transpose(0, 2, 1)),
        "up_wT": np.ascontiguousarray(up_w.transpose(0, 2, 1)),
        "down_wT": np.ascontiguousarray(down_w.transpose(0, 2, 1)),
        "sh_gate_wT": np.ascontiguousarray(sh_gate.T),
        "sh_up_wT": np.ascontiguousarray(sh_up.T),
        "sh_down_wT": np.ascontiguousarray(sh_down.T),
    }
    in_maps = []
    for c in range(R):
        g = c // 2
        m = dict(shared)
        m["x"] = np.ascontiguousarray(x_full[c * TOK:(c + 1) * TOK])
        m["wq_sl_a"] = np.ascontiguousarray(w_q[:, 256 * c:256 * (c + 1)])
        m["wk_sl_a"] = np.ascontiguousarray(w_kv[:, 128 * g:128 * (g + 1)])
        m["wv_sl_a"] = np.ascontiguousarray(w_kv[:, 512 + 128 * g:512 + 128 * (g + 1)])
        m["wq_sl_b"] = np.ascontiguousarray(Wq[:, 256 * c:256 * (c + 1)])
        m["wk_sl_b"] = np.ascontiguousarray(Wk[:, 256 * c:256 * (c + 1)])
        m["wv_sl_b"] = np.ascontiguousarray(Wv[:, 256 * c:256 * (c + 1)])
        in_maps.append(m)
    return in_maps



_CACHE = {}


def kernel(**inputs):
    import concourse.bass as bass_mod
    from concourse.bass_utils import run_bass_kernel_spmd as _run
    if "nc" not in _CACHE:
        nc = bass_mod.Bass("TRN2", target_bir_lowering=False, debug=False)
        d, p = build_full(nc)
        _CACHE["nc"] = nc
        _CACHE["const_vals"] = list(p.const_vals)
    nc = _CACHE["nc"]
    in_maps = make_in_maps(inputs, _CACHE["const_vals"])
    res = _run(nc, in_maps, core_ids=list(range(R)))
    out = np.concatenate([np.asarray(res.results[c]["out"]) for c in range(R)], axis=0)
    return out[None].astype(np.float32)


# version 13 (final)
# speedup vs baseline: 1.1054x; 1.0062x over previous
import contextlib

import numpy as np

import concourse.bass as bass
import concourse.mybir as mybir

F32 = mybir.dt.float32
BF16 = mybir.dt.bfloat16
AF = mybir.ActivationFunctionType
ALU = mybir.AluOpType
AX = mybir.AxisListType

S = 8192
D = 2048
R = 8
TOK = S // R
NTT = TOK // 128
NDC = D // 128
DH = 128
NCH = S // 128
F_EXP = 1408
F_SH = 2816
N_EXP = 8
EPS_LN = 1e-5
ATT_SCALE = float(DH) ** -0.5

ENGINES = ("sync", "scalar", "vector", "tensor", "gpsimd")


class Sem:
    __slots__ = ("h", "n")

    def __init__(self, h):
        self.h = h
        self.n = 0

    def bump(self, k=1):
        self.n += k
        return self.n


MAX_CONSTS = 32


class Prog:

    def __init__(self, nc, es):
        self.nc = nc
        self.es = es
        self.scopes = [es]
        self.q = {e: [] for e in ENGINES}
        self._sems = {}
        self.const_vals = []
        self.const_tile = None
        self.const_dram = None

    def scope(self):
        import contextlib as _cl
        p = self

        class _Scope:
            def __enter__(self):
                self.es = _cl.ExitStack()
                p.scopes.append(self.es)
                return self.es

            def __exit__(self, *a):
                self.es.close()
                p.scopes.pop()
                return False
        return _Scope()

    def barrier(self, engines=ENGINES):
        s_bar = self.sem("bar")
        targets = tuple((s, s.n) for name, s in self._sems.items()
                        if s.n > 0 and name != "bar")
        for eng in engines:
            def f(e, targets=targets):
                for (s, v) in targets:
                    e.wait_ge(s.h, v)
                e.drain()
                e.sem_inc(s_bar.h, 1)
            self.on(eng, f)
        return s_bar, s_bar.bump(len(engines))

    def init_consts(self, const_dram):
        self.const_dram = const_dram
        self.const_tile = self.sb("const_pool", [128, MAX_CONSTS], F32)

    def constv(self, value):
        value = float(value)
        if value not in self.const_vals:
            assert len(self.const_vals) < MAX_CONSTS
            self.const_vals.append(value)
            j = self.const_vals.index(value)
            self.nc.const_aps.register(F32, value, self.const_tile[:, j:j + 1])
        return value

    SEM_POOL = 96

    def sem(self, name) -> Sem:
        if name not in self._sems:
            if not hasattr(self, "_pool"):
                self._pool = []
            if len(self._pool) < self.SEM_POOL:
                s = Sem(self.es.enter_context(self.nc.semaphore(name)))
                self._pool.append(s)
            else:
                s = self._pool[len(self._sems) % self.SEM_POOL]
            self._sems[name] = s
        return self._sems[name]

    def sb(self, name, shape, dtype):
        return self.scopes[-1].enter_context(self.nc.sbuf_tensor(name, shape, dtype))

    def ps(self, name, shape, dtype=F32):
        return self.scopes[-1].enter_context(self.nc.psum_tensor(name, shape, dtype))

    def on(self, eng, f):
        self.q[eng].append(f)

    def wait(self, eng, sem: Sem, v: int):
        def f(e, sem=sem, v=v):
            e.wait_ge(sem.h, v)
        self.on(eng, f)

    def finalize(self):
        if self.const_dram is not None:
            s_const = self.sem("const_ld")
            ctile = self.const_tile
            cdram = self.const_dram

            def cload(e):
                e.dma_start(out=ctile[:], in_=cdram).then_inc(s_const.h, 16)
            self.q["sync"].insert(0, cload)
            v = s_const.bump(16)
            for eng in ENGINES:
                if eng == "sync":
                    continue

                def cw(e, v=v):
                    e.wait_ge(s_const.h, v)
                self.q[eng].insert(0, cw)

        block = self.es.enter_context(self.nc.Block())

        def runner(eng):
            def body(e):
                for f in self.q[eng]:
                    f(e)
            body.__name__ = f"body_{eng}"
            return body

        block.sync(runner("sync"))
        block.scalar(runner("scalar"))
        block.vector(runner("vector"))
        block.tensor(runner("tensor"))
        block.gpsimd(runner("gpsimd"))



def emit_load_x(p: Prog, x_dram, x_sb):
    ld = p.sem("x_ld")
    src = x_dram.rearrange("(t p) d -> p t d", p=128)
    v = ld.bump(16)

    def f(e, src=src):
        e.dma_start(out=x_sb[:], in_=src).then_inc(ld.h, 16)
    p.on("sync", f)
    return ld, v



def emit_ln(p: Prog, x_sb, t_bf, deps, tag, t_lo=None):
    p.constv(EPS_LN)
    p.constv(-0.5)
    sq = p.sb(f"ln_sq_{tag}", [128, D], F32)
    ssq = p.sb(f"ln_ssq_{tag}", [128, 1], F32)
    sx = p.sb(f"ln_sx_{tag}", [128, 1], F32)
    mean = p.sb(f"ln_mean_{tag}", [128, 1], F32)
    var = p.sb(f"ln_var_{tag}", [128, 1], F32)
    rstd = p.sb(f"ln_rstd_{tag}", [128, 1], F32)
    tf = p.sb(f"ln_tf_{tag}", [128, D], F32) if t_lo is not None else None

    s_sq = p.sem("ln_sq")
    s_st = p.sem("ln_st")
    s_rs = p.sem("ln_rs")
    s_done = p.sem("ln_done")

    for i in range(NTT):
        xt = x_sb[:, i, :]

        st_war = s_st.n

        def act1(e, xt=xt, i=i, st_war=st_war):
            if i == 0:
                for (sm, v) in deps:
                    e.wait_ge(sm.h, v)
            else:
                e.wait_ge(s_st.h, st_war)
            e.activation(sq[:], xt, AF.Square, accum_out=ssq[:]).then_inc(s_sq.h, 1)
        p.on("scalar", act1)
        sq_v = s_sq.bump()

        def dve1(e, xt=xt, sq_v=sq_v, i=i):
            if i == 0:
                for (sm, v) in deps:
                    e.wait_ge(sm.h, v)
            e.reduce_sum(sx[:], xt, axis=AX.X)
            e.drain()
            e.wait_ge(s_sq.h, sq_v)
            e.tensor_scalar_mul(mean[:], sx[:], 1.0 / D)
            e.drain()
            e.tensor_tensor(sx[:], mean[:], mean[:], ALU.mult)
            e.tensor_scalar_mul(var[:], ssq[:], 1.0 / D)
            e.drain()
            e.tensor_tensor(var[:], var[:], sx[:], ALU.subtract).then_inc(s_st.h, 1)
        p.on("vector", dve1)
        st_v = s_st.bump()

        def act2(e, st_v=st_v):
            e.wait_ge(s_st.h, st_v)
            e.activation(var[:], var[:], AF.Ln, bias=EPS_LN)
            e.drain()
            e.activation(rstd[:], var[:], AF.Exp, scale=-0.5).then_inc(s_rs.h, 1)
        p.on("scalar", act2)
        rs_v = s_rs.bump()

        def dve2(e, xt=xt, i=i, rs_v=rs_v):
            e.wait_ge(s_rs.h, rs_v)
            if tf is not None:
                e.tensor_scalar(tf[:], xt, mean[:], rstd[:], ALU.subtract, ALU.mult)
                e.tensor_copy(t_bf[:, i, :], tf[:])
                e.tensor_tensor(t_lo[:, i, :], tf[:], t_bf[:, i, :],
                                ALU.subtract).then_inc(s_done.h, 1)
            else:
                e.tensor_scalar(t_bf[:, i, :], xt, mean[:], rstd[:],
                                ALU.subtract, ALU.mult).then_inc(s_done.h, 1)
        p.on("vector", dve2)
        s_done.bump()

    return s_done, s_done.n



def emit_allgather_tm(p: Prog, sb_tile, cc_in, cc_out, deps, tag):
    s_cc = p.sem("ag")

    def f(e):
        for (sm, v) in deps:
            e.wait_ge(sm.h, v)
        e.dma_start(out=cc_in.rearrange("(t p) d -> p t d", p=128), in_=sb_tile[:]).then_inc(s_cc.h, 16)
    p.on("scalar", f)
    v0 = s_cc.bump(16)

    def g(e, v0=v0):
        e.wait_ge(s_cc.h, v0)
        e.collective_compute(
            "AllGather", ALU.bypass,
            replica_groups=[list(range(R))],
            ins=[cc_in],
            outs=[cc_out],
        ).then_inc(s_cc.h, 1)
    p.on("gpsimd", g)
    v1 = s_cc.bump(1)
    return s_cc, v1



def emit_attn_proj(p: Prog, ag_t, specs, deps, tag):
    nm_tot = sum(n_m for (_, _, n_m, _) in specs)
    wsl = []
    s_w = p.sem("prj_w")
    for si, (w_dram, col0, n_m, outT) in enumerate(specs):
        for m in range(n_m):
            wt = p.sb(f"prj_w_{tag}_{si}_{m}", [128, NDC, 128], BF16)
            src = w_dram[:, col0 + m * 128: col0 + (m + 1) * 128].rearrange(
                "(a p) c -> p a c", p=128)

            def f(e, wt=wt, src=src):
                e.dma_start(out=wt[:], in_=src).then_inc(s_w.h, 16)
            p.on("gpsimd", f)
            s_w.bump(16)
            wsl.append((wt, outT, m))
    w_all = s_w.n

    NBUF_T = 1
    tT = [p.sb(f"prj_tT_{tag}_{b}", [128, NDC, TOK], BF16) for b in range(NBUF_T)]
    tT = [tT[b % NBUF_T] for b in range(2)]
    ps = [p.ps(f"prj_ps_{tag}_{b}", [128, 512]) for b in range(2)]
    s_tl = p.sem("prj_tl")
    s_mm = p.sem("prj_mm")
    s_ev = p.sem("prj_ev")
    s_done = p.sem("prj_done")

    n_psum_per_rank = nm_tot * (TOK // 512)
    base_ev = s_ev.n

    for r in range(R):
        buf = tT[r % 2]
        def loads(e, r=r, buf=buf, ev_war=base_ev + r * n_psum_per_rank):
            if r == 0:
                for (sm, v) in deps:
                    e.wait_ge(sm.h, v)
            else:
                e.wait_ge(s_ev.h, ev_war)
            for j in range(NDC):
                e.dma_start_transpose(
                    buf[:, j, :],
                    ag_t[r * TOK:(r + 1) * TOK, j * 128:(j + 1) * 128],
                ).then_inc(s_tl.h, 16)
        p.on("sync", loads)
        tl_v = s_tl.bump(16 * NDC)

        kk = 0
        for (wt, outT, m) in wsl:
            for nck in range(TOK // 512):
                gidx = r * n_psum_per_rank + kk
                bank = ps[gidx % 2]

                def mm(e, wt=wt, nck=nck, bank=bank, buf=buf, tl_v=tl_v,
                       ev_need=base_ev + max(0, gidx - 1), w_all=w_all, first=(kk == 0)):
                    if first:
                        e.wait_ge(s_w.h, w_all)
                        e.wait_ge(s_tl.h, tl_v)
                    e.wait_ge(s_ev.h, ev_need)
                    ins = None
                    for j in range(NDC):
                        ins = e.matmul(bank[:], wt[:, j, :],
                                       buf[:, j, nck * 512:(nck + 1) * 512],
                                       start=(j == 0), stop=(j == NDC - 1))
                    ins.then_inc(s_mm.h, 1)
                p.on("tensor", mm)
                mm_v = s_mm.bump()

                def ev(e, bank=bank, outT=outT, m=m, r=r, nck=nck, mm_v=mm_v):
                    e.wait_ge(s_mm.h, mm_v)
                    e.tensor_copy(outT[:, m, r * TOK + nck * 512: r * TOK + (nck + 1) * 512],
                                  bank[:]).then_inc(s_ev.h, 1)
                p.on("vector", ev)
                s_ev.bump()
                kk += 1

    return s_ev, s_ev.n



def emit_flash_head(p: Prog, qT, kT, v_tm, outT, ones_bf, ones1_f, mask4,
                    deps, tag, olinT=None, wbase=1.0, wlin=0.5, ones_f=None):
    p.constv(ATT_SCALE)
    NQB = S // 512
    BQ = 3
    ps_s = [p.ps(f"fl_s_{tag}_{b}", [128, BQ * 512]) for b in range(2)]
    ps_c = p.ps(f"fl_c_{tag}", [128, 512])
    ps_b = p.ps(f"fl_b_{tag}", [128, 512])
    ps_d = ps_b[0:1, :]
    pT = [p.sb(f"fl_p_{tag}_{b}", [128, BQ * 512], BF16) for b in range(2)]
    sacc = [p.sb(f"fl_sacc_{tag}_{b}", [128, 512], F32) for b in range(2)]
    stmp = [p.sb(f"fl_stmp_{tag}_{b}", [128, 512], BF16) for b in range(2)]
    den = p.sb(f"fl_den_{tag}", [1, 512], F32)
    bc = p.sb(f"fl_bc_{tag}", [128, 512], F32)
    tmp = p.sb(f"fl_tmp_{tag}", [128, 512], F32)

    s_sc = p.sem("fl_sc")
    s_exp = p.sem("fl_exp")
    s_pm = p.sem("fl_pm")
    s_cd = p.sem("fl_cd")
    s_no = p.sem("fl_no")

    base_pm = s_pm.n

    batches = []
    for qb in range(NQB):
        nkt = 4 * (qb + 1)
        kts = [list(range(b0, min(b0 + BQ, nkt))) for b0 in range(0, nkt, BQ)]
        for bi, batch in enumerate(kts):
            batches.append((qb, bi, len(kts), batch))
    NB = len(batches)

    pe_s = [None] * NB
    pe_cd = [None] * NB
    pe_nm = {}
    cd_v_of = {}

    for gbt, (qb, bi, nbt, batch) in enumerate(batches):
        b = gbt % 2
        nb = len(batch)

        def mm_s(e, b=b, batch=batch, qb=qb, pm_need=base_pm + max(0, gbt - 1),
                 first=(gbt == 0)):
            if first:
                for (sm, v) in deps:
                    e.wait_ge(sm.h, v)
            e.wait_ge(s_pm.h, pm_need)
            ins = None
            for i, kt in enumerate(batch):
                ins = e.matmul(ps_s[b][:, i * 512:(i + 1) * 512],
                               kT[:, kt * 128:(kt + 1) * 128],
                               qT[:, qb * 512:(qb + 1) * 512],
                               start=True, stop=True)
            ins.then_inc(s_sc.h, 1)
        pe_s[gbt] = mm_s
        sc_v = s_sc.bump()

        def act_e(e, b=b, nb=nb, sc_v=sc_v):
            e.wait_ge(s_sc.h, sc_v)
            e.activation(pT[b][:, 0:nb * 512], ps_s[b][:, 0:nb * 512], AF.Exp,
                         scale=ATT_SCALE).then_inc(s_exp.h, 1)
        p.on("scalar", act_e)
        exp_v = s_exp.bump()

        ab = bi % 2

        def dve_a(e, b=b, ab=ab, batch=batch, qb=qb, exp_v=exp_v,
                  first=(bi == 0), no_need=s_no.n):
            e.wait_ge(s_exp.h, exp_v)
            for i, kt in enumerate(batch):
                diag = kt - 4 * qb
                if diag >= 0:
                    e.tensor_tensor(pT[b][:, i * 512:(i + 1) * 512],
                                    pT[b][:, i * 512:(i + 1) * 512],
                                    mask4[:, diag, :], ALU.mult)
                    e.drain()
            nbq = len(batch)
            if nbq == 3:
                e.tensor_tensor(stmp[0][:], pT[b][:, 0:512], pT[b][:, 512:1024], ALU.add)
                e.drain()
                e.tensor_tensor(stmp[1][:], stmp[0][:], pT[b][:, 1024:1536], ALU.add)
                e.drain()
                part = stmp[1]
            elif nbq == 2:
                e.tensor_tensor(stmp[1][:], pT[b][:, 0:512], pT[b][:, 512:1024], ALU.add)
                e.drain()
                part = stmp[1]
            else:
                part = None
            if first:
                e.wait_ge(s_no.h, no_need)
                if part is None:
                    e.tensor_copy(sacc[ab][:], pT[b][:, 0:512])
                    e.drain()
                else:
                    e.tensor_copy(sacc[ab][:], part[:])
                    e.drain()
            else:
                e.tensor_tensor(sacc[ab][:], sacc[1 - ab][:],
                                part[:] if part is not None else pT[b][:, 0:512],
                                ALU.add)
                e.drain()
            e.engine_nop().then_inc(s_pm.h, 1)
        p.on("vector", dve_a)
        pm_v = s_pm.bump()

        def mm_cd(e, b=b, batch=batch, pm_v=pm_v, firstb=(bi == 0),
                  lastb=(bi == nbt - 1), no_need=s_no.n, ab=ab):
            e.wait_ge(s_pm.h, pm_v)
            if firstb:
                e.wait_ge(s_no.h, no_need)
            ins = None
            for i, kt in enumerate(batch):
                ins = e.matmul(ps_c[:], v_tm[:, kt, :],
                               pT[b][:, i * 512:(i + 1) * 512],
                               start=(firstb and i == 0),
                               stop=(lastb and i == len(batch) - 1))
            if lastb:
                ins2 = e.matmul(ps_d, ones_f[:], sacc[ab][:], start=True, stop=True)
                ins2.then_inc(s_cd.h, 1)
        pe_cd[gbt] = mm_cd
        if bi == nbt - 1:
            cd_v_of[qb] = s_cd.bump()
            cd_v = cd_v_of[qb]

            def dve_n1(e, cd_v=cd_v, wb=float(wbase)):
                e.wait_ge(s_cd.h, cd_v)
                e.tensor_copy(den[:], ps_d)
                e.drain()
                e.reciprocal(den[:], den[:])
                e.drain()
                e.tensor_scalar_mul(den[:], den[:], wb)
                e.drain()
                e.engine_nop().then_inc(s_no.h, 1)
            p.on("vector", dve_n1)
            no_v1 = s_no.bump()

            def mm_b(e, no_v1=no_v1):
                e.wait_ge(s_no.h, no_v1)
                e.matmul(ps_b[:], ones1_f[:], den[:], start=True,
                         stop=True).then_inc(s_cd.h, 1)
            pe_nm[qb] = mm_b
            cd_v2 = s_cd.bump()

            def dve_n2(e, qb=qb, cd_v2=cd_v2):
                e.wait_ge(s_cd.h, cd_v2)
                e.tensor_copy(bc[:], ps_b[:])
                e.drain()
                sl = slice(qb * 512, (qb + 1) * 512)
                if olinT is None:
                    e.tensor_tensor(outT[:, sl], ps_c[:], bc[:], ALU.mult).then_inc(s_no.h, 1)
                else:
                    e.tensor_tensor(tmp[:], ps_c[:], bc[:], ALU.mult)
                    e.drain()
                    e.scalar_tensor_tensor(outT[:, sl], olinT[:, sl], float(wlin), tmp[:],
                                           ALU.mult, ALU.add).then_inc(s_no.h, 1)
            p.on("vector", dve_n2)
            s_no.bump()

    if NB > 0:
        p.on("tensor", pe_s[0])
    if NB > 1:
        p.on("tensor", pe_s[1])
    for j, (qb, bi, nbt, batch) in enumerate(batches):
        p.on("tensor", pe_cd[j])
        if bi == nbt - 1:
            p.on("tensor", pe_nm[qb])
        if j + 2 < NB:
            p.on("tensor", pe_s[j + 2])

    return s_no, s_no.n



def emit_transpose_fm_tm(p: Prog, srcT, dst_tm, n, ident, deps, tag):
    ps = [p.ps(f"tp_{tag}_{b}", [128, 128], BF16) for b in range(2)]
    s_t = p.sem("tp_t")
    s_e = p.sem("tp_e")
    base_e = s_e.n
    for c in range(n):
        b = c % 2

        def mm(e, c=c, b=b, ev_need=base_e + max(0, c - 1), first=(c == 0)):
            if first:
                for (sm, v) in deps:
                    e.wait_ge(sm.h, v)
            e.wait_ge(s_e.h, ev_need)
            e.transpose(ps[b][:], srcT[:, c * 128:(c + 1) * 128], ident[:]).then_inc(s_t.h, 1)
        p.on("tensor", mm)
        t_v = s_t.bump()

        def ev(e, c=c, b=b, t_v=t_v):
            e.wait_ge(s_t.h, t_v)
            e.tensor_copy(dst_tm[:, c, :], ps[b][:]).then_inc(s_e.h, 1)
        p.on("vector", ev)
        s_e.bump()
    return s_e, s_e.n



def emit_out_proj(p: Prog, mixT_loc, w_dram, x_src, x_sb, deps, tag):
    ctxT = p.sb(f"op_ctx_{tag}", [128, NDC, TOK], BF16)
    w_sb = p.sb(f"op_w_{tag}", [128, NDC, D], BF16)
    ps_o = [p.ps(f"op_ps_{tag}_{b}", [128, D]) for b in range(2)]

    s_ld = p.sem("op_ld")
    s_w = p.sem("op_w")
    s_mm = p.sem("op_mm")
    s_res = p.sem("op_res")

    def loads(e):
        for (sm, v) in deps:
            e.wait_ge(sm.h, v)
        if x_src is not None:
            e.dma_start(out=x_sb[:], in_=x_src.rearrange("(t p) d -> p t d", p=128)).then_inc(s_ld.h, 16)
        for j in range(NDC):
            e.dma_start(out=ctxT[:, j, :],
                        in_=mixT_loc[j * 128:(j + 1) * 128, :]).then_inc(s_ld.h, 16)
    p.on("sync", loads)
    ld_v = s_ld.bump(16 * (NDC + (1 if x_src is not None else 0)))

    for grp in range(4):
        def wload(e, grp=grp):
            if grp == 0:
                for (sm, v) in deps:
                    e.wait_ge(sm.h, v)
            src = w_dram[grp * 512:(grp + 1) * 512, :].rearrange("(a p) d -> p a d", p=128)
            e.dma_start(out=w_sb[:, grp * 4:(grp + 1) * 4, :], in_=src).then_inc(s_w.h, 16)
        p.on("gpsimd", wload)
        s_w.bump(16)
    w_all = s_w.n

    base_res = s_res.n
    for tt in range(NTT):
        bank = ps_o[tt % 2]

        def mm(e, tt=tt, bank=bank, res_need=base_res + max(0, tt - 1), first=(tt == 0)):
            if first:
                e.wait_ge(s_ld.h, ld_v)
                e.wait_ge(s_w.h, w_all)
            e.wait_ge(s_res.h, res_need)
            ins = None
            for j in range(NDC):
                for nch in range(4):
                    ins = e.matmul(bank[:, nch * 512:(nch + 1) * 512],
                                   ctxT[:, j, tt * 128:(tt + 1) * 128],
                                   w_sb[:, j, nch * 512:(nch + 1) * 512],
                                   start=(j == 0), stop=(j == NDC - 1))
            ins.then_inc(s_mm.h, 1)
        p.on("tensor", mm)
        mm_v = s_mm.bump()

        def res(e, tt=tt, bank=bank, mm_v=mm_v):
            e.wait_ge(s_mm.h, mm_v)
            e.tensor_tensor(x_sb[:, tt, :], x_sb[:, tt, :], bank[:], ALU.add).then_inc(s_res.h, 1)
        p.on("vector", res)
        s_res.bump()

    return s_res, s_res.n



def emit_aux_load(p: Prog, dram_ap, shape, dtype, tag):
    sb = p.sb(f"auxl_{tag}", shape, dtype)
    s = p.sem("auxl")

    def lf(e):
        e.dma_start(out=sb[:], in_=dram_ap).then_inc(s.h, 16)
    p.on("sync", lf)
    return sb, (s, s.bump(16))


def emit_attn_a(p: Prog, d, aux, x_sb, xdep, tag="a"):
    with p.scope():
        qT = p.sb(f"qT_{tag}", [128, 2, S], BF16)
        kT = p.sb(f"kT_{tag}", [128, 1, S], BF16)
        vT = p.sb(f"vT_{tag}", [128, 1, S], BF16)

        with p.scope():
            t_bf = p.sb(f"t_{tag}", [128, NTT, D], BF16)
            s_ln, lnv = emit_ln(p, x_sb, t_bf, [xdep], f"{tag}")
            emit_allgather_tm(p, t_bf, d[f"cc_t_{tag}_in"], d[f"cc_t_{tag}"],
                              [(s_ln, lnv)], f"{tag}t")
            bar0 = p.barrier()

        with p.scope():
            specs = [
                (d[f"wq_sl_{tag}"], 0, 2, qT),
                (d[f"wk_sl_{tag}"], 0, 1, kT),
                (d[f"wv_sl_{tag}"], 0, 1, vT),
            ]
            emit_attn_proj(p, d[f"cc_t_{tag}"], specs, [bar0], f"{tag}")
            bar1 = p.barrier()

        with p.scope():
            v_tm = p.sb(f"vtm_{tag}", [128, NCH, 128], BF16)
            mask4, m4dep = emit_aux_load(p, d["mask4"], [128, 4, 512], BF16, f"m4{tag}",
                                         deps=[bar1])
            with p.scope():
                emit_transpose_fm_tm(p, vT[:, 0, :], v_tm, NCH, aux["ident"],
                                     [bar1], f"{tag}v")
                bar_tp = p.barrier(engines=("tensor", "vector"))
            s_mx = p.sem("mx")
            hdep = [bar1, bar_tp, m4dep]
            for h in range(2):
                with p.scope():
                    outT = p.sb(f"mixT_{tag}_{h}", [128, S], BF16)
                    s_fl, flv = emit_flash_head(p, qT[:, h, :], kT[:, 0, :], v_tm, outT,
                                                aux["ones_bf"], aux["ones1_f"], mask4,
                                                hdep, f"{tag}h{h}", ones_f=aux["ones_f"])

                    def mxo(e, h=h, outT=outT, flv=flv, s_fl=s_fl):
                        e.wait_ge(s_fl.h, flv)
                        for j in range(R):
                            e.dma_start(out=d[f"cc_mix_{tag}_in"][256 * j + 128 * h:256 * j + 128 * (h + 1), :],
                                        in_=outT[:, j * TOK:(j + 1) * TOK]).then_inc(s_mx.h, 16)
                    p.on("scalar", mxo)
                    s_mx.bump(16 * R)
                    bar_h = p.barrier(engines=("tensor", "scalar", "vector"))
                    hdep = [bar_h]

            def agm(e, mxv=s_mx.n):
                from concourse.aot_env import allow_ici_all_to_all
                e.wait_ge(s_mx.h, mxv)
                with allow_ici_all_to_all():
                    e.collective_compute(
                        "AllToAll", ALU.bypass,
                        replica_groups=[list(range(R))],
                        ins=[d[f"cc_mix_{tag}_in"]],
                        outs=[d[f"cc_mix_{tag}"]],
                    ).then_inc(s_mx.h, 1)
            p.on("gpsimd", agm)
            s_mx.bump(1)

        bar2 = p.barrier()

    with p.scope():
        s_res, resv = emit_out_proj(p, d[f"cc_mix_{tag}"], d[f"w_o_{tag}"], None,
                                    x_sb, [bar2], f"{tag}")
    return s_res, resv






def emit_ln_dram(p: Prog, x_sb, trb, trb_lo, deps, tag):
    p.constv(EPS_LN)
    p.constv(-0.5)
    sq = p.sb(f"lnd_sq_{tag}", [128, D], F32)
    ssq = p.sb(f"lnd_ssq_{tag}", [128, 1], F32)
    sx = p.sb(f"lnd_sx_{tag}", [128, 1], F32)
    mean = p.sb(f"lnd_mean_{tag}", [128, 1], F32)
    var = p.sb(f"lnd_var_{tag}", [128, 1], F32)
    rstd = p.sb(f"lnd_rstd_{tag}", [128, 1], F32)
    tf = p.sb(f"lnd_tf_{tag}", [128, D], F32)
    stage = [p.sb(f"lnd_st_{tag}_{b}", [128, D], BF16) for b in range(2)]
    stage_lo = [p.sb(f"lnd_sl_{tag}_{b}", [128, D], BF16) for b in range(2)] if trb_lo is not None else None
    ndma = 2 if trb_lo is not None else 1

    s_sq = p.sem("lnd_sq")
    s_st = p.sem("lnd_st")
    s_rs = p.sem("lnd_rs")
    s_done = p.sem("lnd_done")
    s_tb = p.sem("lnd_tb")
    base_tb = s_tb.n

    trb_r = trb.rearrange("(t p) x -> t p x", p=128)
    trb_lo_r = trb_lo.rearrange("(t p) x -> t p x", p=128) if trb_lo is not None else None

    for i in range(NTT):
        xt = x_sb[:, i, :]
        st_war = s_st.n

        def act1(e, xt=xt, i=i, st_war=st_war):
            if i == 0:
                for (sm, v) in deps:
                    e.wait_ge(sm.h, v)
            else:
                e.wait_ge(s_st.h, st_war)
            e.activation(sq[:], xt, AF.Square, accum_out=ssq[:]).then_inc(s_sq.h, 1)
        p.on("scalar", act1)
        sq_v = s_sq.bump()

        def dve1(e, xt=xt, sq_v=sq_v, i=i):
            if i == 0:
                for (sm, v) in deps:
                    e.wait_ge(sm.h, v)
            e.reduce_sum(sx[:], xt, axis=AX.X)
            e.drain()
            e.wait_ge(s_sq.h, sq_v)
            e.tensor_scalar_mul(mean[:], sx[:], 1.0 / D)
            e.drain()
            e.tensor_tensor(sx[:], mean[:], mean[:], ALU.mult)
            e.tensor_scalar_mul(var[:], ssq[:], 1.0 / D)
            e.drain()
            e.tensor_tensor(var[:], var[:], sx[:], ALU.subtract).then_inc(s_st.h, 1)
        p.on("vector", dve1)
        st_v = s_st.bump()

        def act2(e, st_v=st_v):
            e.wait_ge(s_st.h, st_v)
            e.activation(var[:], var[:], AF.Ln, bias=EPS_LN)
            e.drain()
            e.activation(rstd[:], var[:], AF.Exp, scale=-0.5).then_inc(s_rs.h, 1)
        p.on("scalar", act2)
        rs_v = s_rs.bump()

        def dve2(e, xt=xt, i=i, rs_v=rs_v,
                 tb_war=base_tb + i * 16 * ndma):
            e.wait_ge(s_rs.h, rs_v)
            e.wait_ge(s_tb.h, tb_war)
            b = i % 2
            if trb_lo is not None:
                e.tensor_scalar(tf[:], xt, mean[:], rstd[:], ALU.subtract, ALU.mult)
                e.drain()
                e.tensor_copy(stage[b][:], tf[:])
                e.drain()
                e.tensor_tensor(stage_lo[b][:], tf[:], stage[b][:],
                                ALU.subtract).then_inc(s_done.h, 1)
            else:
                e.tensor_scalar(stage[b][:], xt, mean[:], rstd[:],
                                ALU.subtract, ALU.mult).then_inc(s_done.h, 1)
        p.on("vector", dve2)
        dn_v = s_done.bump()

        def tdma(e, i=i, dn_v=dn_v):
            e.wait_ge(s_done.h, dn_v)
            b = i % 2
            e.dma_start(out=trb_r[i], in_=stage[b][:]).then_inc(s_tb.h, 16)
            if trb_lo is not None:
                e.dma_start(out=trb_lo_r[i], in_=stage_lo[b][:]).then_inc(s_tb.h, 16)
        p.on("scalar", tdma)
        s_tb.bump(16 * ndma)

    return s_tb, s_tb.n


def emit_ffn(p: Prog, d, aux, x_sb, deps, tag, upto="full", dbg=None):
    BIG = 1.0e9
    NM_E = F_EXP // 128

    tT = p.sb(f"ffn_tT_{tag}", [128, NDC, TOK], BF16)
    wrows = p.sb(f"ffn_wr_{tag}", [1, N_EXP, TOK], BF16)
    s_tb = p.sem("ffn_tb")
    s_tt = p.sem("ffn_tt")
    s_rt = p.sem("ffn_rt")

    with p.scope():
        s_tbl, tbv = emit_ln_dram(p, x_sb, d[f"trb_{tag}"], d[f"trb_lo_{tag}"],
                                  deps, f"{tag}")

        def tbfwd(e, tbv=tbv):
            e.wait_ge(s_tbl.h, tbv)
            e.sem_inc(s_tb.h, 1)
        p.on("scalar", tbfwd)
        tbv = s_tb.bump(1)
        p.barrier(engines=("vector", "scalar"))

    def treads(e, tbv=tbv):
        e.wait_ge(s_tb.h, tbv)
        for j in range(NDC):
            e.dma_start_transpose(tT[:, j, :],
                                  d[f"trb_{tag}"][:, j * 128:(j + 1) * 128]).then_inc(s_tt.h, 16)
    p.on("sync", treads)
    ttv = s_tt.bump(16 * NDC)

    with p.scope():
        tT_lo = p.sb(f"ffn_tTlo_{tag}", [128, NDC, TOK], BF16)

        def treads_lo(e, tbv=tbv):
            e.wait_ge(s_tb.h, tbv)
            for j in range(NDC):
                e.dma_start_transpose(tT_lo[:, j, :],
                                      d[f"trb_lo_{tag}"][:, j * 128:(j + 1) * 128]).then_inc(s_tt.h, 16)
        p.on("sync", treads_lo)
        ttv_lo = s_tt.bump(16 * NDC)

        gw_f = p.sb(f"ffn_gwf_{tag}", [128, NDC, 8], F32)
        gw_hi = p.sb(f"ffn_gwh_{tag}", [128, NDC, 8], BF16)
        gw_lo = p.sb(f"ffn_gwl_{tag}", [128, NDC, 8], BF16)

        def gwload(e):
            e.dma_start(out=gw_f[:], in_=d["gwT"].rearrange("(a p) n -> p a n", p=128)).then_inc(s_rt.h, 16)
        p.on("sync", gwload)
        rt_gw = s_rt.bump(16)

        def gwsplit(e, rt_gw=rt_gw):
            e.wait_ge(s_rt.h, rt_gw)
            e.tensor_copy(gw_hi[:], gw_f[:])
            e.drain()
            e.tensor_tensor(gw_lo[:], gw_f[:], gw_hi[:], ALU.subtract).then_inc(s_rt.h, 1)
        p.on("vector", gwsplit)
        rt_sp = s_rt.bump()

        ps_lg = p.ps(f"ffn_pslg_{tag}", [128, 8])
        ps_wr = p.ps(f"ffn_pswr_{tag}", [16, 128], BF16)
        lg = p.sb(f"ffn_lg_{tag}", [128, 8], F32)
        elg = p.sb(f"ffn_elg_{tag}", [128, 8], F32)
        ssum = p.sb(f"ffn_ssum_{tag}", [128, 1], F32)
        sc = p.sb(f"ffn_sc_{tag}", [128, 8], F32)
        gs = p.sb(f"ffn_gs_{tag}", [128, 4], F32)
        gx = p.sb(f"ffn_gx_{tag}", [128, 4], F32)
        mcol = p.sb(f"ffn_mcol_{tag}", [128, 1], F32)
        sm8 = p.sb(f"ffn_sm8_{tag}", [128, 8], F32)
        msk = p.sb(f"ffn_msk_{tag}", [128, 8], F32)
        sel = p.sb(f"ffn_sel_{tag}", [128, 8], F32)
        wm = p.sb(f"ffn_wm_{tag}", [128, 8], F32)
        wm_pad = p.sb(f"ffn_wmpad_{tag}", [128, 128], BF16)
        s_lg = p.sem("ffn_lg")
        s_wr = p.sem("ffn_wrow")

        for tt in range(NTT):
            def mml(e, tt=tt, first=(tt == 0), rt_sp=rt_sp, ttv=ttv, ttv_lo=ttv_lo,
                    lg_done=s_lg.n):
                if first:
                    e.wait_ge(s_rt.h, rt_sp)
                    e.wait_ge(s_tt.h, ttv_lo)
                else:
                    e.wait_ge(s_lg.h, lg_done)
                ins = None
                for j in range(NDC):
                    sl = tT[:, j, tt * 128:(tt + 1) * 128]
                    sl_lo = tT_lo[:, j, tt * 128:(tt + 1) * 128]
                    ins = e.matmul(ps_lg[:], sl, gw_hi[:, j, :], start=(j == 0), stop=False)
                    ins = e.matmul(ps_lg[:], sl, gw_lo[:, j, :], start=False, stop=False)
                    ins = e.matmul(ps_lg[:], sl_lo, gw_hi[:, j, :], start=False,
                                   stop=(j == NDC - 1))
                ins.then_inc(s_lg.h, 1)
            p.on("tensor", mml)
            lg_v = s_lg.bump()

            def route(e, lg_v=lg_v):
                e.wait_ge(s_lg.h, lg_v)
                e.tensor_copy(lg[:], ps_lg[:])
                e.drain()
                e.engine_nop().then_inc(s_lg.h, 1)
            p.on("vector", route)
            lg_cp = s_lg.bump()

            def act_exp(e, lg_cp=lg_cp):
                e.wait_ge(s_lg.h, lg_cp)
                e.activation(elg[:], lg[:], AF.Exp, accum_out=ssum[:]).then_inc(s_rt.h, 1)
            p.on("scalar", act_exp)
            rt_e = s_rt.bump()

            def route2(e, rt_e=rt_e):
                e.wait_ge(s_rt.h, rt_e)
                e.reciprocal(ssum[:], ssum[:])
                e.drain()
                e.tensor_scalar_mul(sc[:], elg[:], ssum[:])
                e.drain()
                e.tensor_tensor(gs[:], sc[:, 0:8:2], sc[:, 1:8:2], ALU.max)
                e.drain()
                e.reduce_max(mcol[:], gs[:], axis=AX.X)
                e.drain()
                e.tensor_scalar(gx[:], gs[:], mcol[:], None, ALU.is_ge)
                e.drain()
                e.scalar_tensor_tensor(gx[:], gx[:], -BIG, gs[:], ALU.mult, ALU.add)
                e.drain()
                e.reduce_max(mcol[:], gx[:], axis=AX.X)
                e.drain()
                e.tensor_scalar(gx[:], gs[:], mcol[:], None, ALU.is_ge)
                e.drain()
                e.tensor_copy(sm8[:, 0:8:2], gx[:])
                e.drain()
                e.tensor_copy(sm8[:, 1:8:2], gx[:])
                e.drain()
                e.tensor_tensor(msk[:], sc[:], sm8[:], ALU.mult)
                e.drain()
                e.reduce_max(mcol[:], msk[:], axis=AX.X)
                e.drain()
                e.tensor_scalar(sel[:], msk[:], mcol[:], None, ALU.is_ge)
                e.drain()
                e.scalar_tensor_tensor(sel[:], sel[:], -BIG, msk[:], ALU.mult, ALU.add)
                e.drain()
                e.reduce_max(mcol[:], sel[:], axis=AX.X)
                e.drain()
                e.tensor_scalar(sel[:], msk[:], mcol[:], None, ALU.is_ge)
                e.drain()
                e.tensor_tensor(wm[:], sel[:], msk[:], ALU.mult)
                e.drain()
                e.memset(wm_pad[:], 0.0)
                e.drain()
                e.tensor_copy(wm_pad[:, 0:128:16], wm[:]).then_inc(s_rt.h, 1)
            p.on("vector", route2)
            rt_w = s_rt.bump()

            if upto == "logits":
                continue
            for n in range(N_EXP):
                def wrt(e, n=n, rt_w=rt_w, wr_prev=s_wr.n):
                    e.wait_ge(s_rt.h, rt_w)
                    if wr_prev > 0:
                        e.wait_ge(s_wr.h, wr_prev)
                    e.transpose(ps_wr[:], wm_pad[:, 16 * n:16 * (n + 1)],
                                aux["ident"][:]).then_inc(s_rt.h, 1)
                p.on("tensor", wrt)
                rt_t = s_rt.bump()

                def wre(e, n=n, tt=tt, rt_t=rt_t):
                    e.wait_ge(s_rt.h, rt_t)
                    e.tensor_copy(wrows[0:1, n, tt * 128:(tt + 1) * 128],
                                  ps_wr[0:1, :]).then_inc(s_wr.h, 1)
                p.on("vector", wre)
                s_wr.bump()
            if dbg is not None and "wm_to_x" in dbg:
                def wmdump(e, tt=tt, rt_w=rt_w):
                    e.wait_ge(s_rt.h, rt_w)
                    e.tensor_copy(x_sb[:, tt, 0:8], wm[:])
                    e.drain()
                    e.tensor_copy(x_sb[:, tt, 8:16], sc[:])
                    e.drain()
                    e.tensor_copy(x_sb[:, tt, 16:20], gs[:])
                    e.drain()
                    e.tensor_copy(x_sb[:, tt, 20:24], gx[:])
                    e.drain()
                    e.tensor_copy(x_sb[:, tt, 24:32], sm8[:]).then_inc(s_rt.h, 1)
                p.on("vector", wmdump)
                s_rt.bump(1)
        wr_all = s_wr.n
        if upto in ("router", "logits"):
            return s_rt, s_rt.n
        bar_b = p.barrier(engines=("tensor", "vector", "scalar"))

    with p.scope():
        NRING = 2
        wg_ring = [p.sb(f"ffn_wg_{tag}_{i}", [128, NDC, 128], BF16) for i in range(NRING)]
        wu_ring = [p.sb(f"ffn_wu_{tag}_{i}", [128, NDC, 128], BF16) for i in range(NRING)]
        wd_sb = p.sb(f"ffn_wd_{tag}", [128, NM_E, D], BF16)
        act_sb = p.sb(f"ffn_act_{tag}", [128, NM_E, TOK], BF16)
        bc_bf = p.sb(f"ffn_bc_{tag}", [128, TOK], BF16)
        sg_bf = [p.sb(f"ffn_sg_{tag}_{i}", [128, TOK], BF16) for i in range(2)]
        a1 = [p.sb(f"ffn_a1_{tag}_{i}", [128, TOK], BF16) for i in range(2)]
        ps_g = [p.ps(f"ffn_psg_{tag}_{i}", [128, TOK]) for i in range(2)]
        ps_u = [p.ps(f"ffn_psu_{tag}_{i}", [128, TOK]) for i in range(2)]

        s_wlp = [p.sem("ffn_wl0"), p.sem("ffn_wl1")]
        s_mf = p.sem("ffn_mf")
        s_ae = p.sem("ffn_ae")
        s_av = p.sem("ffn_av")
        s_bc = p.sem("ffn_bc")
        s_dn = p.sem("ffn_dn")
        s_xa = p.sem("ffn_xa")
        s_dl = p.sem("ffn_dl")
        base_wlp = [s_wlp[0].n, s_wlp[1].n]
        base_mf, base_dn, base_dl = s_mf.n, s_dn.n, s_dl.n

        blocks = [("e", n) for n in range(N_EXP)] + [("s", h) for h in range(2)]
        SKIP_DOWN = False

        mseq = 0
        for bi, (kind, n) in enumerate(blocks):
            for m in range(NM_E):
                for (ring, wsel) in ((wg_ring, "g"), (wu_ring, "u")):
                    if kind == "e":
                        wtns = d["gate_wT"] if wsel == "g" else d["up_wT"]
                        src = wtns[n, :, m * 128:(m + 1) * 128]
                    else:
                        wtns = d["sh_gate_wT"] if wsel == "g" else d["sh_up_wT"]
                        c0 = n * F_EXP + m * 128
                        src = wtns[:, c0:c0 + 128]
                    src = src.rearrange("(a p) c -> p a c", p=128)

                    def wl(e, src=src, ring=ring, mseq=mseq, bar_b=bar_b,
                           sw=s_wlp[mseq % 2], need=base_mf + max(0, mseq - 1)):
                        e.wait_ge(bar_b[0].h, bar_b[1])
                        if need > base_mf:
                            e.wait_ge(s_mf.h, need)
                        e.dma_start(out=ring[mseq % NRING][:], in_=src).then_inc(sw.h, 16)
                    p.on("gpsimd", wl)
                    s_wlp[mseq % 2].bump(16)
                mseq += 1
            for k in range(NM_E):
                if kind == "e":
                    dsrc = d["down_wT"][n, k * 128:(k + 1) * 128, :]
                else:
                    r0 = n * F_EXP + k * 128
                    dsrc = d["sh_down_wT"][r0:r0 + 128, :]

                def dl(e, dsrc=dsrc, k=k, bi=bi, bar_b=bar_b,
                       dn_need=base_dn + bi * NTT):
                    e.wait_ge(bar_b[0].h, bar_b[1])
                    if bi > 0:
                        e.wait_ge(s_dn.h, dn_need)
                    e.dma_start(out=wd_sb[:, k, :], in_=dsrc).then_inc(s_dl.h, 16)
                p.on("gpsimd", dl)
                s_dl.bump(16)

        mseq = 0
        for bi, (kind, n) in enumerate(blocks):
            routed = (kind == "e")
            if routed:
                def bcast(e, n=n, wr_all=wr_all, bar_b=bar_b, bi=bi,
                          av_need=s_av.n, xa_need=s_xa.n):
                    e.wait_ge(s_wr.h, wr_all)
                    e.wait_ge(bar_b[0].h, bar_b[1])
                    if bi > 0:
                        e.wait_ge(s_av.h, av_need)
                        e.wait_ge(s_xa.h, xa_need)
                    for nck in range(TOK // 512):
                        e.matmul(ps_g[0][:, nck * 512:(nck + 1) * 512], aux["ones1_bf"][:],
                                 wrows[0:1, n, nck * 512:(nck + 1) * 512],
                                 start=True, stop=True).then_inc(s_bc.h, 1)
                p.on("tensor", bcast)
                bc_f = s_bc.bump(2)

                def bccp(e, bc_f=bc_f):
                    e.wait_ge(s_bc.h, bc_f)
                    e.activation(bc_bf[:], ps_g[0][:], AF.Copy).then_inc(s_bc.h, 1)
                p.on("scalar", bccp)
                bc_c = s_bc.bump()
            else:
                bc_c = s_bc.n

            for m in range(NM_E):
                mb = mseq % 2
                wg = wg_ring[mseq % NRING]
                wu = wu_ring[mseq % NRING]

                def mmg(e, mb=mb, wg=wg, wu=wu, sw=s_wlp[mseq % 2],
                        wl_need=base_wlp[mseq % 2] + 32 * (mseq // 2 + 1),
                        bc_c=bc_c, m=m, routed=routed, bi=bi,
                        av_need=max(0, s_av.n - 1), xa_need=s_xa.n if m <= 1 and bi > 0 else 0):
                    e.wait_ge(sw.h, wl_need)
                    if m == 0 and routed:
                        e.wait_ge(s_bc.h, bc_c)
                    if xa_need > 0:
                        e.wait_ge(s_xa.h, xa_need)
                    e.wait_ge(s_av.h, av_need)
                    ins = None
                    for j in range(NDC):
                        for nck in range(TOK // 512):
                            ins = e.matmul(ps_g[mb][:, nck * 512:(nck + 1) * 512],
                                           wg[:, j, :], tT[:, j, nck * 512:(nck + 1) * 512],
                                           start=(j == 0), stop=(j == NDC - 1))
                    for j in range(NDC):
                        for nck in range(TOK // 512):
                            ins = e.matmul(ps_u[mb][:, nck * 512:(nck + 1) * 512],
                                           wu[:, j, :], tT[:, j, nck * 512:(nck + 1) * 512],
                                           start=(j == 0), stop=(j == NDC - 1))
                    ins.then_inc(s_mf.h, 1)
                p.on("tensor", mmg)
                mf_v = s_mf.bump()

                def silu(e, mb=mb, mf_v=mf_v):
                    e.wait_ge(s_mf.h, mf_v)
                    e.activation(sg_bf[mb][:], ps_g[mb][:], AF.Silu).then_inc(s_ae.h, 1)
                p.on("scalar", silu)
                ae_v = s_ae.bump()

                def acte(e, m=m, mb=mb, ae_v=ae_v, routed=routed,
                         dn_need=base_dn + bi * NTT):
                    e.wait_ge(s_ae.h, ae_v)
                    if dn_need > base_dn:
                        e.wait_ge(s_dn.h, dn_need)
                    if routed:
                        e.tensor_tensor(a1[mb][:], sg_bf[mb][:], ps_u[mb][:], ALU.mult)
                        e.drain()
                        e.tensor_tensor(act_sb[:, m, :], a1[mb][:], bc_bf[:],
                                        ALU.mult).then_inc(s_av.h, 1)
                    else:
                        e.tensor_tensor(act_sb[:, m, :], sg_bf[mb][:], ps_u[mb][:],
                                        ALU.mult).then_inc(s_av.h, 1)
                p.on("vector", acte)
                s_av.bump()
                mseq += 1

            for tt in range(NTT if not SKIP_DOWN else 0):
                def mmd(e, tt=tt, av_all=s_av.n, dl_need=base_dl + 16 * NM_E * (bi + 1),
                        xa_need=s_xa.n):
                    e.wait_ge(s_av.h, av_all)
                    e.wait_ge(s_dl.h, dl_need)
                    if xa_need > 0:
                        e.wait_ge(s_xa.h, xa_need)
                    ins = None
                    if True:
                        for k in range(NM_E):
                            for nch in range(4):
                                bank = ps_g[nch // 2]
                                tgt = bank[:, (nch % 2) * 512:(nch % 2 + 1) * 512]
                                ins = e.matmul(tgt, act_sb[:, k, tt * 128:(tt + 1) * 128],
                                               wd_sb[:, k, nch * 512:(nch + 1) * 512],
                                               start=(k == 0), stop=(k == NM_E - 1))
                    ins.then_inc(s_dn.h, 1)
                p.on("tensor", mmd)
                dn_v = s_dn.bump()

                def xadd(e, tt=tt, dn_v=dn_v):
                    e.wait_ge(s_dn.h, dn_v)
                    e.tensor_tensor(x_sb[:, tt, 0:TOK], x_sb[:, tt, 0:TOK],
                                    ps_g[0][:], ALU.add)
                    e.drain()
                    e.tensor_tensor(x_sb[:, tt, TOK:D], x_sb[:, tt, TOK:D],
                                    ps_g[1][:], ALU.add).then_inc(s_xa.h, 1)
                p.on("vector", xadd)
                s_xa.bump()

    return s_xa, s_xa.n



def emit_delta_pair(p: Prog, heads, aux, deps, tag):
    p.constv(ATT_SCALE)
    GRP = 8
    NG = NCH // GRP
    GW = GRP * 128
    NH = len(heads)

    s_tf = p.sem("dl_tf")
    s_af = p.sem("dl_af")
    s_ae = p.sem("dl_ae")
    s_aa = p.sem("dl_aa")
    s_bf = p.sem("dl_bf")
    s_be = p.sem("dl_be")
    base_be0 = s_be.n

    ctxs = []
    for h in range(NH):
        c = dict(heads[h])
        c["qh_g"] = [p.sb(f"dl_qh_{tag}{h}_{b}", [128, GW], BF16) for b in range(2)]
        c["kt_g"] = [p.sb(f"dl_kt_{tag}{h}_{b}", [128, GW], BF16) for b in range(2)]
        c["kb_g"] = [p.sb(f"dl_kb_{tag}{h}_{b}", [128, GW], BF16) for b in range(2)]
        c["vb_g"] = [p.sb(f"dl_vb_{tag}{h}_{b}", [128, GW], BF16) for b in range(2)]
        c["vb_tm"] = p.sb(f"dl_vbtm_{tag}{h}", [128, 128], BF16)
        c["kb_tm"] = p.sb(f"dl_kbtm_{tag}{h}", [128, 128], BF16)
        c["Lm"] = p.sb(f"dl_L_{tag}{h}", [128, 128], BF16)
        c["LTm"] = p.sb(f"dl_LT_{tag}{h}", [128, 128], BF16)
        c["Pw"] = [p.sb(f"dl_pw_{tag}{h}_{i}", [128, 128], BF16) for i in range(5)]
        c["PwT"] = [p.sb(f"dl_pwt_{tag}{h}_{i}", [128, 128], BF16) for i in range(2)]
        c["Y"] = p.sb(f"dl_Y_{tag}{h}", [128, 128], BF16)
        c["TT"] = p.sb(f"dl_TT_{tag}{h}", [128, 128], BF16)
        c["u_sb"] = [p.sb(f"dl_u_{tag}{h}_{b}", [128, 128], BF16) for b in range(2)]
        c["wT_sb"] = [p.sb(f"dl_w_{tag}{h}_{b}", [128, 128], BF16) for b in range(2)]
        c["at_sb"] = [p.sb(f"dl_at_{tag}{h}_{b}", [128, 128], BF16) for b in range(2)]
        c["ktm_sb"] = [p.sb(f"dl_ktm_{tag}{h}_{b}", [128, 128], BF16) for b in range(2)]
        c["u2"] = p.sb(f"dl_u2_{tag}{h}", [128, 128], BF16)
        c["oc"] = p.sb(f"dl_oc_{tag}{h}", [128, 128], BF16)
        c["S_f"] = p.sb(f"dl_Sf_{tag}{h}", [128, 128], F32)
        c["S_b"] = p.sb(f"dl_Sb_{tag}{h}", [128, 128], BF16)
        c["ps_a"] = p.ps(f"dl_psa_{tag}{h}", [128, 256])
        c["ps_t"] = p.ps(f"dl_pst_{tag}{h}", [128, 128], BF16)
        c["ps_b"] = p.ps(f"dl_psb_{tag}{h}", [128, 256])
        c["grp_done"] = {}
        c["a_done"] = {}
        c["ae_own"] = 0
        c["aa_own"] = 0
        c["sb_ready"] = 0
        ctxs.append(c)
    eq = p.sb(f"dl_eq_{tag}", [128, GW], BF16)
    tmpg = p.sb(f"dl_tmpg_{tag}", [128, GW], F32)
    den = p.sb(f"dl_den_{tag}", [1, GW], F32)
    akg = p.sb(f"dl_ak_{tag}", [128, GW], BF16)
    e2g = p.sb(f"dl_e2_{tag}", [128, GW], F32)
    mng = p.sb(f"dl_mn_{tag}", [128, GW], F32)
    gcg = p.sb(f"dl_gc_{tag}", [128, GW], BF16)
    ps_r = p.ps(f"dl_psr_{tag}", [128, 512])

    def sinit(e, cs=tuple(c["S_f"] for c in ctxs) + tuple(c["S_b"] for c in ctxs)):
        for (sm, v) in deps:
            e.wait_ge(sm.h, v)
        for t in cs:
            e.memset(t[:], 0.0)
            e.drain()
        e.engine_nop().then_inc(s_be.h, 1)
    p.on("vector", sinit)
    s_be.bump()
    for c_ in ctxs:
        c_["sb_ready"] = s_be.n

    def emit_group(hc, g):
        b = g % 2
        qT, kT, vT = hc["qT"], hc["kT"], hc["vT"]
        gsl = slice(g * GW, (g + 1) * GW)
        war = base_be0 + 1 + max(0, (NH * 3 * GRP) * (g - 1))

        def t_act1(e, g=g, gsl=gsl, war=war, prev_grp=s_tf.n, qT=qT):
            if g == 0:
                for (sm, v) in deps:
                    e.wait_ge(sm.h, v)
            e.wait_ge(s_tf.h, prev_grp)
            e.wait_ge(s_be.h, war)
            e.activation(eq[:], qT[:, gsl], AF.Exp).then_inc(s_tf.h, 1)
        p.on("scalar", t_act1)
        v1 = s_tf.bump()

        def softmax_pipe(dst, scale_after, v_ready):
            vr = v_ready
            for ncl in range(2):
                def pe_row(e, ncl=ncl, vr=vr):
                    e.wait_ge(s_tf.h, vr)
                    e.matmul(ps_r[0:1, 0:512], aux["ones_bf"][:],
                             eq[:, ncl * 512:(ncl + 1) * 512],
                             start=True, stop=True).then_inc(s_tf.h, 1)
                p.on("tensor", pe_row)
                v_r = s_tf.bump()

                def dv_rec(e, ncl=ncl, v_r=v_r):
                    e.wait_ge(s_tf.h, v_r)
                    e.tensor_copy(den[0:1, ncl * 512:(ncl + 1) * 512], ps_r[0:1, 0:512])
                    e.drain()
                    e.reciprocal(den[0:1, ncl * 512:(ncl + 1) * 512],
                                 den[0:1, ncl * 512:(ncl + 1) * 512])
                    e.drain()
                    e.engine_nop().then_inc(s_tf.h, 1)
                p.on("vector", dv_rec)
                v_c = s_tf.bump()

                def pe_bc(e, ncl=ncl, v_c=v_c):
                    e.wait_ge(s_tf.h, v_c)
                    e.matmul(ps_r[:, 0:512], aux["ones1_f"][:],
                             den[0:1, ncl * 512:(ncl + 1) * 512],
                             start=True, stop=True).then_inc(s_tf.h, 1)
                p.on("tensor", pe_bc)
                v_b = s_tf.bump()

                def dv_fin(e, dst=dst, ncl=ncl, v_b=v_b, scale_after=scale_after):
                    e.wait_ge(s_tf.h, v_b)
                    csl = slice(ncl * 512, (ncl + 1) * 512)
                    e.tensor_tensor(tmpg[:, csl], eq[:, csl], ps_r[:, 0:512], ALU.mult)
                    e.drain()
                    if scale_after is None:
                        e.tensor_scalar(dst[:, csl], tmpg[:, csl], 1e-6, 1.0 - 1e-6,
                                        ALU.max, ALU.min).then_inc(s_tf.h, 1)
                    else:
                        e.tensor_scalar(tmpg[:, csl], tmpg[:, csl], 1e-6, 1.0 - 1e-6,
                                        ALU.max, ALU.min)
                        e.drain()
                        e.tensor_scalar_mul(dst[:, csl], tmpg[:, csl],
                                            float(scale_after)).then_inc(s_tf.h, 1)
                p.on("vector", dv_fin)
                vr = s_tf.bump()
            return vr

        vq = softmax_pipe(hc["qh_g"][b], ATT_SCALE, v1)

        def t_act2(e, gsl=gsl, vq=vq, kT=kT):
            e.wait_ge(s_tf.h, vq)
            e.activation(eq[:], kT[:, gsl], AF.Exp).then_inc(s_tf.h, 1)
        p.on("scalar", t_act2)
        v2 = s_tf.bump()
        vk = softmax_pipe(hc["kt_g"][b], None, v2)

        def t_gate_a(e, gsl=gsl, vk=vk, kT=kT):
            e.wait_ge(s_tf.h, vk)
            e.activation(akg[:], kT[:, gsl], AF.Abs)
            e.drain()
            e.activation(e2g[:], akg[:], AF.Exp, scale=-1.0)
            e.drain()
            e.activation(e2g[:], e2g[:], AF.Ln, bias=1.0).then_inc(s_tf.h, 1)
        p.on("scalar", t_gate_a)
        v4 = s_tf.bump()

        def t_gate_d2(e, gsl=gsl, b=b, v4=v4, hc=hc, vT=vT, kT=kT):
            e.wait_ge(s_tf.h, v4)
            e.tensor_scalar(mng[:], kT[:, gsl], 0.0, None, ALU.min)
            e.drain()
            e.tensor_tensor(mng[:], mng[:], e2g[:], ALU.subtract)
            e.drain()
            e.tensor_scalar(mng[:], mng[:], 1.0 / 16.0, -16.0, ALU.mult, ALU.max)
            e.drain()
            e.tensor_scalar(gcg[:], mng[:], 16.0, None, ALU.min)
            e.drain()
            e.tensor_tensor(hc["kb_g"][b][:], hc["kt_g"][b][:], gcg[:], ALU.mult)
            e.drain()
            e.tensor_tensor(hc["vb_g"][b][:], vT[:, gsl], gcg[:], ALU.mult).then_inc(s_tf.h, 1)
        p.on("vector", t_gate_d2)
        return s_tf.bump()

    def wait_evacs(e, ae_n, aa_n):
        e.wait_ge(s_ae.h, ae_n)
        e.wait_ge(s_aa.h, aa_n)

    def gen_A(hc, c):
        g = c // GRP
        b2 = c % 2
        bg = g % 2
        lsl = slice((c % GRP) * 128, (c % GRP + 1) * 128)
        qh, kt_, kb, vb = (hc["qh_g"][bg], hc["kt_g"][bg], hc["kb_g"][bg],
                          hc["vb_g"][bg])
        ps_a, ps_t = hc["ps_a"], hc["ps_t"]
        gd = hc["grp_done"][g]
        be_war = s_be.n

        for (srcbuf, dst) in ((vb, hc["vb_tm"]), (kb, hc["kb_tm"]),
                              (kt_, hc["ktm_sb"][b2])):
            def tp_pe(e, srcbuf=srcbuf, lsl=lsl, gd=gd, be_war=be_war,
                      ae_n=hc["ae_own"], aa_n=hc["aa_own"], first=(srcbuf is vb)):
                if first:
                    e.wait_ge(s_tf.h, gd)
                    if be_war > base_be0 + 1:
                        e.wait_ge(s_be.h, be_war)
                wait_evacs(e, ae_n, aa_n)
                e.transpose(ps_t[:], srcbuf[:, lsl], aux["ident"][:]).then_inc(s_af.h, 1)
            p.on("tensor", tp_pe)
            af_v = s_af.bump()

            def tp_cp(e, ps_t=ps_t, dst=dst, af_v=af_v):
                e.wait_ge(s_af.h, af_v)
                e.copy(dst[:], ps_t[:]).then_inc(s_aa.h, 1)
            p.on("scalar", tp_cp)
            hc["aa_own"] = s_aa.bump()
            yield

        def llt_pe(e, kb=kb, kt_=kt_, lsl=lsl, ae_n=hc["ae_own"], aa_n=hc["aa_own"]):
            wait_evacs(e, ae_n, aa_n)
            e.matmul(ps_a[:, 0:128], kb[:, lsl], kt_[:, lsl], start=True, stop=True)
            e.matmul(ps_a[:, 128:256], kt_[:, lsl], kb[:, lsl], start=True,
                     stop=True).then_inc(s_af.h, 1)
        p.on("tensor", llt_pe)
        af_v = s_af.bump()

        def llt_dv(e, hc=hc, af_v=af_v):
            e.wait_ge(s_af.h, af_v)
            e.tensor_tensor(hc["Lm"][:], ps_a[:, 0:128], aux["m_sl"][:], ALU.mult)
            e.tensor_tensor(hc["LTm"][:], ps_a[:, 128:256], aux["m_su"][:],
                            ALU.mult).then_inc(s_ae.h, 1)
        p.on("vector", llt_dv)
        hc["ae_own"] = s_ae.bump()
        yield

        def at_pe(e, kt_=kt_, qh=qh, lsl=lsl, ae_n=hc["ae_own"], aa_n=hc["aa_own"]):
            wait_evacs(e, ae_n, aa_n)
            e.matmul(ps_a[:, 0:128], kt_[:, lsl], qh[:, lsl], start=True,
                     stop=True).then_inc(s_af.h, 1)
        p.on("tensor", at_pe)
        af_v = s_af.bump()

        def at_dv(e, hc=hc, b2=b2, af_v=af_v):
            e.wait_ge(s_af.h, af_v)
            e.tensor_tensor(hc["at_sb"][b2][:], ps_a[:, 0:128], aux["m_iu"][:],
                            ALU.mult).then_inc(s_ae.h, 1)
        p.on("vector", at_dv)
        hc["ae_own"] = s_ae.bump()
        yield

        for lev in range(6):
            curL = hc["Lm"] if lev == 0 else hc["Pw"][lev - 1]
            curLT = hc["LTm"] if lev == 0 else hc["PwT"][(lev - 1) % 2]
            last = (lev == 5)

            def p_pe(e, curL=curL, curLT=curLT, last=last, ae_n=hc["ae_own"],
                     aa_n=hc["aa_own"]):
                wait_evacs(e, ae_n, aa_n)
                if not last:
                    e.matmul(ps_a[:, 0:128], curLT[:], curL[:], start=True, stop=True)
                e.matmul(ps_a[:, 128:256], curL[:], curLT[:], start=True,
                         stop=True).then_inc(s_af.h, 1)
            p.on("tensor", p_pe)
            af_v = s_af.bump()

            def p_cp(e, hc=hc, lev=lev, last=last, af_v=af_v):
                e.wait_ge(s_af.h, af_v)
                if not last:
                    e.copy(hc["Pw"][lev][:], ps_a[:, 0:128])
                e.copy(hc["PwT"][lev % 2][:], ps_a[:, 128:256]).then_inc(s_aa.h, 1)
            p.on("scalar", p_cp)
            hc["aa_own"] = s_aa.bump()
            yield

        def y_init(e, hc=hc, ae_n=hc["ae_own"], aa_n=hc["aa_own"]):
            wait_evacs(e, ae_n, aa_n)
            e.tensor_tensor(hc["Y"][:], hc["PwT"][1][:], aux["eye_bf"][:],
                            ALU.add).then_inc(s_ae.h, 1)
        p.on("vector", y_init)
        hc["ae_own"] = s_ae.bump()
        yield

        prod_lhs = [hc["Pw"][4], hc["Pw"][3], hc["Pw"][2], hc["Pw"][1],
                    hc["Pw"][0], hc["Lm"]]
        for pi, lhs in enumerate(prod_lhs):
            lastp = (pi == 5)

            def pr_pe(e, lhs=lhs, hc=hc, ae_n=hc["ae_own"], aa_n=hc["aa_own"]):
                wait_evacs(e, ae_n, aa_n)
                e.matmul(ps_a[:, 0:128], lhs[:], hc["Y"][:], start=True,
                         stop=True).then_inc(s_af.h, 1)
            p.on("tensor", pr_pe)
            af_v = s_af.bump()

            def pr_dv(e, hc=hc, lastp=lastp, af_v=af_v):
                e.wait_ge(s_af.h, af_v)
                if lastp:
                    e.tensor_tensor(hc["TT"][:], hc["Y"][:], ps_a[:, 0:128],
                                    ALU.subtract).then_inc(s_ae.h, 1)
                else:
                    e.tensor_tensor(hc["Y"][:], hc["Y"][:], ps_a[:, 0:128],
                                    ALU.add).then_inc(s_ae.h, 1)
            p.on("vector", pr_dv)
            hc["ae_own"] = s_ae.bump()
            yield

        def uw_pe(e, hc=hc, ae_n=hc["ae_own"], aa_n=hc["aa_own"]):
            wait_evacs(e, ae_n, aa_n)
            e.matmul(ps_a[:, 0:128], hc["TT"][:], hc["vb_tm"][:], start=True, stop=True)
            e.matmul(ps_a[:, 128:256], hc["kb_tm"][:], hc["TT"][:], start=True,
                     stop=True).then_inc(s_af.h, 1)
        p.on("tensor", uw_pe)
        af_v = s_af.bump()

        def uw_cp(e, hc=hc, b2=b2, af_v=af_v):
            e.wait_ge(s_af.h, af_v)
            e.copy(hc["u_sb"][b2][:], ps_a[:, 0:128])
            e.copy(hc["wT_sb"][b2][:], ps_a[:, 128:256]).then_inc(s_aa.h, 1)
        p.on("scalar", uw_cp)
        hc["aa_own"] = s_aa.bump()
        hc["a_done"][c] = (s_ae.n, s_aa.n)
        yield

    def gen_B(hc, c):
        b2 = c % 2
        csl = slice(c * 128, (c + 1) * 128)
        lsl = slice((c % GRP) * 128, (c % GRP + 1) * 128)
        g = c // GRP
        qh = hc["qh_g"][g % 2]
        ps_b2, ps_t = hc["ps_b"], hc["ps_t"]
        a_done = hc["a_done"][c]

        def b_pe1(e, hc=hc, b2=b2, a_done=a_done, be_n=hc["sb_ready"]):
            wait_evacs(e, a_done[0], a_done[1])
            e.wait_ge(s_be.h, be_n)
            e.matmul(ps_b2[:, 0:128], hc["wT_sb"][b2][:], hc["S_b"][:], start=True,
                     stop=True).then_inc(s_bf.h, 1)
        p.on("tensor", b_pe1)
        bf1 = s_bf.bump()

        def b_dv1(e, hc=hc, b2=b2, bf1=bf1):
            e.wait_ge(s_bf.h, bf1)
            e.tensor_tensor(hc["u2"][:], hc["u_sb"][b2][:], ps_b2[:, 0:128],
                            ALU.subtract).then_inc(s_be.h, 1)
        p.on("vector", b_dv1)
        be1 = s_be.bump()
        yield

        def b_pe2(e, hc=hc, b2=b2, lsl=lsl, qh=qh, be1=be1):
            e.wait_ge(s_be.h, be1)
            e.matmul(ps_b2[:, 128:256], qh[:, lsl], hc["S_b"][:], start=True, stop=False)
            e.matmul(ps_b2[:, 128:256], hc["at_sb"][b2][:], hc["u2"][:], start=False,
                     stop=True)
            e.matmul(ps_b2[:, 0:128], hc["ktm_sb"][b2][:], hc["u2"][:], start=True,
                     stop=True).then_inc(s_bf.h, 1)
        p.on("tensor", b_pe2)
        bf2 = s_bf.bump()

        def b_dv2(e, hc=hc, bf2=bf2):
            e.wait_ge(s_bf.h, bf2)
            e.tensor_copy(hc["oc"][:], ps_b2[:, 128:256])
            e.tensor_tensor(hc["S_f"][:], hc["S_f"][:], ps_b2[:, 0:128], ALU.add)
            e.drain()
            e.tensor_copy(hc["S_b"][:], hc["S_f"][:]).then_inc(s_be.h, 1)
        p.on("vector", b_dv2)
        be2 = s_be.bump()
        hc["sb_ready"] = be2
        yield

        def b_pe3(e, hc=hc, be2=be2, ae_n=hc["ae_own"], aa_n=hc["aa_own"]):
            e.wait_ge(s_be.h, be2)
            wait_evacs(e, ae_n, aa_n)
            e.transpose(ps_t[:], hc["oc"][:], aux["ident"][:]).then_inc(s_bf.h, 1)
        p.on("tensor", b_pe3)
        bf3 = s_bf.bump()

        def b_dv3(e, hc=hc, csl=csl, bf3=bf3):
            e.wait_ge(s_bf.h, bf3)
            e.tensor_copy(hc["olinT"][:, csl], ps_t[:]).then_inc(s_be.h, 1)
        p.on("vector", b_dv3)
        s_be.bump()
        yield

    def drive(gens):
        gens = [g for g in gens if g is not None]
        while gens:
            nxt = []
            for g in gens:
                try:
                    next(g)
                    nxt.append(g)
                except StopIteration:
                    pass
            gens = nxt

    for hc in ctxs:
        hc["grp_done"][0] = emit_group(hc, 0)
    drive([gen_A(hc, 0) for hc in ctxs])
    for c in range(NCH):
        if c + 1 < NCH:
            g_next = (c + 1) // GRP
            for hc in ctxs:
                if g_next not in hc["grp_done"]:
                    hc["grp_done"][g_next] = emit_group(hc, g_next)
            drive([gen_A(hc, c + 1) for hc in ctxs] + [gen_B(hc, c) for hc in ctxs])
        else:
            drive([gen_B(hc, c) for hc in ctxs])

    return s_be, s_be.n



def emit_aux_load(p: Prog, dram_ap, shape, dtype, tag, deps=()):
    sb = p.sb(f"auxl_{tag}", shape, dtype)
    s = p.sem("auxl")

    def lf(e, deps=tuple(deps)):
        for (sm, v) in deps:
            e.wait_ge(sm.h, v)
        e.dma_start(out=sb[:], in_=dram_ap).then_inc(s.h, 16)
    p.on("sync", lf)
    return sb, (s, s.bump(16))


def emit_attn_a(p: Prog, d, aux, x_sb, xdep, tag="a"):
    with p.scope():
        qT = p.sb(f"qT_{tag}", [128, 2, S], BF16)
        kT = p.sb(f"kT_{tag}", [128, 1, S], BF16)
        vT = p.sb(f"vT_{tag}", [128, 1, S], BF16)

        with p.scope():
            t_bf = p.sb(f"t_{tag}", [128, NTT, D], BF16)
            s_ln, lnv = emit_ln(p, x_sb, t_bf, [xdep], f"{tag}")
            emit_allgather_tm(p, t_bf, d[f"cc_t_{tag}_in"], d[f"cc_t_{tag}"],
                              [(s_ln, lnv)], f"{tag}t")
            bar0 = p.barrier()

        with p.scope():
            specs = [
                (d[f"wq_sl_{tag}"], 0, 2, qT),
                (d[f"wk_sl_{tag}"], 0, 1, kT),
                (d[f"wv_sl_{tag}"], 0, 1, vT),
            ]
            emit_attn_proj(p, d[f"cc_t_{tag}"], specs, [bar0], f"{tag}")
            bar1 = p.barrier()

        with p.scope():
            v_tm = p.sb(f"vtm_{tag}", [128, NCH, 128], BF16)
            mask4, m4dep = emit_aux_load(p, d["mask4"], [128, 4, 512], BF16, f"m4{tag}",
                                         deps=[bar1])
            with p.scope():
                emit_transpose_fm_tm(p, vT[:, 0, :], v_tm, NCH, aux["ident"],
                                     [bar1], f"{tag}v")
                bar_tp = p.barrier(engines=("tensor", "vector"))
            s_mx = p.sem("mx")
            hdep = [bar1, bar_tp, m4dep]
            for h in range(2):
                with p.scope():
                    outT = p.sb(f"mixT_{tag}_{h}", [128, S], BF16)
                    s_fl, flv = emit_flash_head(p, qT[:, h, :], kT[:, 0, :], v_tm, outT,
                                                aux["ones_bf"], aux["ones1_f"], mask4,
                                                hdep, f"{tag}h{h}", ones_f=aux["ones_f"])

                    def mxo(e, h=h, outT=outT, flv=flv, s_fl=s_fl):
                        e.wait_ge(s_fl.h, flv)
                        for j in range(R):
                            e.dma_start(out=d[f"cc_mix_{tag}_in"][256 * j + 128 * h:256 * j + 128 * (h + 1), :],
                                        in_=outT[:, j * TOK:(j + 1) * TOK]).then_inc(s_mx.h, 16)
                    p.on("scalar", mxo)
                    s_mx.bump(16 * R)
                    bar_h = p.barrier(engines=("tensor", "scalar", "vector"))
                    hdep = [bar_h]

            def agm(e, mxv=s_mx.n):
                from concourse.aot_env import allow_ici_all_to_all
                e.wait_ge(s_mx.h, mxv)
                with allow_ici_all_to_all():
                    e.collective_compute(
                        "AllToAll", ALU.bypass,
                        replica_groups=[list(range(R))],
                        ins=[d[f"cc_mix_{tag}_in"]],
                        outs=[d[f"cc_mix_{tag}"]],
                    ).then_inc(s_mx.h, 1)
            p.on("gpsimd", agm)
            s_mx.bump(1)

        bar2 = p.barrier()

    with p.scope():
        s_res, resv = emit_out_proj(p, d[f"cc_mix_{tag}"], d[f"w_o_{tag}"], None,
                                    x_sb, [bar2], f"{tag}")
    return s_res, resv






def emit_ln_dram(p: Prog, x_sb, trb, trb_lo, deps, tag):
    p.constv(EPS_LN)
    p.constv(-0.5)
    sq = p.sb(f"lnd_sq_{tag}", [128, D], F32)
    ssq = p.sb(f"lnd_ssq_{tag}", [128, 1], F32)
    sx = p.sb(f"lnd_sx_{tag}", [128, 1], F32)
    mean = p.sb(f"lnd_mean_{tag}", [128, 1], F32)
    var = p.sb(f"lnd_var_{tag}", [128, 1], F32)
    rstd = p.sb(f"lnd_rstd_{tag}", [128, 1], F32)
    tf = p.sb(f"lnd_tf_{tag}", [128, D], F32)
    stage = [p.sb(f"lnd_st_{tag}_{b}", [128, D], BF16) for b in range(2)]
    stage_lo = [p.sb(f"lnd_sl_{tag}_{b}", [128, D], BF16) for b in range(2)] if trb_lo is not None else None
    ndma = 2 if trb_lo is not None else 1

    s_sq = p.sem("lnd_sq")
    s_st = p.sem("lnd_st")
    s_rs = p.sem("lnd_rs")
    s_done = p.sem("lnd_done")
    s_tb = p.sem("lnd_tb")
    base_tb = s_tb.n

    trb_r = trb.rearrange("(t p) x -> t p x", p=128)
    trb_lo_r = trb_lo.rearrange("(t p) x -> t p x", p=128) if trb_lo is not None else None

    for i in range(NTT):
        xt = x_sb[:, i, :]
        st_war = s_st.n

        def act1(e, xt=xt, i=i, st_war=st_war):
            if i == 0:
                for (sm, v) in deps:
                    e.wait_ge(sm.h, v)
            else:
                e.wait_ge(s_st.h, st_war)
            e.activation(sq[:], xt, AF.Square, accum_out=ssq[:]).then_inc(s_sq.h, 1)
        p.on("scalar", act1)
        sq_v = s_sq.bump()

        def dve1(e, xt=xt, sq_v=sq_v, i=i):
            if i == 0:
                for (sm, v) in deps:
                    e.wait_ge(sm.h, v)
            e.reduce_sum(sx[:], xt, axis=AX.X)
            e.drain()
            e.wait_ge(s_sq.h, sq_v)
            e.tensor_scalar_mul(mean[:], sx[:], 1.0 / D)
            e.drain()
            e.tensor_tensor(sx[:], mean[:], mean[:], ALU.mult)
            e.tensor_scalar_mul(var[:], ssq[:], 1.0 / D)
            e.drain()
            e.tensor_tensor(var[:], var[:], sx[:], ALU.subtract).then_inc(s_st.h, 1)
        p.on("vector", dve1)
        st_v = s_st.bump()

        def act2(e, st_v=st_v):
            e.wait_ge(s_st.h, st_v)
            e.activation(var[:], var[:], AF.Ln, bias=EPS_LN)
            e.drain()
            e.activation(rstd[:], var[:], AF.Exp, scale=-0.5).then_inc(s_rs.h, 1)
        p.on("scalar", act2)
        rs_v = s_rs.bump()

        def dve2(e, xt=xt, i=i, rs_v=rs_v,
                 tb_war=base_tb + i * 16 * ndma):
            e.wait_ge(s_rs.h, rs_v)
            e.wait_ge(s_tb.h, tb_war)
            b = i % 2
            if trb_lo is not None:
                e.tensor_scalar(tf[:], xt, mean[:], rstd[:], ALU.subtract, ALU.mult)
                e.drain()
                e.tensor_copy(stage[b][:], tf[:])
                e.drain()
                e.tensor_tensor(stage_lo[b][:], tf[:], stage[b][:],
                                ALU.subtract).then_inc(s_done.h, 1)
            else:
                e.tensor_scalar(stage[b][:], xt, mean[:], rstd[:],
                                ALU.subtract, ALU.mult).then_inc(s_done.h, 1)
        p.on("vector", dve2)
        dn_v = s_done.bump()

        def tdma(e, i=i, dn_v=dn_v):
            e.wait_ge(s_done.h, dn_v)
            b = i % 2
            e.dma_start(out=trb_r[i], in_=stage[b][:]).then_inc(s_tb.h, 16)
            if trb_lo is not None:
                e.dma_start(out=trb_lo_r[i], in_=stage_lo[b][:]).then_inc(s_tb.h, 16)
        p.on("scalar", tdma)
        s_tb.bump(16 * ndma)

    return s_tb, s_tb.n


def emit_ffn(p: Prog, d, aux, x_sb, deps, tag, upto="full", dbg=None):
    BIG = 1.0e9
    NM_E = F_EXP // 128

    tT = p.sb(f"ffn_tT_{tag}", [128, NDC, TOK], BF16)
    wrows = p.sb(f"ffn_wr_{tag}", [1, N_EXP, TOK], BF16)
    s_tb = p.sem("ffn_tb")
    s_tt = p.sem("ffn_tt")
    s_rt = p.sem("ffn_rt")

    with p.scope():
        s_tbl, tbv = emit_ln_dram(p, x_sb, d[f"trb_{tag}"], d[f"trb_lo_{tag}"],
                                  deps, f"{tag}")

        def tbfwd(e, tbv=tbv):
            e.wait_ge(s_tbl.h, tbv)
            e.sem_inc(s_tb.h, 1)
        p.on("scalar", tbfwd)
        tbv = s_tb.bump(1)
        p.barrier(engines=("vector", "scalar"))

    def treads(e, tbv=tbv):
        e.wait_ge(s_tb.h, tbv)
        for j in range(NDC):
            e.dma_start_transpose(tT[:, j, :],
                                  d[f"trb_{tag}"][:, j * 128:(j + 1) * 128]).then_inc(s_tt.h, 16)
    p.on("sync", treads)
    ttv = s_tt.bump(16 * NDC)

    with p.scope():
        tT_lo = p.sb(f"ffn_tTlo_{tag}", [128, NDC, TOK], BF16)

        def treads_lo(e, tbv=tbv):
            e.wait_ge(s_tb.h, tbv)
            for j in range(NDC):
                e.dma_start_transpose(tT_lo[:, j, :],
                                      d[f"trb_lo_{tag}"][:, j * 128:(j + 1) * 128]).then_inc(s_tt.h, 16)
        p.on("sync", treads_lo)
        ttv_lo = s_tt.bump(16 * NDC)

        gw_f = p.sb(f"ffn_gwf_{tag}", [128, NDC, 8], F32)
        gw_hi = p.sb(f"ffn_gwh_{tag}", [128, NDC, 8], BF16)
        gw_lo = p.sb(f"ffn_gwl_{tag}", [128, NDC, 8], BF16)

        def gwload(e):
            e.dma_start(out=gw_f[:], in_=d["gwT"].rearrange("(a p) n -> p a n", p=128)).then_inc(s_rt.h, 16)
        p.on("sync", gwload)
        rt_gw = s_rt.bump(16)

        def gwsplit(e, rt_gw=rt_gw):
            e.wait_ge(s_rt.h, rt_gw)
            e.tensor_copy(gw_hi[:], gw_f[:])
            e.drain()
            e.tensor_tensor(gw_lo[:], gw_f[:], gw_hi[:], ALU.subtract).then_inc(s_rt.h, 1)
        p.on("vector", gwsplit)
        rt_sp = s_rt.bump()

        ps_lg = p.ps(f"ffn_pslg_{tag}", [128, 8])
        ps_wr = p.ps(f"ffn_pswr_{tag}", [16, 128], BF16)
        lg = p.sb(f"ffn_lg_{tag}", [128, 8], F32)
        elg = p.sb(f"ffn_elg_{tag}", [128, 8], F32)
        ssum = p.sb(f"ffn_ssum_{tag}", [128, 1], F32)
        sc = p.sb(f"ffn_sc_{tag}", [128, 8], F32)
        gs = p.sb(f"ffn_gs_{tag}", [128, 4], F32)
        gx = p.sb(f"ffn_gx_{tag}", [128, 4], F32)
        mcol = p.sb(f"ffn_mcol_{tag}", [128, 1], F32)
        sm8 = p.sb(f"ffn_sm8_{tag}", [128, 8], F32)
        msk = p.sb(f"ffn_msk_{tag}", [128, 8], F32)
        sel = p.sb(f"ffn_sel_{tag}", [128, 8], F32)
        wm = p.sb(f"ffn_wm_{tag}", [128, 8], F32)
        wm_pad = p.sb(f"ffn_wmpad_{tag}", [128, 128], BF16)
        s_lg = p.sem("ffn_lg")
        s_wr = p.sem("ffn_wrow")

        for tt in range(NTT):
            def mml(e, tt=tt, first=(tt == 0), rt_sp=rt_sp, ttv=ttv, ttv_lo=ttv_lo,
                    lg_done=s_lg.n):
                if first:
                    e.wait_ge(s_rt.h, rt_sp)
                    e.wait_ge(s_tt.h, ttv_lo)
                else:
                    e.wait_ge(s_lg.h, lg_done)
                ins = None
                for j in range(NDC):
                    sl = tT[:, j, tt * 128:(tt + 1) * 128]
                    sl_lo = tT_lo[:, j, tt * 128:(tt + 1) * 128]
                    ins = e.matmul(ps_lg[:], sl, gw_hi[:, j, :], start=(j == 0), stop=False)
                    ins = e.matmul(ps_lg[:], sl, gw_lo[:, j, :], start=False, stop=False)
                    ins = e.matmul(ps_lg[:], sl_lo, gw_hi[:, j, :], start=False,
                                   stop=(j == NDC - 1))
                ins.then_inc(s_lg.h, 1)
            p.on("tensor", mml)
            lg_v = s_lg.bump()

            def route(e, lg_v=lg_v):
                e.wait_ge(s_lg.h, lg_v)
                e.tensor_copy(lg[:], ps_lg[:])
                e.drain()
                e.engine_nop().then_inc(s_lg.h, 1)
            p.on("vector", route)
            lg_cp = s_lg.bump()

            def act_exp(e, lg_cp=lg_cp):
                e.wait_ge(s_lg.h, lg_cp)
                e.activation(elg[:], lg[:], AF.Exp, accum_out=ssum[:]).then_inc(s_rt.h, 1)
            p.on("scalar", act_exp)
            rt_e = s_rt.bump()

            def route2(e, rt_e=rt_e):
                e.wait_ge(s_rt.h, rt_e)
                e.reciprocal(ssum[:], ssum[:])
                e.drain()
                e.tensor_scalar_mul(sc[:], elg[:], ssum[:])
                e.drain()
                e.tensor_tensor(gs[:], sc[:, 0:8:2], sc[:, 1:8:2], ALU.max)
                e.drain()
                e.reduce_max(mcol[:], gs[:], axis=AX.X)
                e.drain()
                e.tensor_scalar(gx[:], gs[:], mcol[:], None, ALU.is_ge)
                e.drain()
                e.scalar_tensor_tensor(gx[:], gx[:], -BIG, gs[:], ALU.mult, ALU.add)
                e.drain()
                e.reduce_max(mcol[:], gx[:], axis=AX.X)
                e.drain()
                e.tensor_scalar(gx[:], gs[:], mcol[:], None, ALU.is_ge)
                e.drain()
                e.tensor_copy(sm8[:, 0:8:2], gx[:])
                e.drain()
                e.tensor_copy(sm8[:, 1:8:2], gx[:])
                e.drain()
                e.tensor_tensor(msk[:], sc[:], sm8[:], ALU.mult)
                e.drain()
                e.reduce_max(mcol[:], msk[:], axis=AX.X)
                e.drain()
                e.tensor_scalar(sel[:], msk[:], mcol[:], None, ALU.is_ge)
                e.drain()
                e.scalar_tensor_tensor(sel[:], sel[:], -BIG, msk[:], ALU.mult, ALU.add)
                e.drain()
                e.reduce_max(mcol[:], sel[:], axis=AX.X)
                e.drain()
                e.tensor_scalar(sel[:], msk[:], mcol[:], None, ALU.is_ge)
                e.drain()
                e.tensor_tensor(wm[:], sel[:], msk[:], ALU.mult)
                e.drain()
                e.memset(wm_pad[:], 0.0)
                e.drain()
                e.tensor_copy(wm_pad[:, 0:128:16], wm[:]).then_inc(s_rt.h, 1)
            p.on("vector", route2)
            rt_w = s_rt.bump()

            if upto == "logits":
                continue
            for n in range(N_EXP):
                def wrt(e, n=n, rt_w=rt_w, wr_prev=s_wr.n):
                    e.wait_ge(s_rt.h, rt_w)
                    if wr_prev > 0:
                        e.wait_ge(s_wr.h, wr_prev)
                    e.transpose(ps_wr[:], wm_pad[:, 16 * n:16 * (n + 1)],
                                aux["ident"][:]).then_inc(s_rt.h, 1)
                p.on("tensor", wrt)
                rt_t = s_rt.bump()

                def wre(e, n=n, tt=tt, rt_t=rt_t):
                    e.wait_ge(s_rt.h, rt_t)
                    e.tensor_copy(wrows[0:1, n, tt * 128:(tt + 1) * 128],
                                  ps_wr[0:1, :]).then_inc(s_wr.h, 1)
                p.on("vector", wre)
                s_wr.bump()
            if dbg is not None and "wm_to_x" in dbg:
                def wmdump(e, tt=tt, rt_w=rt_w):
                    e.wait_ge(s_rt.h, rt_w)
                    e.tensor_copy(x_sb[:, tt, 0:8], wm[:])
                    e.drain()
                    e.tensor_copy(x_sb[:, tt, 8:16], sc[:])
                    e.drain()
                    e.tensor_copy(x_sb[:, tt, 16:20], gs[:])
                    e.drain()
                    e.tensor_copy(x_sb[:, tt, 20:24], gx[:])
                    e.drain()
                    e.tensor_copy(x_sb[:, tt, 24:32], sm8[:]).then_inc(s_rt.h, 1)
                p.on("vector", wmdump)
                s_rt.bump(1)
        wr_all = s_wr.n
        if upto in ("router", "logits"):
            return s_rt, s_rt.n
        bar_b = p.barrier(engines=("tensor", "vector", "scalar"))

    with p.scope():
        NRING = 2
        wg_ring = [p.sb(f"ffn_wg_{tag}_{i}", [128, NDC, 128], BF16) for i in range(NRING)]
        wu_ring = [p.sb(f"ffn_wu_{tag}_{i}", [128, NDC, 128], BF16) for i in range(NRING)]
        wd_sb = p.sb(f"ffn_wd_{tag}", [128, NM_E, D], BF16)
        act_sb = p.sb(f"ffn_act_{tag}", [128, NM_E, TOK], BF16)
        bc_bf = p.sb(f"ffn_bc_{tag}", [128, TOK], BF16)
        sg_bf = [p.sb(f"ffn_sg_{tag}_{i}", [128, TOK], BF16) for i in range(2)]
        a1 = [p.sb(f"ffn_a1_{tag}_{i}", [128, TOK], BF16) for i in range(2)]
        ps_g = [p.ps(f"ffn_psg_{tag}_{i}", [128, TOK]) for i in range(2)]
        ps_u = [p.ps(f"ffn_psu_{tag}_{i}", [128, TOK]) for i in range(2)]

        s_wlp = [p.sem("ffn_wl0"), p.sem("ffn_wl1")]
        s_mf = p.sem("ffn_mf")
        s_ae = p.sem("ffn_ae")
        s_av = p.sem("ffn_av")
        s_bc = p.sem("ffn_bc")
        s_dn = p.sem("ffn_dn")
        s_xa = p.sem("ffn_xa")
        s_dl = p.sem("ffn_dl")
        base_wlp = [s_wlp[0].n, s_wlp[1].n]
        base_mf, base_dn, base_dl = s_mf.n, s_dn.n, s_dl.n

        blocks = [("e", n) for n in range(N_EXP)] + [("s", h) for h in range(2)]
        SKIP_DOWN = False

        mseq = 0
        for bi, (kind, n) in enumerate(blocks):
            for m in range(NM_E):
                for (ring, wsel) in ((wg_ring, "g"), (wu_ring, "u")):
                    if kind == "e":
                        wtns = d["gate_wT"] if wsel == "g" else d["up_wT"]
                        src = wtns[n, :, m * 128:(m + 1) * 128]
                    else:
                        wtns = d["sh_gate_wT"] if wsel == "g" else d["sh_up_wT"]
                        c0 = n * F_EXP + m * 128
                        src = wtns[:, c0:c0 + 128]
                    src = src.rearrange("(a p) c -> p a c", p=128)

                    def wl(e, src=src, ring=ring, mseq=mseq, bar_b=bar_b,
                           sw=s_wlp[mseq % 2], need=base_mf + max(0, mseq - 1)):
                        e.wait_ge(bar_b[0].h, bar_b[1])
                        if need > base_mf:
                            e.wait_ge(s_mf.h, need)
                        e.dma_start(out=ring[mseq % NRING][:], in_=src).then_inc(sw.h, 16)
                    p.on("gpsimd", wl)
                    s_wlp[mseq % 2].bump(16)
                mseq += 1
            for k in range(NM_E):
                if kind == "e":
                    dsrc = d["down_wT"][n, k * 128:(k + 1) * 128, :]
                else:
                    r0 = n * F_EXP + k * 128
                    dsrc = d["sh_down_wT"][r0:r0 + 128, :]

                def dl(e, dsrc=dsrc, k=k, bi=bi, bar_b=bar_b,
                       dn_need=base_dn + bi * NTT):
                    e.wait_ge(bar_b[0].h, bar_b[1])
                    if bi > 0:
                        e.wait_ge(s_dn.h, dn_need)
                    e.dma_start(out=wd_sb[:, k, :], in_=dsrc).then_inc(s_dl.h, 16)
                p.on("gpsimd", dl)
                s_dl.bump(16)

        mseq = 0
        for bi, (kind, n) in enumerate(blocks):
            routed = (kind == "e")
            if routed:
                def bcast(e, n=n, wr_all=wr_all, bar_b=bar_b, bi=bi,
                          av_need=s_av.n, xa_need=s_xa.n):
                    e.wait_ge(s_wr.h, wr_all)
                    e.wait_ge(bar_b[0].h, bar_b[1])
                    if bi > 0:
                        e.wait_ge(s_av.h, av_need)
                        e.wait_ge(s_xa.h, xa_need)
                    for nck in range(TOK // 512):
                        e.matmul(ps_g[0][:, nck * 512:(nck + 1) * 512], aux["ones1_bf"][:],
                                 wrows[0:1, n, nck * 512:(nck + 1) * 512],
                                 start=True, stop=True).then_inc(s_bc.h, 1)
                p.on("tensor", bcast)
                bc_f = s_bc.bump(2)

                def bccp(e, bc_f=bc_f):
                    e.wait_ge(s_bc.h, bc_f)
                    e.activation(bc_bf[:], ps_g[0][:], AF.Copy).then_inc(s_bc.h, 1)
                p.on("scalar", bccp)
                bc_c = s_bc.bump()
            else:
                bc_c = s_bc.n

            for m in range(NM_E):
                mb = mseq % 2
                wg = wg_ring[mseq % NRING]
                wu = wu_ring[mseq % NRING]

                def mmg(e, mb=mb, wg=wg, wu=wu, sw=s_wlp[mseq % 2],
                        wl_need=base_wlp[mseq % 2] + 32 * (mseq // 2 + 1),
                        bc_c=bc_c, m=m, routed=routed, bi=bi,
                        av_need=max(0, s_av.n - 1), xa_need=s_xa.n if m <= 1 and bi > 0 else 0):
                    e.wait_ge(sw.h, wl_need)
                    if m == 0 and routed:
                        e.wait_ge(s_bc.h, bc_c)
                    if xa_need > 0:
                        e.wait_ge(s_xa.h, xa_need)
                    e.wait_ge(s_av.h, av_need)
                    ins = None
                    for j in range(NDC):
                        for nck in range(TOK // 512):
                            ins = e.matmul(ps_g[mb][:, nck * 512:(nck + 1) * 512],
                                           wg[:, j, :], tT[:, j, nck * 512:(nck + 1) * 512],
                                           start=(j == 0), stop=(j == NDC - 1))
                    for j in range(NDC):
                        for nck in range(TOK // 512):
                            ins = e.matmul(ps_u[mb][:, nck * 512:(nck + 1) * 512],
                                           wu[:, j, :], tT[:, j, nck * 512:(nck + 1) * 512],
                                           start=(j == 0), stop=(j == NDC - 1))
                    ins.then_inc(s_mf.h, 1)
                p.on("tensor", mmg)
                mf_v = s_mf.bump()

                def silu(e, mb=mb, mf_v=mf_v):
                    e.wait_ge(s_mf.h, mf_v)
                    e.activation(sg_bf[mb][:], ps_g[mb][:], AF.Silu).then_inc(s_ae.h, 1)
                p.on("scalar", silu)
                ae_v = s_ae.bump()

                def acte(e, m=m, mb=mb, ae_v=ae_v, routed=routed,
                         dn_need=base_dn + bi * NTT):
                    e.wait_ge(s_ae.h, ae_v)
                    if dn_need > base_dn:
                        e.wait_ge(s_dn.h, dn_need)
                    if routed:
                        e.tensor_tensor(a1[mb][:], sg_bf[mb][:], ps_u[mb][:], ALU.mult)
                        e.drain()
                        e.tensor_tensor(act_sb[:, m, :], a1[mb][:], bc_bf[:],
                                        ALU.mult).then_inc(s_av.h, 1)
                    else:
                        e.tensor_tensor(act_sb[:, m, :], sg_bf[mb][:], ps_u[mb][:],
                                        ALU.mult).then_inc(s_av.h, 1)
                p.on("vector", acte)
                s_av.bump()
                mseq += 1

            for tt in range(NTT if not SKIP_DOWN else 0):
                def mmd(e, tt=tt, av_all=s_av.n, dl_need=base_dl + 16 * NM_E * (bi + 1),
                        xa_need=s_xa.n):
                    e.wait_ge(s_av.h, av_all)
                    e.wait_ge(s_dl.h, dl_need)
                    if xa_need > 0:
                        e.wait_ge(s_xa.h, xa_need)
                    ins = None
                    if True:
                        for k in range(NM_E):
                            for nch in range(4):
                                bank = ps_g[nch // 2]
                                tgt = bank[:, (nch % 2) * 512:(nch % 2 + 1) * 512]
                                ins = e.matmul(tgt, act_sb[:, k, tt * 128:(tt + 1) * 128],
                                               wd_sb[:, k, nch * 512:(nch + 1) * 512],
                                               start=(k == 0), stop=(k == NM_E - 1))
                    ins.then_inc(s_dn.h, 1)
                p.on("tensor", mmd)
                dn_v = s_dn.bump()

                def xadd(e, tt=tt, dn_v=dn_v):
                    e.wait_ge(s_dn.h, dn_v)
                    e.tensor_tensor(x_sb[:, tt, 0:TOK], x_sb[:, tt, 0:TOK],
                                    ps_g[0][:], ALU.add)
                    e.drain()
                    e.tensor_tensor(x_sb[:, tt, TOK:D], x_sb[:, tt, TOK:D],
                                    ps_g[1][:], ALU.add).then_inc(s_xa.h, 1)
                p.on("vector", xadd)
                s_xa.bump()

    return s_xa, s_xa.n



def emit_delta_head(p: Prog, qT, kT, vT, olinT, aux, deps, tag):
    p.constv(ATT_SCALE)
    base_be0 = p.sem("dl_be").n
    GRP = 8
    NG = NCH // GRP
    GW = GRP * 128

    qh_g = [p.sb(f"dl_qh_{tag}_{b}", [128, GW], BF16) for b in range(2)]
    kt_g = [p.sb(f"dl_kt_{tag}_{b}", [128, GW], BF16) for b in range(2)]
    kb_g = [p.sb(f"dl_kb_{tag}_{b}", [128, GW], BF16) for b in range(2)]
    vb_g = [p.sb(f"dl_vb_{tag}_{b}", [128, GW], BF16) for b in range(2)]
    eq = p.sb(f"dl_eq_{tag}", [128, GW], BF16)
    tmpg = p.sb(f"dl_tmpg_{tag}", [128, GW], F32)
    den = p.sb(f"dl_den_{tag}", [1, GW], F32)
    akg = p.sb(f"dl_ak_{tag}", [128, GW], BF16)
    e2g = p.sb(f"dl_e2_{tag}", [128, GW], F32)
    mng = p.sb(f"dl_mn_{tag}", [128, GW], F32)
    gcg = p.sb(f"dl_gc_{tag}", [128, GW], BF16)

    vb_tm = p.sb(f"dl_vbtm_{tag}", [128, 128], BF16)
    kb_tm = p.sb(f"dl_kbtm_{tag}", [128, 128], BF16)
    Lm = p.sb(f"dl_L_{tag}", [128, 128], BF16)
    LTm = p.sb(f"dl_LT_{tag}", [128, 128], BF16)
    Pw = [p.sb(f"dl_pw_{tag}_{i}", [128, 128], BF16) for i in range(5)]
    PwT = [p.sb(f"dl_pwt_{tag}_{i}", [128, 128], BF16) for i in range(2)]
    Y = p.sb(f"dl_Y_{tag}", [128, 128], BF16)
    TT = p.sb(f"dl_TT_{tag}", [128, 128], BF16)
    u_sb = [p.sb(f"dl_u_{tag}_{b}", [128, 128], BF16) for b in range(2)]
    wT_sb = [p.sb(f"dl_w_{tag}_{b}", [128, 128], BF16) for b in range(2)]
    at_sb = [p.sb(f"dl_at_{tag}_{b}", [128, 128], BF16) for b in range(2)]
    ktm_sb = [p.sb(f"dl_ktm_{tag}_{b}", [128, 128], BF16) for b in range(2)]
    u2 = p.sb(f"dl_u2_{tag}", [128, 128], BF16)
    oc = p.sb(f"dl_oc_{tag}", [128, 128], BF16)
    S_f = p.sb(f"dl_Sf_{tag}", [128, 128], F32)
    S_b = p.sb(f"dl_Sb_{tag}", [128, 128], BF16)

    ps_a = [p.ps(f"dl_psa_{tag}_{i}", [128, 128]) for i in range(2)]
    ps_r = p.ps(f"dl_psr_{tag}", [128, 512])
    ps_t = p.ps(f"dl_pst_{tag}", [128, 128], BF16)
    ps_b = [p.ps(f"dl_psb_{tag}_{i}", [128, 128]) for i in range(2)]

    s_tf = p.sem("dl_tf")
    s_af = p.sem("dl_af")
    s_ae = p.sem("dl_ae")
    s_bf = p.sem("dl_bf")
    s_be = p.sem("dl_be")

    def sinit(e):
        for (sm, v) in deps:
            e.wait_ge(sm.h, v)
        e.memset(S_f[:], 0.0)
        e.drain()
        e.memset(S_b[:], 0.0)
        e.drain()
        e.engine_nop().then_inc(s_be.h, 1)
    p.on("vector", sinit)
    s_be.bump()

    def emit_group(g):
        b = g % 2
        gsl = slice(g * GW, (g + 1) * GW)
        war = base_be0 + 1 + max(0, 24 * (g - 1))

        def t_act1(e, g=g, gsl=gsl, war=war, prev_grp=s_tf.n):
            if g == 0:
                for (sm, v) in deps:
                    e.wait_ge(sm.h, v)
            e.wait_ge(s_tf.h, prev_grp)
            e.wait_ge(s_be.h, war)
            e.activation(eq[:], qT[:, gsl], AF.Exp).then_inc(s_tf.h, 1)
        p.on("scalar", t_act1)
        v1 = s_tf.bump()

        def softmax_pipe(src_fm, dst, scale_after, v_ready):
            vr = v_ready
            for ncl in range(2):
                def pe_row(e, src_fm=src_fm, ncl=ncl, vr=vr, ae_w=s_tf.n):
                    e.wait_ge(s_tf.h, vr)
                    e.matmul(ps_r[0:1, 0:512], aux["ones_bf"][:],
                             eq[:, ncl * 512:(ncl + 1) * 512],
                             start=True, stop=True).then_inc(s_tf.h, 1)
                p.on("tensor", pe_row)
                v_r = s_tf.bump()

                def dv_rec(e, ncl=ncl, v_r=v_r):
                    e.wait_ge(s_tf.h, v_r)
                    e.tensor_copy(den[0:1, ncl * 512:(ncl + 1) * 512], ps_r[0:1, 0:512])
                    e.drain()
                    e.reciprocal(den[0:1, ncl * 512:(ncl + 1) * 512],
                                 den[0:1, ncl * 512:(ncl + 1) * 512])
                    e.drain()
                    e.engine_nop().then_inc(s_tf.h, 1)
                p.on("vector", dv_rec)
                v_c = s_tf.bump()

                def pe_bc(e, ncl=ncl, v_c=v_c):
                    e.wait_ge(s_tf.h, v_c)
                    e.matmul(ps_r[:, 0:512], aux["ones1_f"][:],
                             den[0:1, ncl * 512:(ncl + 1) * 512],
                             start=True, stop=True).then_inc(s_tf.h, 1)
                p.on("tensor", pe_bc)
                v_b = s_tf.bump()

                def dv_fin(e, dst=dst, ncl=ncl, v_b=v_b, scale_after=scale_after):
                    e.wait_ge(s_tf.h, v_b)
                    csl = slice(ncl * 512, (ncl + 1) * 512)
                    e.tensor_tensor(tmpg[:, csl], eq[:, csl], ps_r[:, 0:512], ALU.mult)
                    e.drain()
                    if scale_after is None:
                        e.tensor_scalar(dst[:, csl], tmpg[:, csl], 1e-6, 1.0 - 1e-6,
                                        ALU.max, ALU.min).then_inc(s_tf.h, 1)
                    else:
                        e.tensor_scalar(tmpg[:, csl], tmpg[:, csl], 1e-6, 1.0 - 1e-6,
                                        ALU.max, ALU.min)
                        e.drain()
                        e.tensor_scalar_mul(dst[:, csl], tmpg[:, csl],
                                            float(scale_after)).then_inc(s_tf.h, 1)
                p.on("vector", dv_fin)
                vr = s_tf.bump()
            return vr

        vq = softmax_pipe(None, qh_g[b], ATT_SCALE, v1)

        def t_act2(e, gsl=gsl, vq=vq):
            e.wait_ge(s_tf.h, vq)
            e.activation(eq[:], kT[:, gsl], AF.Exp).then_inc(s_tf.h, 1)
        p.on("scalar", t_act2)
        v2 = s_tf.bump()
        vk = softmax_pipe(None, kt_g[b], None, v2)

        def t_gate_a(e, gsl=gsl, vk=vk):
            e.wait_ge(s_tf.h, vk)
            e.activation(akg[:], kT[:, gsl], AF.Abs)
            e.drain()
            e.activation(e2g[:], akg[:], AF.Exp, scale=-1.0)
            e.drain()
            e.activation(e2g[:], e2g[:], AF.Ln, bias=1.0).then_inc(s_tf.h, 1)
        p.on("scalar", t_gate_a)
        v4 = s_tf.bump()

        def t_gate_d2(e, gsl=gsl, b=b, v4=v4):
            e.wait_ge(s_tf.h, v4)
            e.tensor_scalar(mng[:], kT[:, gsl], 0.0, None, ALU.min)
            e.drain()
            e.tensor_tensor(mng[:], mng[:], e2g[:], ALU.subtract)
            e.drain()
            e.tensor_scalar(mng[:], mng[:], 1.0 / 16.0, -16.0, ALU.mult, ALU.max)
            e.drain()
            e.tensor_scalar(gcg[:], mng[:], 16.0, None, ALU.min)
            e.drain()
            e.tensor_tensor(kb_g[b][:], kt_g[b][:], gcg[:], ALU.mult)
            e.drain()
            e.tensor_tensor(vb_g[b][:], vT[:, gsl], gcg[:], ALU.mult).then_inc(s_tf.h, 1)
        p.on("vector", t_gate_d2)
        return s_tf.bump()

    grp_done = {}
    def emit_A(c):
        g = c // GRP
        b2 = c % 2
        bg = g % 2
        csl = slice((c % GRP) * 128, (c % GRP + 1) * 128)
        qh, kt_, kb, vb = qh_g[bg], kt_g[bg], kb_g[bg], vb_g[bg]

        be_war = s_be.n

        def a_pe(e, c=c, csl=csl, qh=qh, kt_=kt_, kb=kb, vb=vb, b2=b2,
                 gd=grp_done[g], be_war=s_be.n, ae_prev=s_ae.n):
            e.wait_ge(s_tf.h, gd)
            if c >= 2:
                e.wait_ge(s_be.h, be_war)
            if c >= 1:
                e.wait_ge(s_ae.h, ae_prev)
            e.transpose(ps_t[:], vb[:, csl], aux["ident"][:]).then_inc(s_af.h, 1)
            e.wait_ge(s_ae.h, ae_prev + 1)
            e.transpose(ps_t[:], kb[:, csl], aux["ident"][:]).then_inc(s_af.h, 1)
            e.wait_ge(s_ae.h, ae_prev + 2)
            e.transpose(ps_t[:], kt_[:, csl], aux["ident"][:]).then_inc(s_af.h, 1)
            e.matmul(ps_a[0][:], kb[:, csl], kt_[:, csl], start=True, stop=True).then_inc(s_af.h, 1)
            e.matmul(ps_a[1][:], kt_[:, csl], kb[:, csl], start=True, stop=True).then_inc(s_af.h, 1)
            e.wait_ge(s_ae.h, ae_prev + 5)
            e.matmul(ps_a[0][:], kt_[:, csl], qh[:, csl], start=True, stop=True).then_inc(s_af.h, 1)
        p.on("tensor", a_pe)
        ae_prev = s_ae.n
        s_af.bump(6)

        def a_dv(e, c=c, b2=b2, af0=ae_prev * 0 + s_af.n - 6):
            e.wait_ge(s_af.h, af0 + 1)
            e.tensor_copy(vb_tm[:], ps_t[:]).then_inc(s_ae.h, 1)
            e.wait_ge(s_af.h, af0 + 2)
            e.tensor_copy(kb_tm[:], ps_t[:]).then_inc(s_ae.h, 1)
            e.wait_ge(s_af.h, af0 + 3)
            e.tensor_copy(ktm_sb[b2][:], ps_t[:]).then_inc(s_ae.h, 1)
            e.wait_ge(s_af.h, af0 + 4)
            e.tensor_tensor(Lm[:], ps_a[0][:], aux["m_sl"][:], ALU.mult).then_inc(s_ae.h, 1)
            e.wait_ge(s_af.h, af0 + 5)
            e.tensor_tensor(LTm[:], ps_a[1][:], aux["m_su"][:], ALU.mult).then_inc(s_ae.h, 1)
            e.wait_ge(s_af.h, af0 + 6)
            e.tensor_tensor(at_sb[b2][:], ps_a[0][:], aux["m_iu"][:], ALU.mult).then_inc(s_ae.h, 1)
        p.on("vector", a_dv)
        s_ae.bump(6)

        for lev in range(6):
            curL = Lm if lev == 0 else Pw[lev - 1]
            curLT = LTm if lev == 0 else PwT[(lev - 1) % 2]
            last = (lev == 5)

            def p_pe(e, curL=curL, curLT=curLT, last=last, ae_n=hc["ae_own"]):
                e.wait_ge(s_ae.h, ae_n)
                if not last:
                    e.matmul(ps_a[0][:], curLT[:], curL[:], start=True, stop=True)
                e.matmul(ps_a[1][:], curL[:], curLT[:], start=True,
                         stop=True).then_inc(s_af.h, 1)
            p.on("tensor", p_pe)
            af_v = s_af.bump()

            def p_dv(e, lev=lev, last=last, af_v=af_v):
                e.wait_ge(s_af.h, af_v)
                if not last:
                    e.tensor_copy(Pw[lev][:], ps_a[0][:])
                    e.drain()
                e.tensor_copy(PwT[lev % 2][:], ps_a[1][:]).then_inc(s_ae.h, 1)
            p.on("vector", p_dv)
            s_ae.bump()

        def y_init(e, ae_n=s_ae.n):
            e.wait_ge(s_ae.h, ae_n)
            e.tensor_tensor(Y[:], PwT[5 % 2][:], aux["eye_bf"][:], ALU.add).then_inc(s_ae.h, 1)
        p.on("vector", y_init)
        s_ae.bump()

        prod_lhs = [Pw[4], Pw[3], Pw[2], Pw[1], Pw[0], Lm]
        for pi, lhs in enumerate(prod_lhs):
            lastp = (pi == 5)

            def pr_pe(e, lhs=lhs, ae_n=s_ae.n):
                e.wait_ge(s_ae.h, ae_n)
                e.matmul(ps_a[0][:], lhs[:], Y[:], start=True, stop=True).then_inc(s_af.h, 1)
            p.on("tensor", pr_pe)
            af_v = s_af.bump()

            def pr_dv(e, lastp=lastp, af_v=af_v):
                e.wait_ge(s_af.h, af_v)
                if lastp:
                    e.tensor_tensor(TT[:], Y[:], ps_a[0][:], ALU.subtract).then_inc(s_ae.h, 1)
                else:
                    e.tensor_tensor(Y[:], Y[:], ps_a[0][:], ALU.add).then_inc(s_ae.h, 1)
            p.on("vector", pr_dv)
            s_ae.bump()

        def uw_pe(e, ae_n=s_ae.n):
            e.wait_ge(s_ae.h, ae_n)
            e.matmul(ps_a[0][:], TT[:], vb_tm[:], start=True, stop=True)
            e.matmul(ps_a[1][:], kb_tm[:], TT[:], start=True, stop=True).then_inc(s_af.h, 1)
        p.on("tensor", uw_pe)
        af_v = s_af.bump()

        def uw_dv(e, b2=b2, af_v=af_v):
            e.wait_ge(s_af.h, af_v)
            e.tensor_copy(u_sb[b2][:], ps_a[0][:])
            e.drain()
            e.tensor_copy(wT_sb[b2][:], ps_a[1][:]).then_inc(s_ae.h, 1)
        p.on("vector", uw_dv)
        s_ae.bump()
        return s_ae.n

    def emit_B(c, a_done):
        b2 = c % 2
        csl = slice(c * 128, (c + 1) * 128)
        lsl = slice((c % GRP) * 128, (c % GRP + 1) * 128)
        g = c // GRP
        bg = g % 2
        qh = qh_g[bg]

        def b_pe1(e, b2=b2, a_done=a_done, be_n=s_be.n):
            e.wait_ge(s_ae.h, a_done)
            e.wait_ge(s_be.h, be_n)
            e.matmul(ps_b[0][:], wT_sb[b2][:], S_b[:], start=True, stop=True).then_inc(s_bf.h, 1)
        p.on("tensor", b_pe1)
        bf1 = s_bf.bump()

        def b_dv1(e, b2=b2, bf1=bf1):
            e.wait_ge(s_bf.h, bf1)
            e.tensor_tensor(u2[:], u_sb[b2][:], ps_b[0][:], ALU.subtract).then_inc(s_be.h, 1)
        p.on("vector", b_dv1)
        be1 = s_be.bump()

        def b_pe2(e, b2=b2, lsl=lsl, qh=qh, be1=be1):
            e.wait_ge(s_be.h, be1)
            e.matmul(ps_b[1][:], qh[:, lsl], S_b[:], start=True, stop=False)
            e.matmul(ps_b[1][:], at_sb[b2][:], u2[:], start=False, stop=True)
            e.matmul(ps_b[0][:], ktm_sb[b2][:], u2[:], start=True,
                     stop=True).then_inc(s_bf.h, 1)
        p.on("tensor", b_pe2)
        bf2 = s_bf.bump()

        def b_dv2(e, bf2=bf2):
            e.wait_ge(s_bf.h, bf2)
            e.tensor_copy(oc[:], ps_b[1][:])
            e.drain()
            e.tensor_tensor(S_f[:], S_f[:], ps_b[0][:], ALU.add)
            e.drain()
            e.tensor_copy(S_b[:], S_f[:]).then_inc(s_be.h, 1)
        p.on("vector", b_dv2)
        be2 = s_be.bump()

        def b_pe3(e, be2=be2, ae_n=s_ae.n):
            e.wait_ge(s_be.h, be2)
            e.transpose(ps_t[:], oc[:], aux["ident"][:]).then_inc(s_bf.h, 1)
        p.on("tensor", b_pe3)
        bf3 = s_bf.bump()

        def b_dv3(e, csl=csl, bf3=bf3):
            e.wait_ge(s_bf.h, bf3)
            e.tensor_copy(olinT[:, csl], ps_t[:]).then_inc(s_be.h, 1)
        p.on("vector", b_dv3)
        s_be.bump()

    a_done_of = {}
    grp_done[0] = emit_group(0)
    a_done_of[0] = emit_A(0)
    for c in range(NCH):
        g_next = (c + 1) // GRP
        if c + 1 < NCH:
            if g_next not in grp_done:
                grp_done[g_next] = emit_group(g_next)
            a_done_of[c + 1] = emit_A(c + 1)
        emit_B(c, a_done_of[c])

    return s_be, s_be.n



def emit_attn_b_core(p: Prog, d, aux, deps, tag="b"):
    with p.scope():
        qT = p.sb(f"qT_{tag}", [128, 2, S], BF16)
        kT = p.sb(f"kT_{tag}", [128, 2, S], BF16)
        vT = p.sb(f"vT_{tag}", [128, 2, S], BF16)
        with p.scope():
            specs = [
                (d[f"wq_sl_{tag}"], 0, 2, qT),
                (d[f"wk_sl_{tag}"], 0, 2, kT),
                (d[f"wv_sl_{tag}"], 0, 2, vT),
            ]
            emit_attn_proj(p, d[f"cc_t_{tag}"], specs, deps, f"{tag}")
            bar1 = p.barrier()

        s_mx = p.sem("mx")
        mask4, m4dep = emit_aux_load(p, d["mask4"], [128, 4, 512], BF16, f"m4{tag}",
                                     deps=deps)
        olins = [p.sb(f"olinT_{tag}_{h}", [128, S], BF16) for h in range(2)]
        with p.scope():
            heads = [dict(qT=qT[:, h, :], kT=kT[:, h, :], vT=vT[:, h, :],
                          olinT=olins[h]) for h in range(2)]
            emit_delta_pair(p, heads, aux, [bar1], f"{tag}")
            bar_d = p.barrier(engines=("tensor", "vector", "scalar"))

        hdep = [bar_d, m4dep]
        for h in range(2):
            with p.scope():
                v_tm = p.sb(f"vtm_{tag}_{h}", [128, NCH, 128], BF16)
                with p.scope():
                    emit_transpose_fm_tm(p, vT[:, h, :], v_tm, NCH,
                                         aux["ident"], hdep, f"{tag}v{h}")
                    bar_tp = p.barrier(engines=("tensor", "vector"))
                outT = p.sb(f"mixT_{tag}_{h}", [128, S], BF16)
                s_fl, flv = emit_flash_head(p, qT[:, h, :], kT[:, h, :], v_tm, outT,
                                            aux["ones_bf"], aux["ones1_f"], mask4,
                                            hdep + [bar_tp], f"{tag}h{h}",
                                            olinT=olins[h], wbase=0.5, wlin=0.5,
                                            ones_f=aux["ones_f"])

                def mxo(e, h=h, outT=outT, flv=flv, s_fl=s_fl):
                    e.wait_ge(s_fl.h, flv)
                    for j in range(R):
                        e.dma_start(out=d[f"cc_mix_{tag}_in"][256 * j + 128 * h:256 * j + 128 * (h + 1), :],
                                    in_=outT[:, j * TOK:(j + 1) * TOK]).then_inc(s_mx.h, 16)
                p.on("scalar", mxo)
                s_mx.bump(16 * R)
                bar_h = p.barrier(engines=("tensor", "scalar", "vector"))
                hdep = [bar_h, m4dep]

        def agm(e, mxv=s_mx.n):
            from concourse.aot_env import allow_ici_all_to_all
            e.wait_ge(s_mx.h, mxv)
            with allow_ici_all_to_all():
                e.collective_compute(
                    "AllToAll", ALU.bypass,
                    replica_groups=[list(range(R))],
                    ins=[d[f"cc_mix_{tag}_in"]],
                    outs=[d[f"cc_mix_{tag}"]],
                ).then_inc(s_mx.h, 1)
        p.on("gpsimd", agm)
        mxv2 = s_mx.bump(1)
    return s_mx, mxv2



AUX_SPECS = [("ident", [128, 128], "bf16"), ("ones_bf", [128, 1], "bf16"),
             ("ones1_f", [1, 128], "f32"), ("ones1_bf", [1, 128], "bf16"),
             ("ones_f", [128, 1], "f32"), ("m_sl", [128, 128], "bf16"),
             ("m_su", [128, 128], "bf16"), ("m_iu", [128, 128], "bf16")]


def build_full(nc):
    import contextlib
    d = {}

    def din(name, shape, dt):
        d[name] = nc.dram_tensor(name, shape, dt, kind="ExternalInput").ap()

    din("x", [TOK, D], F32)
    din("consts", [128, MAX_CONSTS], F32)
    for nm, shape, dt in AUX_SPECS:
        din(nm, shape, BF16 if dt == "bf16" else F32)
    din("mask4", [128, 4, 512], BF16)
    din("wq_sl_a", [D, 256], F32)
    din("wk_sl_a", [D, 128], F32)
    din("wv_sl_a", [D, 128], F32)
    din("w_o_a", [D, D], F32)
    din("wq_sl_b", [D, 256], F32)
    din("wk_sl_b", [D, 256], F32)
    din("wv_sl_b", [D, 256], F32)
    din("w_o_b", [D, D], F32)
    din("gwT", [D, 8], F32)
    din("gate_wT", [N_EXP, D, F_EXP], F32)
    din("up_wT", [N_EXP, D, F_EXP], F32)
    din("down_wT", [N_EXP, F_EXP, D], F32)
    din("sh_gate_wT", [D, F_SH], F32)
    din("sh_up_wT", [D, F_SH], F32)
    din("sh_down_wT", [F_SH, D], F32)
    d["out"] = nc.dram_tensor("out", [TOK, D], F32, kind="ExternalOutput").ap()
    d["cc_t_a_in"] = nc.dram_tensor("cc_t_a_in", [TOK, D], BF16).ap()
    d["cc_t_a"] = nc.dram_tensor("cc_t_a", [S, D], BF16).ap()
    d["cc_mix_a_in"] = nc.dram_tensor("cc_mix_a_in", [D, TOK], BF16).ap()
    d["cc_mix_a"] = nc.dram_tensor("cc_mix_a", [D, TOK], BF16).ap()
    d["cc_t_b_in"] = nc.dram_tensor("cc_t_b_in", [TOK, D], BF16).ap()
    d["cc_t_b"] = nc.dram_tensor("cc_t_b", [S, D], BF16).ap()
    d["cc_mix_b_in"] = nc.dram_tensor("cc_mix_b_in", [D, TOK], BF16).ap()
    d["cc_mix_b"] = nc.dram_tensor("cc_mix_b", [D, TOK], BF16).ap()
    for t in ("f1", "f2"):
        d[f"trb_{t}"] = nc.dram_tensor(f"trb_{t}", [TOK, D], BF16).ap()
        d[f"trb_lo_{t}"] = nc.dram_tensor(f"trb_lo_{t}", [TOK, D], BF16).ap()
    d["xs_dram"] = nc.dram_tensor("xs_dram", [TOK, D], F32).ap()

    es = contextlib.ExitStack()
    with es:
        p = Prog(nc, es)
        p.init_consts(d["consts"])
        aux = {}
        s_aux = p.sem("aux_ld")
        for nm, shape, dt in AUX_SPECS:
            aux[nm] = p.sb(nm + "_s", shape, BF16 if dt == "bf16" else F32)

            def lf(e, nm=nm):
                e.dma_start(out=aux[nm][:], in_=d[nm]).then_inc(s_aux.h, 16)
            p.on("sync", lf)
            s_aux.bump(16)
        aux["eye_bf"] = aux["ident"]
        for eng in ("tensor", "vector", "scalar", "gpsimd"):
            p.wait(eng, s_aux, s_aux.n)

        st = p.sem("store")
        UPTO = "full"

        def early_store(x_buf, dep):
            def stf(e, dep=dep, stv=st.n + 16):
                e.wait_ge(dep[0].h, dep[1])
                e.dma_start(out=d["out"].rearrange("(t p) x -> p t x", p=128),
                            in_=x_buf[:]).then_inc(st.h, 16)
                e.wait_ge(st.h, stv)
            p.on("sync", stf)
            st.bump(16)

        with p.scope():
            x_sb = p.sb("x_sb", [128, NTT, D], F32)
            ld, ldv = emit_load_x(p, d["x"], x_sb)
            sra, vra = emit_attn_a(p, d, aux, x_sb, (ld, ldv), tag="a")
            bar_a = p.barrier()
            if UPTO == "attn_a":
                early_store(x_sb, bar_a)
                p.finalize()
                return d, p
            emit_ffn(p, d, aux, x_sb, [bar_a], "f1")
            bar_f1 = p.barrier()
            if UPTO == "ffn1":
                early_store(x_sb, bar_f1)
                p.finalize()
                return d, p
            with p.scope():
                t_bf3 = p.sb("t_b3", [128, NTT, D], BF16)
                s_ln3, ln3v = emit_ln(p, x_sb, t_bf3, [bar_f1], "t3")
                emit_allgather_tm(p, t_bf3, d["cc_t_b_in"], d["cc_t_b"],
                                  [(s_ln3, ln3v)], "bt")

                def spill(e, ln3v=ln3v):
                    e.wait_ge(s_ln3.h, ln3v)
                    e.dma_start(out=d["xs_dram"].rearrange("(t p) x -> p t x", p=128),
                                in_=x_sb[:]).then_inc(st.h, 16)
                p.on("sync", spill)
                st.bump(16)
                bar_x = p.barrier()

        if UPTO == "ln3":
            with p.scope():
                x_sb3 = p.sb("x_sb3", [128, NTT, D], F32)
                sld = p.sem("xreload")

                def rl(e, bx=bar_x):
                    e.wait_ge(bx[0].h, bx[1])
                    e.dma_start(out=x_sb3[:], in_=d["xs_dram"].rearrange("(t p) x -> p t x", p=128)).then_inc(sld.h, 16)
                p.on("sync", rl)
                early_store(x_sb3, (sld, sld.bump(16)))
                p.finalize()
                return d, p

        with p.scope():
            emit_attn_b_core(p, d, aux, [bar_x], "b")
            bar_b = p.barrier()

        if UPTO == "attn_b":
            with p.scope():
                x_sb3 = p.sb("x_sb3", [128, NTT, D], F32)
                sld = p.sem("xreload")

                def rl(e, bx=bar_b):
                    e.wait_ge(bx[0].h, bx[1])
                    e.dma_start(out=x_sb3[:], in_=d["xs_dram"].rearrange("(t p) x -> p t x", p=128)).then_inc(sld.h, 16)
                p.on("sync", rl)
                early_store(x_sb3, (sld, sld.bump(16)))
                p.finalize()
                return d, p

        with p.scope():
            x_sb2 = p.sb("x_sb2", [128, NTT, D], F32)
            with p.scope():
                s_rb, v_rb = emit_out_proj(p, d["cc_mix_b"], d["w_o_b"], d["xs_dram"], x_sb2,
                                           [bar_b], "bo")
                bar_bo = p.barrier()
            if UPTO == "outproj":
                early_store(x_sb2, bar_bo)
                p.finalize()
                return d, p
            s_xa2, xa2v = emit_ffn(p, d, aux, x_sb2, [bar_bo], "f2")

            def store(e, xa2v=xa2v, stv=st.n + 16):
                e.wait_ge(s_xa2.h, xa2v)
                e.dma_start(out=d["out"].rearrange("(t p) x -> p t x", p=128),
                            in_=x_sb2[:]).then_inc(st.h, 16)
                e.wait_ge(st.h, stv)
            p.on("sync", store)
            st.bump(16)
        p.finalize()
    return d, p


def make_in_maps(inputs, const_vals):
    import ml_dtypes
    x_full = np.asarray(inputs["L0_attn_a_x"][0], np.float32)
    w_q = np.asarray(inputs["L0_attn_a_w_q"], np.float32)
    w_kv = np.asarray(inputs["L0_attn_a_w_kv"], np.float32)
    w_dense = np.ascontiguousarray(np.asarray(inputs["L0_attn_a_w_dense"], np.float32))
    Wq = np.asarray(inputs["L0_attn_b_Wq"], np.float32)
    Wk = np.asarray(inputs["L0_attn_b_Wk"], np.float32)
    Wv = np.asarray(inputs["L0_attn_b_Wv"], np.float32)
    Wo = np.ascontiguousarray(np.asarray(inputs["L0_attn_b_Wo"], np.float32))
    gw = np.asarray(inputs["L0_ffn_gate_weight"], np.float32)
    gate_w = np.asarray(inputs["L0_ffn_gate_w"], np.float32)
    up_w = np.asarray(inputs["L0_ffn_up_w"], np.float32)
    down_w = np.asarray(inputs["L0_ffn_down_w"], np.float32)
    sh_gate = np.asarray(inputs["L0_ffn_sh_gate_w"], np.float32)
    sh_up = np.asarray(inputs["L0_ffn_sh_up_w"], np.float32)
    sh_down = np.asarray(inputs["L0_ffn_sh_down_w"], np.float32)

    cv = np.zeros((128, MAX_CONSTS), np.float32)
    cv[:, :len(const_vals)] = np.array(const_vals, np.float32)[None, :]
    ident = np.eye(128, dtype=ml_dtypes.bfloat16)
    ones_bf = np.ones((128, 1), ml_dtypes.bfloat16)
    ones1_f = np.ones((1, 128), np.float32)
    ones1_bf = np.ones((1, 128), ml_dtypes.bfloat16)
    ones_f = np.ones((128, 1), np.float32)
    mask4 = np.zeros((128, 4, 512), np.float32)
    for o in range(4):
        for kk in range(128):
            mask4[kk, o, :] = (np.arange(512) >= 128 * o + kk)
    mask4 = mask4.astype(ml_dtypes.bfloat16)
    m_sl = np.tril(np.ones((128, 128), np.float32), -1).astype(ml_dtypes.bfloat16)
    m_su = np.triu(np.ones((128, 128), np.float32), 1).astype(ml_dtypes.bfloat16)
    m_iu = np.triu(np.ones((128, 128), np.float32)).astype(ml_dtypes.bfloat16)

    shared = {
        "consts": cv, "ident": ident, "ones_bf": ones_bf, "ones1_f": ones1_f,
        "ones1_bf": ones1_bf, "ones_f": ones_f, "mask4": mask4, "m_sl": m_sl,
        "m_su": m_su, "m_iu": m_iu,
        "w_o_a": w_dense, "w_o_b": Wo,
        "gwT": np.ascontiguousarray(gw.T),
        "gate_wT": np.ascontiguousarray(gate_w.transpose(0, 2, 1)),
        "up_wT": np.ascontiguousarray(up_w.transpose(0, 2, 1)),
        "down_wT": np.ascontiguousarray(down_w.transpose(0, 2, 1)),
        "sh_gate_wT": np.ascontiguousarray(sh_gate.T),
        "sh_up_wT": np.ascontiguousarray(sh_up.T),
        "sh_down_wT": np.ascontiguousarray(sh_down.T),
    }
    in_maps = []
    for c in range(R):
        g = c // 2
        m = dict(shared)
        m["x"] = np.ascontiguousarray(x_full[c * TOK:(c + 1) * TOK])
        m["wq_sl_a"] = np.ascontiguousarray(w_q[:, 256 * c:256 * (c + 1)])
        m["wk_sl_a"] = np.ascontiguousarray(w_kv[:, 128 * g:128 * (g + 1)])
        m["wv_sl_a"] = np.ascontiguousarray(w_kv[:, 512 + 128 * g:512 + 128 * (g + 1)])
        m["wq_sl_b"] = np.ascontiguousarray(Wq[:, 256 * c:256 * (c + 1)])
        m["wk_sl_b"] = np.ascontiguousarray(Wk[:, 256 * c:256 * (c + 1)])
        m["wv_sl_b"] = np.ascontiguousarray(Wv[:, 256 * c:256 * (c + 1)])
        in_maps.append(m)
    return in_maps



_CACHE = {}


def kernel(**inputs):
    import concourse.bass as bass_mod
    from concourse.bass_utils import run_bass_kernel_spmd as _run
    if "nc" not in _CACHE:
        nc = bass_mod.Bass("TRN2", target_bir_lowering=False, debug=False)
        d, p = build_full(nc)
        _CACHE["nc"] = nc
        _CACHE["const_vals"] = list(p.const_vals)
    nc = _CACHE["nc"]
    in_maps = make_in_maps(inputs, _CACHE["const_vals"])
    res = _run(nc, in_maps, core_ids=list(range(R)))
    out = np.concatenate([np.asarray(res.results[c]["out"]) for c in range(R)], axis=0)
    return out[None].astype(np.float32)
